# version 05
# baseline (speedup 1.0000x reference)
import numpy as np
import concourse.bass as bass
import concourse.mybir as mybir
from concourse.bass_utils import run_bass_kernel_spmd
from contextlib import ExitStack

F32 = mybir.dt.float32
BF16 = mybir.dt.bfloat16
AF = mybir.ActivationFunctionType
ALU = mybir.AluOpType

B = 4
L = 4096
H = 2048
I = 4096
NOC = 64
NKT = 32
NN = 16
R = 128
ICF = 5632
NIC = 44
NHC = 16
LC = 512
NCH = L // LC
SUP = 1024
NSUP = L // SUP
C4 = SUP // LC
TFF = 512
NTF = L // TFF
KB = 4
NKB = NKT // KB
EPS = 1e-5

C_EPS, C_ONE, C_NEG1, C_NEGH, C_15, C_HALF, C_I2048, C_I128, C_I16, C_ZERO = range(10)
NCST = 10

STREAM_OF = {"pe": "tensor", "act": "scalar", "dve": "vector", "pool": "gpsimd",
             "sq": "sync", "aq": "scalar", "gq": "gpsimd"}
COMPUTE_ENGS = {"pe", "act", "dve", "pool"}
ENG_OF_STREAM = {"tensor": "pe", "scalar": "act", "vector": "dve",
                 "gpsimd": "pool", "sync": "sq"}


class Op:
    __slots__ = ("eng", "emit", "afters", "inc", "flag", "val", "seq", "key")

    def __init__(self, eng, emit, afters, inc, seq, key):
        self.eng = eng
        self.emit = emit
        self.afters = afters
        self.inc = inc
        self.flag = False
        self.val = None
        self.seq = seq
        self.key = key


def _flat(after):
    out = []
    for a in after:
        if a is None:
            continue
        if isinstance(a, (list, tuple)):
            out.extend(x for x in a if x is not None)
        else:
            out.append(a)
    return out


class Prog:

    def __init__(self):
        self.ops = []
        self.last_real = {}
        self.chan_last = {}

    def do(self, eng, emit, after=(), dma=False, chan=None):
        key = chan if dma else eng
        assert (chan is None) or dma
        if dma:
            assert chan is not None, "dma needs a channel"
        op = Op(eng, emit, _flat(after), 16 if dma else 1, len(self.ops), key)
        self.ops.append(op)
        if emit is not None:
            self.last_real[STREAM_OF[eng]] = op
            if dma:
                self.chan_last[chan] = op
        return op

    def barrier(self):
        lasts = list(self.last_real.items())
        chans = list(self.chan_last.values())
        for st in ("tensor", "scalar", "vector", "gpsimd", "sync"):
            afters = [op for s2, op in lasts if s2 != st] + chans
            if afters:
                self.do(ENG_OF_STREAM[st], None, after=afters)

    def finalize(self):
        keys = []
        seen = set()
        for op in self.ops:
            if op.key not in seen:
                seen.add(op.key)
                keys.append(op.key)
        for op in self.ops:
            for a in op.afters:
                a.flag = True
        counters = {k: 0 for k in keys}
        for op in self.ops:
            if op.flag:
                assert op.emit is not None, "cannot wait on a wait-only op"
            if op.flag or (op.inc == 16 and op.emit is not None):
                op.flag = True
                counters[op.key] += op.inc
                op.val = counters[op.key]
        per_stream = {s: [] for s in ("tensor", "scalar", "vector", "gpsimd", "sync")}
        for op in self.ops:
            per_stream[STREAM_OF[op.eng]].append(op)
        return per_stream, keys

    @staticmethod
    def emit_stream(eng_obj, stream, ops, sems, hw):
        for op in ops:
            for a in op.afters:
                if (STREAM_OF[a.eng] == stream and a.eng in COMPUTE_ENGS
                        and a.seq < op.seq):
                    continue
                if hw.get((stream, a.key), 0) >= a.val:
                    continue
                eng_obj.wait_ge(sems[a.key], a.val)
                hw[(stream, a.key)] = a.val
            if op.emit is not None:
                instr = op.emit(eng_obj)
                if op.flag:
                    instr.then_inc(sems[op.key], op.inc)


class Arena:

    def __init__(self, tensor, ncols_bf16):
        self.t = tensor
        self.n = ncols_bf16
        self.cur = 0

    def alloc(self, cols, dtype):
        units = cols * (2 if dtype == F32 else 1)
        units = (units + 31) & ~31
        assert self.cur + units <= self.n, f"arena overflow {self.cur}+{units}>{self.n}"
        ap = self.t[:, self.cur:self.cur + units]
        self.cur += units
        if dtype == F32:
            ap = ap.bitcast(F32)
        return ap[:, 0:cols]

    def mark(self):
        return self.cur

    def reset(self, mark):
        self.cur = mark


class Banks:
    def __init__(self, tensors):
        self.t = tensors
        self.w = [None] * len(tensors)
        self.r = [None] * len(tensors)


def build_nc(debug=False, paranoid=0):
    nc = bass.Bass("TRN2", target_bir_lowering=False)

    xT = nc.dram_tensor("xT", [H, L], F32, kind="ExternalInput")
    w_in_t = nc.dram_tensor("w_in_t", [NOC, 128, H], F32, kind="ExternalInput")
    wx_t = nc.dram_tensor("wx_t", [NKT, 128, 160], F32, kind="ExternalInput")
    wdt_t = nc.dram_tensor("wdt_t", [R, I], F32, kind="ExternalInput")
    wout_t = nc.dram_tensor("wout_t", [NHC, 128, I], F32, kind="ExternalInput")
    wg_t = nc.dram_tensor("wg_t", [NIC, 128, H], F32, kind="ExternalInput")
    wu_t = nc.dram_tensor("wu_t", [NIC, 128, H], F32, kind="ExternalInput")
    wd_t = nc.dram_tensor("wd_t", [NHC, 128, ICF], F32, kind="ExternalInput")
    convw_s = nc.dram_tensor("convw_s", [128, 128], F32, kind="ExternalInput")
    convb_s = nc.dram_tensor("convb_s", [128, 32], F32, kind="ExternalInput")
    dtb_s = nc.dram_tensor("dtb_s", [128, 32], F32, kind="ExternalInput")
    A_s = nc.dram_tensor("A_s", [128, 512], F32, kind="ExternalInput")
    D_s = nc.dram_tensor("D_s", [128, 32], F32, kind="ExternalInput")
    gs_s = nc.dram_tensor("gs_s", [128, NIC], F32, kind="ExternalInput")
    us_s = nc.dram_tensor("us_s", [128, NIC], F32, kind="ExternalInput")
    ds_s = nc.dram_tensor("ds_s", [128, NHC], F32, kind="ExternalInput")
    dtln_s = nc.dram_tensor("dtln_s", [128, 1], F32, kind="ExternalInput")
    bln_s = nc.dram_tensor("bln_s", [16, 1], F32, kind="ExternalInput")
    cln_s = nc.dram_tensor("cln_s", [16, 1], F32, kind="ExternalInput")
    cst_in = nc.dram_tensor("cst_in", [128, NCST], F32, kind="ExternalInput")

    o_out = nc.dram_tensor("o_out", [H, L], F32, kind="ExternalOutput")

    d_xn1 = nc.dram_tensor("d_xn1", [H, L], BF16)
    d_x1 = nc.dram_tensor("d_x1", [H, L], F32)
    d_xn2 = nc.dram_tensor("d_xn2", [H, L], BF16)
    d_hconv = nc.dram_tensor("d_hconv", [I, L], F32)
    d_gate = nc.dram_tensor("d_gate", [I, L], F32)
    d_bc = nc.dram_tensor("d_bc", [32, L], F32)
    d_x2 = nc.dram_tensor("d_x2", [H, L], F32)
    d_xn3 = nc.dram_tensor("d_xn3", [H, L], BF16)
    d_stat = nc.dram_tensor("d_stat", [4, L], F32)
    d_wg = nc.dram_tensor("d_wg", [NIC, 128, H], BF16)
    d_wu = nc.dram_tensor("d_wu", [NIC, 128, H], BF16)
    d_wd = nc.dram_tensor("d_wd", [NHC, 128, ICF], BF16)
    d_wout = nc.dram_tensor("d_wout", [NHC, 128, I], BF16)

    with ExitStack() as ctx:
        sb = lambda name, shape, dt_: ctx.enter_context(nc.sbuf_tensor(name, shape, dt_))
        cst = sb("cst", [128, NCST], F32)
        ones = sb("ones", [128, 1], F32)
        convw = sb("convw", [128, 128], F32)
        convb = sb("convb", [128, 32], F32)
        dtb = sb("dtb", [128, 32], F32)
        A_sb = sb("A_sb", [128, 512], F32)
        D_sb = sb("D_sb", [128, 32], F32)
        gs = sb("gs", [128, NIC], F32)
        us = sb("us", [128, NIC], F32)
        ds = sb("ds", [128, NHC], F32)
        hds = sb("hds", [128, NHC], F32)
        dtln = sb("dtln", [128, 1], F32)
        bln = sb("bln", [16, 1], F32)
        cln = sb("cln", [16, 1], F32)
        wx_bf = sb("wx_bf", [128, NKT * 160], BF16)
        wdt_bf = sb("wdt_bf", [128, I], BF16)
        ts_n = sb("ts_n", [128, L], BF16)
        state = sb("state", [128, NKT * NN], F32)
        halo = sb("halo", [128, 32 * 4], F32)
        s_ring = sb("s_ring", [128, KB * (LC + 1)], F32)
        stat_row = sb("stat_row", [4, 512], F32)
        stat2 = sb("stat2", [4, 512], F32)
        AR_N = 83840
        arena = Arena(sb("arena", [128, AR_N], BF16), AR_N)

        ps_t = [ctx.enter_context(nc.psum_tensor(f"ps{i}", [128, 512], F32))
                for i in range(8)]
        banks = Banks(ps_t)

        p = Prog()

        def ld(dst_ap, src_ap, after=(), q="sq", chan=None):
            return p.do(q, lambda e, d=dst_ap, s=src_ap: e.dma_start(out=d, in_=s),
                        after=after, dma=True, chan=chan)

        st_ = ld

        def mm(bank_i, lhsT, rhs, start, stop, after=(), base=0, m=128):
            out = banks.t[bank_i][base:base + m, :rhs.shape[-1]]
            extra = [banks.r[bank_i]] if start else []
            op = p.do("pe", lambda e, o=out, lh=lhsT, r=rhs, s1=start, s2=stop:
                      e.matmul(o, lh, r, start=s1, stop=s2),
                      after=list(after) + extra)
            banks.w[bank_i] = op
            return op

        def act(out, in_, func, bias=0.0, scale=1.0, after=()):
            return p.do("act", lambda e, o=out, i=in_, f=func, b=bias, s=scale:
                        e.activation(o, i, f, bias=b, scale=s), after=after)

        def tt(out, a, b_, aop, after=(), eng="dve"):
            return p.do(eng, lambda e, o=out, x=a, y=b_, op_=aop:
                        e.tensor_tensor(o, x, y, op_), after=after)

        def ts(out, a, s1, s2, op0, op1=None, after=(), eng="dve"):
            if op1 is None:
                return p.do(eng, lambda e, o=out, x=a, u=s1, v=s2, p0=op0:
                            e.tensor_scalar(o, x, u, v, p0), after=after)
            return p.do(eng, lambda e, o=out, x=a, u=s1, v=s2, p0=op0, p1=op1:
                        e.tensor_scalar(o, x, u, v, p0, p1), after=after)

        def stt(out, in0, scal, in1, op0, op1, after=(), eng="dve"):
            return p.do(eng, lambda e, o=out, x=in0, s=scal, y=in1, p0=op0, p1=op1:
                        e.scalar_tensor_tensor(o, x, s, y, p0, p1), after=after)

        def cp(out, in_, after=(), eng="dve"):
            return p.do(eng, lambda e, o=out, i=in_: e.tensor_copy(o, i), after=after)

        def memset(ap, v, after=(), eng="dve"):
            return p.do(eng, lambda e, a=ap, vv=v: e.memset(a, vv), after=after)

        def scan(out, d0, d1, init, after=()):
            return p.do("dve", lambda e, o=out, a=d0, b_=d1, i=init:
                        e.tensor_tensor_scan(o, a, b_, i, ALU.mult, ALU.add),
                        after=after)

        cEPS1 = cst[0:1, C_EPS:C_EPS + 1]
        cNEGH1 = cst[0:1, C_NEGH:C_NEGH + 1]
        c151 = cst[0:1, C_15:C_15 + 1]
        cI2048 = cst[0:1, C_I2048:C_I2048 + 1]
        cI128 = cst[0:1, C_I128:C_I128 + 1]
        cI16 = cst[0:1, C_I16:C_I16 + 1]

        stat_busy = [None]
        stat2_busy = [None]

        def rstd_chain(src_row, prescale, rows, cols, after=()):
            row_v, row_r0, row_t = rows
            o1 = ts(row_v[:, 0:cols], src_row, prescale, cEPS1,
                    ALU.mult, ALU.add, after=after)
            o2 = act(row_r0[:, 0:cols], row_v[:, 0:cols], AF.Ln, after=[o1])
            o3 = act(row_r0[:, 0:cols], row_r0[:, 0:cols], AF.Exp, scale=cNEGH1,
                     after=[o2])
            o4 = tt(row_t[:, 0:cols], row_r0[:, 0:cols], row_r0[:, 0:cols],
                    ALU.mult, after=[o3])
            o5 = tt(row_t[:, 0:cols], row_t[:, 0:cols], row_v[:, 0:cols],
                    ALU.mult, after=[o4, o1])
            o6 = ts(row_t[:, 0:cols], row_t[:, 0:cols], cNEGH1, c151,
                    ALU.mult, ALU.add, after=[o5])
            o7 = tt(stat_row[0:1, 0:cols], row_r0[:, 0:cols], row_t[:, 0:cols],
                    ALU.mult, after=[o6, o3, stat_busy[0]])
            return o7

        lds = []
        for dst, src in ((cst, cst_in), (convw, convw_s), (convb, convb_s),
                         (dtb, dtb_s), (A_sb, A_s), (D_sb, D_s), (gs, gs_s),
                         (us, us_s), (ds, ds_s), (dtln, dtln_s)):
            lds.append(ld(dst[:], src.ap(), chan="p0c"))
        lds.append(ld(bln[0:16, :], bln_s.ap(), chan="p0c"))
        lds.append(ld(cln[0:16, :], cln_s.ap(), chan="p0c"))
        wx_src = bass.AP(wx_t, 0, [[160, 128], [128 * 160, NKT], [1, 160]])
        lds.append(ld(wx_bf[:], wx_src, q="gq", chan="p0g"))
        lds.append(ld(wdt_bf[:], wdt_t.ap(), q="gq", chan="p0g"))

        memset(ones[:], 1.0)
        ts(hds[:], ds[:], cst[:, C_HALF:C_HALF + 1], None, ALU.mult, after=lds)
        memset(state[:], 0.0)
        memset(halo[:], 0.0)

        m0 = arena.mark()
        wc_f = [arena.alloc(8192, F32) for _ in range(2)]
        wc_b = [arena.alloc(8192, BF16) for _ in range(2)]
        cast_jobs = []
        for ic0 in range(0, NIC, 4):
            n = min(4, NIC - ic0)
            for (src, dst) in ((wg_t, d_wg), (wu_t, d_wu)):
                ap_pat = [[H, 128], [128 * H, n], [1, H]]
                cast_jobs.append((bass.AP(src, ic0 * 128 * H, ap_pat),
                                  bass.AP(dst, ic0 * 128 * H, ap_pat), n * H))
        for hc in range(NHC):
            ap_pat = [[ICF, 128], [1, ICF]]
            cast_jobs.append((bass.AP(wd_t, hc * 128 * ICF, ap_pat),
                              bass.AP(d_wd, hc * 128 * ICF, ap_pat), ICF))
        for ot in range(NHC):
            ap_pat = [[I, 128], [1, I]]
            cast_jobs.append((bass.AP(wout_t, ot * 128 * I, ap_pat),
                              bass.AP(d_wout, ot * 128 * I, ap_pat), I))
        cast_c = [None, None]
        cast_s = [None, None]
        for j, (srcap, dstap, cols) in enumerate(cast_jobs):
            bi = j % 2
            l_ = ld(wc_f[bi][:, 0:cols], srcap, after=[cast_c[bi]],
                    chan=f"cstl{bi}")
            c_ = act(wc_b[bi][:, 0:cols], wc_f[bi][:, 0:cols], AF.Copy,
                     after=[l_, cast_s[bi]])
            s_ = st_(dstap, wc_b[bi][:, 0:cols], after=[c_], chan=f"csts{bi}")
            cast_c[bi] = c_
            cast_s[bi] = s_
        p.barrier()
        arena.reset(m0)

        def ln_phase(src_dram, dst_dram):
            m0 = arena.mark()
            x_sb = [arena.alloc(8192, F32) for _ in range(2)]
            sq_sb = [arena.alloc(512, F32) for _ in range(2)]
            rows = [arena.alloc(512, F32)[0:1, :] for _ in range(3)]
            row_mu = arena.alloc(512, F32)[0:1, :]
            rstd_b = [arena.alloc(512, F32) for _ in range(2)]
            c_b = [arena.alloc(512, F32) for _ in range(2)]
            t1 = arena.alloc(512, F32)
            xn_sb = [arena.alloc(8192, BF16) for _ in range(2)]
            lu_x = [None, None]
            lu_bc = [None, None]
            lu_xn = [None, None]
            sqmm = [None, None]
            for t in range(NTF):
                bi = t % 2
                bk_x = 4 + bi
                bk_s = 6 + bi
                ldx = []
                for g in range(4):
                    srcap = bass.AP(src_dram, g * 4 * 128 * L + t * TFF,
                                    [[L, 128], [128 * L, 4], [1, TFF]])
                    ldx.append(ld(x_sb[bi][:, g * 2048:(g + 1) * 2048], srcap,
                                  after=[lu_x[bi]], chan=f"lnx{bi}"))
                mmx = None
                mms = None
                for hc in range(NHC):
                    xs = x_sb[bi][:, hc * 512:(hc + 1) * 512]
                    mmx = mm(bk_x, ones[:, 0:1], xs, start=(hc == 0), stop=(hc == NHC - 1),
                             after=ldx, base=0, m=1)
                    sq_ev = act(sq_sb[hc % 2], xs, AF.Square,
                                after=ldx + [sqmm[hc % 2]])
                    mms = mm(bk_s, ones[:, 0:1], sq_sb[hc % 2],
                             start=(hc == 0), stop=(hc == NHC - 1),
                             after=[sq_ev], base=0, m=1)
                    sqmm[hc % 2] = mms
                mu = act(row_mu, banks.t[bk_x][0:1, :], AF.Copy, scale=cI2048,
                         after=[mmx])
                bank_read(bk_x, mu)
                ex2 = act(rows[0][0:1, :], banks.t[bk_s][0:1, :], AF.Copy,
                          scale=cI2048, after=[mms])
                bank_read(bk_s, ex2)
                mu2 = act(rows[2][0:1, :], row_mu, AF.Square, after=[mu])
                var = tt(rows[0][0:1, :], rows[0][0:1, :], rows[2][0:1, :],
                         ALU.subtract, after=[ex2, mu2])
                o1 = ts(rows[0][0:1, :], rows[0][0:1, :],
                        cst[0:1, C_ONE:C_ONE + 1], cEPS1,
                        ALU.mult, ALU.add, after=[var])
                o2 = act(rows[1][0:1, :], rows[0][0:1, :], AF.Ln, after=[o1])
                o3 = act(rows[1][0:1, :], rows[1][0:1, :], AF.Exp, scale=cNEGH1,
                         after=[o2])
                o4 = tt(rows[2][0:1, :], rows[1][0:1, :], rows[1][0:1, :],
                        ALU.mult, after=[o3])
                o5 = tt(rows[2][0:1, :], rows[2][0:1, :], rows[0][0:1, :],
                        ALU.mult, after=[o4])
                o6 = ts(rows[2][0:1, :], rows[2][0:1, :], cNEGH1, c151,
                        ALU.mult, ALU.add, after=[o5])
                o7 = tt(stat_row[0:1, :], rows[1][0:1, :], rows[2][0:1, :],
                        ALU.mult, after=[o6, stat_busy[0]])
                oc = tt(stat2[0:1, :], row_mu, stat_row[0:1, :], ALU.mult,
                        after=[o7, stat2_busy[0]])
                sst = st_(bass.AP(d_stat, t * TFF, [[L, 1], [1, TFF]]),
                          stat_row[0:1, :], after=[o7], chan="lnst")
                sst2 = st_(bass.AP(d_stat, L + t * TFF, [[L, 1], [1, TFF]]),
                           stat2[0:1, :], after=[oc], chan="lnst2")
                stat_busy[0] = sst
                stat2_busy[0] = sst2
                b1 = ld(rstd_b[bi][:], bass.AP(d_stat, t * TFF, [[0, 128], [1, TFF]]),
                        after=[sst, lu_bc[bi]], chan=f"lnb{bi}")
                b2 = ld(c_b[bi][:], bass.AP(d_stat, L + t * TFF, [[0, 128], [1, TFF]]),
                        after=[sst2, lu_bc[bi]], chan=f"lnc{bi}")
                m2s = []
                for hc in range(NHC):
                    xs = x_sb[bi][:, hc * 512:(hc + 1) * 512]
                    m1 = tt(t1, xs, rstd_b[bi][:], ALU.mult,
                            after=[b1, lu_xn[bi], (m2s[-1] if m2s else None)])
                    m2 = tt(xn_sb[bi][:, hc * 512:(hc + 1) * 512], t1, c_b[bi][:],
                            ALU.subtract, after=[m1, b2])
                    m2s.append(m2)
                lu_x[bi] = m2s[-1]
                lu_bc[bi] = m2s[-1]
                outs = []
                for g in range(4):
                    dstap = bass.AP(dst_dram, g * 4 * 128 * L + t * TFF,
                                    [[L, 128], [128 * L, 4], [1, TFF]])
                    outs.append(st_(dstap, xn_sb[bi][:, g * 2048:(g + 1) * 2048],
                                    after=[m2s[-1]], chan=f"lno{bi}"))
                lu_xn[bi] = outs[-1]
                if paranoid & 1:
                    p.barrier()
            p.barrier()
            arena.reset(m0)

        def bank_read(bi_, op):
            banks.r[bi_] = op
            return op

        def ffn_phase(xn_dram, res_dram, out_dram):
            m0 = arena.mark()
            xn_T = [arena.alloc(8192, BF16) for _ in range(2)]
            hdn_T = arena.alloc(NIC * 512, BF16)
            wgb = [arena.alloc(2048, BF16) for _ in range(2)]
            wub = [arena.alloc(2048, BF16) for _ in range(2)]
            wdb = [arena.alloc(ICF, BF16) for _ in range(2)]
            gsl = [arena.alloc(512, F32) for _ in range(2)]
            usl = [arena.alloc(512, F32) for _ in range(2)]
            x0b = [arena.alloc(512, F32) for _ in range(2)]
            x1b = [arena.alloc(512, F32) for _ in range(2)]
            lu_xn = [None, None]
            lu_wg = [None, None]; lu_wu = [None, None]; lu_wd = [None, None]
            lu_gs = [None, None]; lu_us = [None, None]
            lu_x0 = [None, None]; lu_x1 = [None, None]
            hdn_reader = None
            for t in range(NTF):
                bi = t % 2
                ldx = []
                for g in range(4):
                    srcap = bass.AP(xn_dram, g * 4 * 128 * L + t * TFF,
                                    [[L, 128], [128 * L, 4], [1, TFF]])
                    ldx.append(ld(xn_T[bi][:, g * 2048:(g + 1) * 2048], srcap,
                                  after=[lu_xn[bi]], chan=f"fxn{bi}"))
                hdn_ops = []
                for ic in range(NIC):
                    wi = ic % 2
                    bg = wi * 2
                    bu = wi * 2 + 1
                    lg = ld(wgb[wi][:], bass.AP(d_wg, ic * 128 * H, [[H, 128], [1, H]]),
                            after=[lu_wg[wi]], chan=f"fwg{wi}")
                    lu_ = ld(wub[wi][:], bass.AP(d_wu, ic * 128 * H, [[H, 128], [1, H]]),
                             after=[lu_wu[wi]], chan=f"fwu{wi}")
                    gm = um = None
                    for hc in range(NHC):
                        gm = mm(bg, wgb[wi][:, hc * 128:(hc + 1) * 128],
                                xn_T[bi][:, hc * 512:(hc + 1) * 512],
                                start=(hc == 0), stop=(hc == NHC - 1),
                                after=ldx + [lg])
                        um = mm(bu, wub[wi][:, hc * 128:(hc + 1) * 128],
                                xn_T[bi][:, hc * 512:(hc + 1) * 512],
                                start=(hc == 0), stop=(hc == NHC - 1),
                                after=ldx + [lu_])
                    lu_wg[wi] = gm
                    lu_wu[wi] = um
                    eg = act(gsl[wi], banks.t[bg][:], AF.Silu,
                             scale=gs[:, ic:ic + 1], after=[gm, lu_gs[wi]])
                    bank_read(bg, eg)
                    eu = act(usl[wi], banks.t[bu][:], AF.Copy,
                             scale=us[:, ic:ic + 1], after=[um, lu_us[wi]])
                    bank_read(bu, eu)
                    hm = tt(hdn_T[:, ic * 512:(ic + 1) * 512], gsl[wi], usl[wi],
                            ALU.mult, after=[eg, eu, hdn_reader])
                    lu_gs[wi] = hm
                    lu_us[wi] = hm
                    hdn_ops.append(hm)
                lu_xn[bi] = hdn_ops[-1]
                ev_last = None
                for hc in range(NHC):
                    wi = hc % 2
                    bd = 4 + wi
                    lw = ld(wdb[wi][:], bass.AP(d_wd, hc * 128 * ICF,
                                                [[ICF, 128], [1, ICF]]),
                            after=[lu_wd[wi]], chan=f"fwd{wi}")
                    lx = ld(x0b[wi][:], bass.AP(res_dram, hc * 128 * L + t * TFF,
                                                [[L, 128], [1, TFF]]),
                            after=[lu_x0[wi]], chan=f"fx0{wi}")
                    dm = None
                    for ic in range(NIC):
                        dm = mm(bd, wdb[wi][:, ic * 128:(ic + 1) * 128],
                                hdn_T[:, ic * 512:(ic + 1) * 512],
                                start=(ic == 0), stop=(ic == NIC - 1),
                                after=[lw, hdn_ops[ic]])
                    lu_wd[wi] = dm
                    ev = stt(x1b[wi][:], banks.t[bd][:], hds[:, hc:hc + 1],
                             x0b[wi][:], ALU.mult, ALU.add,
                             after=[dm, lx, lu_x1[wi]])
                    bank_read(bd, ev)
                    lu_x0[wi] = ev
                    so = st_(bass.AP(out_dram, hc * 128 * L + t * TFF,
                                     [[L, 128], [1, TFF]]),
                             x1b[wi][:], after=[ev], chan=f"fx1{wi}")
                    lu_x1[wi] = so
                    ev_last = ev
                hdn_reader = ev_last
                if paranoid & 2:
                    p.barrier()
            p.barrier()
            arena.reset(m0)

        def pass1():
            m0 = arena.mark()
            xn2 = arena.alloc(NHC * SUP, BF16)
            winb = [arena.alloc(2048, BF16) for _ in range(2)]
            conv_in = [arena.alloc(LC + 4, F32) for _ in range(2)]
            acc_a = [arena.alloc(512, F32) for _ in range(2)]
            acc_b = [arena.alloc(512, F32) for _ in range(2)]
            hcv = [arena.alloc(512, F32) for _ in range(2)]
            hcb = [arena.alloc(512, BF16) for _ in range(2)]
            gat = [arena.alloc(512, F32) for _ in range(2)]
            ts_acc = arena.alloc(SUP, F32)
            b_acc = arena.alloc(SUP, F32)
            c_acc = arena.alloc(SUP, F32)
            rows = [arena.alloc(512, F32)[0:1, :] for _ in range(3)]
            rstd_b = arena.alloc(512, F32)
            bcn = arena.alloc(512, F32)
            tsn_f = arena.alloc(512, F32)
            lu_win = [None, None]
            lu_xn2 = None
            prev_super_done = None
            conv_free = [None, None]
            hcv_free = [None, None]
            hcb_free = [None, None]
            gat_free = [None, None]
            for s in range(NSUP):
                ldx = []
                for g in range(4):
                    srcap = bass.AP(d_xn2, g * 4 * 128 * L + s * SUP,
                                    [[L, 128], [128 * L, 4], [1, SUP]])
                    ldx.append(ld(xn2[:, g * 4 * SUP:(g + 1) * 4 * SUP], srcap,
                                  after=[lu_xn2, prev_super_done], chan="p1xn"))
                ms_t = memset(ts_acc[:], 0.0, after=[prev_super_done])
                ms_b = memset(b_acc[0:16, :], 0.0, after=[prev_super_done])
                ms_c = memset(c_acc[0:16, :], 0.0, after=[prev_super_done])
                ssm_last = [None] * C4
                for ot in range(NOC):
                    wi = ot % 2
                    is_h = ot < NKT
                    lw = ld(winb[wi][:],
                            bass.AP(w_in_t, ot * 128 * H, [[H, 128], [1, H]]),
                            after=[lu_win[wi]], q="gq", chan=f"p1w{wi}")
                    for c4 in range(C4):
                        bpj = c4 % 2
                        gchunk = s * C4 + c4
                        pm = None
                        for hc in range(NHC):
                            pm = mm(bpj, winb[wi][:, hc * 128:(hc + 1) * 128],
                                    xn2[:, hc * SUP + c4 * LC: hc * SUP + c4 * LC + LC],
                                    start=(hc == 0), stop=(hc == NHC - 1),
                                    after=ldx + [lw])
                        if c4 == C4 - 1:
                            lu_win[wi] = pm
                        if is_h:
                            ci = conv_in[wi]
                            if gchunk == 0:
                                hr = memset(ci[:, 0:3], 0.0, after=[conv_free[wi]])
                            else:
                                hr = cp(ci[:, 0:3], halo[:, ot * 3:ot * 3 + 3],
                                        after=[conv_free[wi]])
                            ev = act(ci[:, 3:3 + LC], banks.t[bpj][:],
                                     AF.Copy, after=[pm, conv_free[wi]])
                            bank_read(bpj, ev)
                            hs = cp(halo[:, ot * 3:ot * 3 + 3],
                                    ci[:, LC:LC + 3], after=[ev, hr])
                            c0 = ts(acc_a[wi], ci[:, 0:LC],
                                    convw[:, ot * 4:ot * 4 + 1], None, ALU.mult,
                                    after=[ev, hr])
                            c1 = stt(acc_b[wi], ci[:, 1:1 + LC],
                                     convw[:, ot * 4 + 1:ot * 4 + 2], acc_a[wi],
                                     ALU.mult, ALU.add, after=[c0])
                            c2 = stt(acc_a[wi], ci[:, 2:2 + LC],
                                     convw[:, ot * 4 + 2:ot * 4 + 3], acc_b[wi],
                                     ALU.mult, ALU.add, after=[c1])
                            c3 = stt(acc_b[wi], ci[:, 3:3 + LC],
                                     convw[:, ot * 4 + 3:ot * 4 + 4], acc_a[wi],
                                     ALU.mult, ALU.add, after=[c2])
                            conv_free[wi] = c3
                            sl = act(hcv[wi], acc_b[wi], AF.Silu,
                                     bias=convb[:, ot:ot + 1],
                                     after=[c3, hcv_free[wi]])
                            cb = cp(hcb[wi], hcv[wi], after=[sl, hcb_free[wi]])
                            so = st_(bass.AP(d_hconv, ot * 128 * L + gchunk * LC,
                                             [[L, 128], [1, LC]]), hcv[wi],
                                     after=[sl], chan=f"p1h{wi}")
                            hcv_free[wi] = [cb, so]
                            bts, bbb, bcc = 2 + 3 * (wi % 2), 3 + 3 * (wi % 2), 7
                            m1 = mm(bts, wx_bf[:, ot * 160:ot * 160 + 128], hcb[wi],
                                    start=True, stop=True, after=[cb])
                            m2 = mm(bbb, wx_bf[:, ot * 160 + 128:ot * 160 + 144],
                                    hcb[wi], start=True, stop=True, after=[cb], m=16)
                            m3 = mm(bcc, wx_bf[:, ot * 160 + 144:ot * 160 + 160],
                                    hcb[wi], start=True, stop=True, after=[cb],
                                    m=16, base=32 * (wi % 2))
                            hcb_free[wi] = m3
                            a1 = tt(ts_acc[:, c4 * LC:(c4 + 1) * LC],
                                    ts_acc[:, c4 * LC:(c4 + 1) * LC],
                                    banks.t[bts][:], ALU.add,
                                    after=[m1, ms_t, ssm_last[c4]])
                            bank_read(bts, a1)
                            a2 = tt(b_acc[0:16, c4 * LC:(c4 + 1) * LC],
                                    b_acc[0:16, c4 * LC:(c4 + 1) * LC],
                                    banks.t[bbb][0:16, :], ALU.add,
                                    after=[m2, ms_b, a1])
                            bank_read(bbb, a2)
                            a3 = tt(c_acc[0:16, c4 * LC:(c4 + 1) * LC],
                                    c_acc[0:16, c4 * LC:(c4 + 1) * LC],
                                    banks.t[bcc][32 * (wi % 2):32 * (wi % 2) + 16, :],
                                    ALU.add,
                                    after=[m3, ms_c, a2])
                            bank_read(bcc, a3)
                            ssm_last[c4] = a3
                        else:
                            ev = act(gat[wi], banks.t[bpj][:], AF.Silu,
                                     after=[pm, gat_free[wi]])
                            bank_read(bpj, ev)
                            so = st_(bass.AP(d_gate,
                                             (ot - NKT) * 128 * L + gchunk * LC,
                                             [[L, 128], [1, LC]]), gat[wi],
                                     after=[ev], chan=f"p1g{wi}")
                            gat_free[wi] = so
                lu_xn2 = p.last_real["tensor"]
                rms_last = None
                last_pe_read = None
                for c4 in range(C4):
                    gchunk = s * C4 + c4
                    tss = ts_acc[:, c4 * LC:(c4 + 1) * LC]
                    bk = 6
                    sq = act(conv_in[0][:, 0:LC], tss, AF.Square,
                             after=[ssm_last[c4], rms_last] + conv_free)
                    sm = mm(bk, ones[:, 0:1], conv_in[0][:, 0:LC], start=True,
                            stop=True, after=[sq], m=1)
                    r7 = rstd_chain(banks.t[bk][0:1, :], cI128, rows, LC,
                                    after=[sm])
                    bank_read(bk, r7)
                    ss = st_(bass.AP(d_stat, gchunk * LC, [[L, 1], [1, LC]]),
                             stat_row[0:1, :], after=[r7], chan="p1st")
                    stat_busy[0] = ss
                    bb = ld(rstd_b[:], bass.AP(d_stat, gchunk * LC,
                                               [[0, 128], [1, LC]]), after=[ss],
                            chan="p1b")
                    t1_ = tt(tsn_f, tss, rstd_b[:], ALU.mult, after=[bb, sq])
                    t2_ = ts(ts_n[:, gchunk * LC:(gchunk + 1) * LC], tsn_f,
                             dtln[:, 0:1], None, ALU.mult, after=[t1_])
                    prev = t2_
                    for half, w_ln, srow in ((0, bln, 1), (1, cln, 2)):
                        acc_src = b_acc if half == 0 else c_acc
                        bcx = acc_src[0:16, c4 * LC:(c4 + 1) * LC]
                        sq2 = act(conv_in[0][0:16, 0:LC], bcx, AF.Square,
                                  after=[prev, ssm_last[c4], sm])
                        sm2 = mm(bk, ones[0:16, 0:1], conv_in[0][0:16, 0:LC],
                                 start=True, stop=True, after=[sq2], m=1)
                        last_pe_read = sm2
                        r72 = rstd_chain(banks.t[bk][0:1, :], cI16, rows, LC,
                                         after=[sm2])
                        bank_read(bk, r72)
                        ss2 = st_(bass.AP(d_stat, srow * L + gchunk * LC,
                                          [[L, 1], [1, LC]]),
                                  stat_row[0:1, :], after=[r72],
                                  chan=f"p1st{srow}")
                        stat_busy[0] = ss2
                        bb2 = ld(rstd_b[0:16, :],
                                 bass.AP(d_stat, srow * L + gchunk * LC,
                                         [[0, 16], [1, LC]]), after=[ss2],
                                 chan="p1b2")
                        n1 = tt(bcn[0:16, :], bcx, rstd_b[0:16, :], ALU.mult,
                                after=[bb2, sq2])
                        n2 = ts(bcn[0:16, :], bcn[0:16, :], w_ln[0:16, 0:1],
                                None, ALU.mult, after=[n1])
                        so2 = st_(bass.AP(d_bc, half * 16 * L + gchunk * LC,
                                          [[L, 16], [1, LC]]),
                                  bcn[0:16, :], after=[n2], chan="p1bc")
                        prev = so2
                    rms_last = prev
                conv_free = [last_pe_read, conv_free[1]]
                prev_super_done = rms_last
                if paranoid & 4:
                    p.barrier()
            p.barrier()
            arena.reset(m0)

        def pass2():
            m0 = arena.mark()
            hcv = [arena.alloc(KB * 512, F32) for _ in range(2)]
            gat = [arena.alloc(KB * 512, F32) for _ in range(2)]
            delta = arena.alloc(KB * 512, F32)
            du = arena.alloc(KB * 512, F32)
            yb = arena.alloc(KB * 512, F32)
            dab = [arena.alloc(LC + 1, F32) for _ in range(2)]
            dbb = [arena.alloc(LC + 1, F32) for _ in range(2)]
            bcb = [arena.alloc(4 * 512, F32) for _ in range(2)]
            ccb = [arena.alloc(4 * 512, F32) for _ in range(2)]
            ymt_d = arena.alloc(512, F32)
            ymt_p = arena.alloc(512, F32)
            tmp_sp = [arena.alloc(512, F32) for _ in range(2)]
            x1b = [arena.alloc(512, F32) for _ in range(2)]
            x2b = [arena.alloc(512, F32) for _ in range(2)]
            y_g = arena.alloc(NKT * 512, BF16)
            woutb = [arena.alloc(I, BF16) for _ in range(2)]
            pad_ok = [memset(dab[0][:, LC:LC + 1], 1.0),
                      memset(dab[1][:, LC:LC + 1], 1.0),
                      memset(dbb[0][:, LC:LC + 1], 0.0),
                      memset(dbb[1][:, LC:LC + 1], 0.0)]
            lu_h = [None, None]; lu_g = [None, None]
            lu_bc = [None, None]; lu_cc = [None, None]
            lu_w = [None, None]; lu_x1 = [None, None]; lu_x2 = [None, None]
            yg_last = {"dve": None, "pool": None}
            wout_done = None
            delta_busy = None
            du_busy = None
            tmp_busy = [None, None]
            st_carry = None
            scan_carry = {}
            sring_v = s_ring[:].rearrange("p (k l) -> p k l", l=LC + 1)
            state_v = state[:].rearrange("p (k n) -> p k n", n=NN)
            for c in range(NCH):
                for kb in range(NKB):
                    bi = kb % 2
                    d1_ops = []
                    y0_ops = []
                    lg_ops = []
                    delta_ops = []
                    for kk in range(KB):
                        k = kb * KB + kk
                        lh = ld(hcv[bi][:, kk * 512:kk * 512 + LC],
                                bass.AP(d_hconv, k * 128 * L + c * LC,
                                        [[L, 128], [1, LC]]),
                                after=[lu_h[bi]], chan=f"p2h{bi}")
                        lg = ld(gat[bi][:, kk * 512:kk * 512 + LC],
                                bass.AP(d_gate, k * 128 * L + c * LC,
                                        [[L, 128], [1, LC]]),
                                after=[lu_g[bi]], chan=f"p2g{bi}")
                        bdt = 6 + (kk % 2)
                        dmm = mm(bdt, wdt_bf[:, k * 128:(k + 1) * 128],
                                 ts_n[:, c * LC:(c + 1) * LC], start=True,
                                 stop=True, after=[])
                        dl = delta[:, kk * 512:kk * 512 + LC]
                        sp1 = act(tmp_sp[kk % 2], banks.t[bdt][:, 0:LC],
                                  AF.Abs, bias=dtb[:, k:k + 1],
                                  after=[dmm, tmp_busy[kk % 2]])
                        sp2 = act(tmp_sp[kk % 2], tmp_sp[kk % 2], AF.Exp,
                                  scale=cst[:, C_NEG1:C_NEG1 + 1], after=[sp1])
                        sp3 = act(tmp_sp[kk % 2], tmp_sp[kk % 2], AF.Ln,
                                  bias=cst[:, C_ONE:C_ONE + 1], after=[sp2])
                        sp4 = act(dl, banks.t[bdt][:, 0:LC], AF.Relu,
                                  bias=dtb[:, k:k + 1],
                                  after=[sp3, delta_busy])
                        bank_read(bdt, sp4)
                        sp5 = tt(dl, dl, tmp_sp[kk % 2], ALU.add,
                                 after=[sp4, delta_busy])
                        tmp_busy[kk % 2] = sp5
                        delta_ops.append(sp5)
                        d1 = tt(du[:, kk * 512:kk * 512 + LC], dl,
                                hcv[bi][:, kk * 512:kk * 512 + LC], ALU.mult,
                                after=[sp5, lh, du_busy])
                        y0 = ts(yb[:, kk * 512:kk * 512 + LC],
                                hcv[bi][:, kk * 512:kk * 512 + LC],
                                D_sb[:, k:k + 1], None, ALU.mult,
                                after=[lh, yg_last["dve"], yg_last["pool"]])
                        d1_ops.append(d1)
                        y0_ops.append(y0)
                        lg_ops.append(lg)
                    lu_h[bi] = [d1_ops[-1], y0_ops[-1]]
                    scan_prev = dict(scan_carry)
                    yop_prev = dict(enumerate(y0_ops))
                    st_prev = st_carry
                    last_ex = None
                    for n in range(NN):
                        nbi = (n // 4) % 2
                        if n % 4 == 0:
                            lb = ld(bcb[nbi][:],
                                    bass.AP(d_bc, n * L + c * LC,
                                            [[0, 128], [L, 4], [1, LC]]),
                                    after=[lu_bc[nbi]], chan=f"p2b{nbi}")
                            lc_ = ld(ccb[nbi][:],
                                     bass.AP(d_bc, (16 + n) * L + c * LC,
                                             [[0, 128], [L, 4], [1, LC]]),
                                     after=[lu_cc[nbi]], chan=f"p2c{nbi}")
                            cur_lb, cur_lc = lb, lc_
                        bslice = bcb[nbi][:, (n % 4) * 512:(n % 4) * 512 + LC]
                        cslice = ccb[nbi][:, (n % 4) * 512:(n % 4) * 512 + LC]
                        lb, lc_ = cur_lb, cur_lc
                        scan_ops = []
                        for kk in range(KB):
                            k = kb * KB + kk
                            abi = kk % 2
                            dl = delta[:, kk * 512:kk * 512 + LC]
                            ex = act(dab[abi][:, 0:LC], dl, AF.Exp,
                                     scale=A_sb[:, k * NN + n:k * NN + n + 1],
                                     after=[scan_prev.get(abi), delta_ops[kk]]
                                     + pad_ok)
                            last_ex = ex
                            db_ = tt(dbb[abi][:, 0:LC],
                                     du[:, kk * 512:kk * 512 + LC],
                                     bslice, ALU.mult,
                                     after=[lb, scan_prev.get(abi), d1_ops[kk]])
                            sc = scan(sring_v[:, kk, :], dab[abi][:],
                                      dbb[abi][:],
                                      state[:, k * NN + n:k * NN + n + 1],
                                      after=[ex, db_, yop_prev.get(kk), st_prev])
                            scan_prev[abi] = sc
                            scan_ops.append(sc)
                            yeng = "dve" if kk < 1 else "pool"
                            ymt = ymt_d if kk < 1 else ymt_p
                            y1 = tt(ymt, sring_v[:, kk, 0:LC], cslice,
                                    ALU.mult,
                                    after=[sc, lc_, yg_last[yeng]], eng=yeng)
                            y2 = tt(yb[:, kk * 512:kk * 512 + LC],
                                    yb[:, kk * 512:kk * 512 + LC], ymt,
                                    ALU.add, after=[y1, yop_prev.get(kk)],
                                    eng=yeng)
                            yop_prev[kk] = y2
                            yg_last[yeng] = y2
                        if n % 4 == 3:
                            lu_bc[nbi] = scan_ops[-1]
                            lu_cc[nbi] = [yop_prev[0], yop_prev[KB - 1]]
                        st_prev = act(state_v[:, kb * KB:(kb + 1) * KB, n],
                                      sring_v[:, :, LC], AF.Copy,
                                      after=scan_ops)
                    st_carry = st_prev
                    scan_carry = dict(scan_prev)
                    delta_busy = [last_ex, d1_ops[-1]]
                    du_busy = scan_ops[-1]
                    for kk in range(KB):
                        k = kb * KB + kk
                        yeng = "dve" if kk < 1 else "pool"
                        yg = tt(y_g[:, k * 512:k * 512 + LC],
                                yb[:, kk * 512:kk * 512 + LC],
                                gat[bi][:, kk * 512:kk * 512 + LC], ALU.mult,
                                after=[yop_prev[kk], st_prev, lg_ops[kk],
                                       yg_last[yeng], wout_done], eng=yeng)
                        yg_last[yeng] = yg
                    lu_g[bi] = [yg_last["dve"], yg_last["pool"]]
                wev = None
                for ot in range(NHC):
                    wi = ot % 2
                    bw = 2 + wi
                    lw = ld(woutb[wi][:],
                            bass.AP(d_wout, ot * 128 * I, [[I, 128], [1, I]]),
                            after=[lu_w[wi]], chan=f"p2w{wi}")
                    lx = ld(x1b[wi][:], bass.AP(d_x1, ot * 128 * L + c * LC,
                                                [[L, 128], [1, LC]]),
                            after=[lu_x1[wi]], chan=f"p2x1{wi}")
                    wm = None
                    for ic in range(NKT):
                        wm = mm(bw, woutb[wi][:, ic * 128:(ic + 1) * 128],
                                y_g[:, ic * 512:ic * 512 + LC],
                                start=(ic == 0), stop=(ic == NKT - 1),
                                after=[lw, yg_last["dve"], yg_last["pool"]])
                    lu_w[wi] = wm
                    ev = tt(x2b[wi][:], banks.t[bw][:, 0:LC], x1b[wi][:],
                            ALU.add, after=[wm, lx, lu_x2[wi]])
                    bank_read(bw, ev)
                    lu_x1[wi] = ev
                    so = st_(bass.AP(d_x2, ot * 128 * L + c * LC,
                                     [[L, 128], [1, LC]]), x2b[wi][:], after=[ev],
                             chan=f"p2x2{wi}")
                    lu_x2[wi] = so
                    wev = ev
                wout_done = wev
                if paranoid & 8:
                    p.barrier()
            p.barrier()
            arena.reset(m0)

        ln_phase(xT, d_xn1)
        ffn_phase(d_xn1, xT, d_x1)
        ln_phase(d_x1, d_xn2)
        pass1()
        pass2()
        ln_phase(d_x2, d_xn3)
        ffn_phase(d_xn3, d_x2, o_out)

        if debug:
            p.barrier()
            dbg_map = {"d_xn1": (d_xn1, [H, L], BF16), "d_x1": (d_x1, [H, L], F32),
                       "d_xn2": (d_xn2, [H, L], BF16),
                       "d_hconv": (d_hconv, [I, L], F32),
                       "d_gate": (d_gate, [I, L], F32),
                       "d_bc": (d_bc, [32, L], F32),
                       "d_x2": (d_x2, [H, L], F32),
                       "d_xn3": (d_xn3, [H, L], BF16),
                       "d_stat": (d_stat, [4, L], F32)}
            for wname, wt in (("d_wg", d_wg), ("d_wu", d_wu)):
                dbgw = nc.dram_tensor("dbg_" + wname, [8, 128, H], BF16,
                                      kind="ExternalOutput")
                ld(dbgw.ap(), bass.AP(wt, 0, [[1, 8 * 128 * H]]), chan="dbg")
            for name, (src, shape, dt_) in dbg_map.items():
                dbg = nc.dram_tensor("dbg_" + name, shape, dt_,
                                     kind="ExternalOutput")
                ld(dbg.ap(), src.ap(), chan="dbg")
            dbg_tsn = nc.dram_tensor("dbg_tsn", [128, L], BF16,
                                     kind="ExternalOutput")
            dbg_state = nc.dram_tensor("dbg_state", [128, NKT * NN], F32,
                                       kind="ExternalOutput")
            ld(dbg_tsn.ap(), ts_n[:], chan="dbg")
            ld(dbg_state.ap(), state[:], chan="dbg")

        p.do("sq", None, after=list(p.chan_last.values()))

        per_stream, keys = p.finalize()
        sems = {k: ctx.enter_context(nc.semaphore(f"sem_{k}")) for k in keys}
        hw = {}
        block = ctx.enter_context(nc.Block())

        @block.sync
        def _(sync):
            Prog.emit_stream(sync, "sync", per_stream["sync"], sems, hw)

        @block.tensor
        def _(tensor):
            Prog.emit_stream(tensor, "tensor", per_stream["tensor"], sems, hw)

        @block.scalar
        def _(scalar):
            Prog.emit_stream(scalar, "scalar", per_stream["scalar"], sems, hw)

        @block.vector
        def _(vector):
            Prog.emit_stream(vector, "vector", per_stream["vector"], sems, hw)

        @block.gpsimd
        def _(gpsimd):
            Prog.emit_stream(gpsimd, "gpsimd", per_stream["gpsimd"], sems, hw)

    return nc


def _prep_shared(inp):
    W_in = inp["L0_attn_W_in"]
    Wx = inp["L0_attn_Wx"]
    Wdt = inp["L0_attn_Wdt"]
    Wout = inp["L0_attn_Wout"]
    gw = inp["L0_ffn_gate_w"]
    uw = inp["L0_ffn_up_w"]
    dw = inp["L0_ffn_down_w"]
    d = {}
    d["w_in_t"] = np.ascontiguousarray(
        np.asarray(W_in).reshape(NOC, 128, NHC, 128).transpose(0, 3, 2, 1)
        .reshape(NOC, 128, H))
    d["wx_t"] = np.ascontiguousarray(np.asarray(Wx).T.reshape(NKT, 128, 160))
    d["wdt_t"] = np.ascontiguousarray(np.asarray(Wdt).T)
    d["wout_t"] = np.ascontiguousarray(
        np.asarray(Wout).reshape(NHC, 128, NKT, 128).transpose(0, 3, 2, 1)
        .reshape(NHC, 128, I))
    d["wg_t"] = np.ascontiguousarray(
        np.asarray(gw).reshape(NIC, 128, NHC, 128).transpose(0, 3, 2, 1)
        .reshape(NIC, 128, H))
    d["wu_t"] = np.ascontiguousarray(
        np.asarray(uw).reshape(NIC, 128, NHC, 128).transpose(0, 3, 2, 1)
        .reshape(NIC, 128, H))
    d["wd_t"] = np.ascontiguousarray(
        np.asarray(dw).reshape(NHC, 128, NIC, 128).transpose(0, 3, 2, 1)
        .reshape(NHC, 128, ICF))
    d["convw_s"] = np.ascontiguousarray(
        np.asarray(inp["L0_attn_conv_w"]).reshape(32, 128, 4)
        .transpose(1, 0, 2).reshape(128, 128))
    d["convb_s"] = np.ascontiguousarray(
        np.asarray(inp["L0_attn_conv_b"]).reshape(32, 128).T)
    d["dtb_s"] = np.ascontiguousarray(
        np.asarray(inp["L0_attn_dt_bias"]).reshape(32, 128).T)
    d["A_s"] = np.ascontiguousarray(
        np.asarray(inp["L0_attn_A"]).reshape(32, 128, 16)
        .transpose(1, 0, 2).reshape(128, 512))
    d["D_s"] = np.ascontiguousarray(np.asarray(inp["L0_attn_D"]).reshape(32, 128).T)
    d["gs_s"] = np.ascontiguousarray(
        np.asarray(inp["L0_ffn_gate_s"]).reshape(NIC, 128).T)
    d["us_s"] = np.ascontiguousarray(
        np.asarray(inp["L0_ffn_up_s"]).reshape(NIC, 128).T)
    d["ds_s"] = np.ascontiguousarray(
        np.asarray(inp["L0_ffn_down_s"]).reshape(NHC, 128).T)
    d["dtln_s"] = np.ascontiguousarray(
        np.asarray(inp["L0_attn_dt_ln_w"]).reshape(128, 1))
    d["bln_s"] = np.ascontiguousarray(
        np.asarray(inp["L0_attn_b_ln_w"]).reshape(16, 1))
    d["cln_s"] = np.ascontiguousarray(
        np.asarray(inp["L0_attn_c_ln_w"]).reshape(16, 1))
    cvals = np.zeros(NCST, np.float32)
    cvals[C_EPS] = EPS
    cvals[C_ONE] = 1.0
    cvals[C_NEG1] = -1.0
    cvals[C_NEGH] = -0.5
    cvals[C_15] = 1.5
    cvals[C_HALF] = 0.5
    cvals[C_I2048] = 1.0 / 2048.0
    cvals[C_I128] = 1.0 / 128.0
    cvals[C_I16] = 1.0 / 16.0
    d["cst_in"] = np.ascontiguousarray(
        np.broadcast_to(cvals.astype(np.float32), (128, NCST)))
    return d


def kernel(**inputs):
    shared = _prep_shared(inputs)
    x = np.asarray(inputs["L0_attn_x"])
    nc = build_nc()
    in_maps = []
    for core in range(8):
        b = core // 2
        m = dict(shared)
        m["xT"] = np.ascontiguousarray(x[b].T)
        in_maps.append(m)
    res = run_bass_kernel_spmd(nc, in_maps, core_ids=list(range(8)))
    out = np.empty((B, L, H), np.float32)
    for b in range(B):
        out[b] = np.asarray(res.results[2 * b]["o_out"]).T
    return out


# version 06
# speedup vs baseline: 1.7219x; 1.7219x over previous
import numpy as np
import concourse.bass as bass
import concourse.mybir as mybir
from concourse.bass_utils import run_bass_kernel_spmd
from contextlib import ExitStack

F32 = mybir.dt.float32
BF16 = mybir.dt.bfloat16
AF = mybir.ActivationFunctionType
ALU = mybir.AluOpType

B = 4
LG = 4096
LT = 2048
H = 2048
I = 4096
IH = 2048
MYKT = 16
NOT = 32
NN = 16
R = 128
ICF = 5632
NIC = 44
NHC = 16
LC = 512
NCHG = LG // LC
TFF = 512
NTF = LT // TFF
BLK = 1024
KB = 4
NKB = MYKT // KB
EPS = 1e-5
PAIRS = [[0, 1], [2, 3], [4, 5], [6, 7]]

C_EPS, C_ONE, C_NEG1, C_NEGH, C_15, C_HALF, C_I2048, C_I128, C_I16, C_ZERO = range(10)
NCST = 10

STREAM_OF = {"pe": "tensor", "act": "scalar", "dve": "vector", "pool": "gpsimd",
             "sq": "sync", "aq": "scalar", "gq": "gpsimd"}
COMPUTE_ENGS = {"pe", "act", "dve", "pool"}
ENG_OF_STREAM = {"tensor": "pe", "scalar": "act", "vector": "dve",
                 "gpsimd": "pool", "sync": "sq"}


class Op:
    __slots__ = ("eng", "emit", "afters", "inc", "flag", "val", "seq", "key", "cc")

    def __init__(self, eng, emit, afters, inc, seq, key, cc=False):
        self.eng = eng
        self.emit = emit
        self.afters = afters
        self.inc = inc
        self.flag = False
        self.val = None
        self.seq = seq
        self.key = key
        self.cc = cc


def _flat(after):
    out = []
    for a in after:
        if a is None:
            continue
        if isinstance(a, (list, tuple)):
            out.extend(x for x in a if x is not None)
        else:
            out.append(a)
    return out


class Prog:

    def __init__(self):
        self.ops = []
        self.last_real = {}
        self.chan_last = {}

    def do(self, eng, emit, after=(), dma=False, chan=None, cc=False):
        key = chan if (dma or cc) else eng
        if dma or cc:
            assert chan is not None
        op = Op(eng, emit, _flat(after), 1 if cc else (16 if dma else 1),
                len(self.ops), key, cc)
        self.ops.append(op)
        if emit is not None:
            self.last_real[STREAM_OF[eng]] = op
            if dma or cc:
                self.chan_last[chan] = op
        return op

    def barrier(self):
        lasts = list(self.last_real.items())
        chans = list(self.chan_last.values())
        for st in ("tensor", "scalar", "vector", "gpsimd", "sync"):
            afters = [op for s2, op in lasts if s2 != st] + chans
            if afters:
                self.do(ENG_OF_STREAM[st], None, after=afters)

    def finalize(self):
        keys = []
        seen = set()
        for op in self.ops:
            if op.key not in seen:
                seen.add(op.key)
                keys.append(op.key)
        for op in self.ops:
            for a in op.afters:
                a.flag = True
        counters = {k: 0 for k in keys}
        for op in self.ops:
            if op.flag:
                assert op.emit is not None, "cannot wait on a wait-only op"
            if op.flag or (op.inc == 16 and op.emit is not None) or op.cc:
                op.flag = True
                counters[op.key] += op.inc
                op.val = counters[op.key]
        per_stream = {s: [] for s in ("tensor", "scalar", "vector", "gpsimd", "sync")}
        for op in self.ops:
            per_stream[STREAM_OF[op.eng]].append(op)
        return per_stream, keys

    @staticmethod
    def emit_stream(eng_obj, stream, ops, sems, hw):
        for op in ops:
            for a in op.afters:
                if (STREAM_OF[a.eng] == stream and a.eng in COMPUTE_ENGS
                        and not a.cc and a.seq < op.seq):
                    continue
                if hw.get((stream, a.key), 0) >= a.val:
                    continue
                eng_obj.wait_ge(sems[a.key], a.val)
                hw[(stream, a.key)] = a.val
            if op.emit is not None:
                instr = op.emit(eng_obj)
                if op.flag:
                    if op.cc:
                        instr.then_inc(sems[op.key])
                    else:
                        instr.then_inc(sems[op.key], op.inc)


class Arena:

    def __init__(self, tensor, ncols_bf16):
        self.t = tensor
        self.n = ncols_bf16
        self.cur = 0

    def alloc(self, cols, dtype):
        units = cols * (2 if dtype == F32 else 1)
        units = (units + 31) & ~31
        assert self.cur + units <= self.n, f"arena overflow {self.cur}+{units}>{self.n}"
        ap = self.t[:, self.cur:self.cur + units]
        self.cur += units
        if dtype == F32:
            ap = ap.bitcast(F32)
        return ap[:, 0:cols]

    def mark(self):
        return self.cur

    def reset(self, mark):
        self.cur = mark


class Banks:
    def __init__(self, tensors):
        self.t = tensors
        self.w = [None] * len(tensors)
        self.r = [None] * len(tensors)


def build_nc(debug=False, paranoid=0):
    nc = bass.Bass("TRN2", target_bir_lowering=False)

    xT = nc.dram_tensor("xT", [H, LT], F32, kind="ExternalInput")
    w_in_t = nc.dram_tensor("w_in_t", [NOT, 128, H], F32, kind="ExternalInput")
    wx_t = nc.dram_tensor("wx_t", [MYKT, 128, 160], F32, kind="ExternalInput")
    wdt_t = nc.dram_tensor("wdt_t", [R, IH], F32, kind="ExternalInput")
    wout_t = nc.dram_tensor("wout_t", [NHC, 128, IH], F32, kind="ExternalInput")
    wg_t = nc.dram_tensor("wg_t", [NIC, 128, H], F32, kind="ExternalInput")
    wu_t = nc.dram_tensor("wu_t", [NIC, 128, H], F32, kind="ExternalInput")
    wd_t = nc.dram_tensor("wd_t", [NHC, 128, ICF], F32, kind="ExternalInput")
    convw_s = nc.dram_tensor("convw_s", [128, 64], F32, kind="ExternalInput")
    convb_s = nc.dram_tensor("convb_s", [128, 16], F32, kind="ExternalInput")
    dtb_s = nc.dram_tensor("dtb_s", [128, 16], F32, kind="ExternalInput")
    A_s = nc.dram_tensor("A_s", [128, 256], F32, kind="ExternalInput")
    D_s = nc.dram_tensor("D_s", [128, 16], F32, kind="ExternalInput")
    gs_s = nc.dram_tensor("gs_s", [128, NIC], F32, kind="ExternalInput")
    us_s = nc.dram_tensor("us_s", [128, NIC], F32, kind="ExternalInput")
    ds_s = nc.dram_tensor("ds_s", [128, NHC], F32, kind="ExternalInput")
    dtln_s = nc.dram_tensor("dtln_s", [128, 1], F32, kind="ExternalInput")
    bln_s = nc.dram_tensor("bln_s", [16, 1], F32, kind="ExternalInput")
    cln_s = nc.dram_tensor("cln_s", [16, 1], F32, kind="ExternalInput")
    oneh_s = nc.dram_tensor("oneh_s", [16, NN * 128], F32, kind="ExternalInput")
    cst_in = nc.dram_tensor("cst_in", [128, NCST], F32, kind="ExternalInput")

    o_out = nc.dram_tensor("o_out", [H, LT], F32, kind="ExternalOutput")

    d_xn1 = nc.dram_tensor("d_xn1", [H, LT], BF16)
    d_x1 = nc.dram_tensor("d_x1", [H, LT], F32)
    d_x2 = nc.dram_tensor("d_x2", [H, LT], F32)
    d_xn3 = nc.dram_tensor("d_xn3", [H, LT], BF16)
    d_h = nc.dram_tensor("d_h", [MYKT, 128, LG], F32)
    d_u = nc.dram_tensor("d_u", [MYKT, 128, LG], F32)
    d_gate = nc.dram_tensor("d_gate", [MYKT, 128, LG], F32)
    d_bc = nc.dram_tensor("d_bc", [32, LG], F32)
    d_wg = nc.dram_tensor("d_wg", [NIC, 128, H], BF16)
    d_wu = nc.dram_tensor("d_wu", [NIC, 128, H], BF16)
    d_wd = nc.dram_tensor("d_wd", [NHC, 128, ICF], BF16)
    d_wout = nc.dram_tensor("d_wout", [NHC, 128, IH], BF16)
    cc1_in = [nc.dram_tensor(f"cc1i{s}", [H, BLK], BF16) for s in range(2)]
    cc1_out = [nc.dram_tensor(f"cc1o{s}", [2 * H, BLK], BF16) for s in range(2)]
    cc2_in = [nc.dram_tensor(f"cc2i{s}", [160, LT], F32) for s in range(2)]
    cc2_out = [nc.dram_tensor(f"cc2o{s}", [160, LT], F32) for s in range(2)]
    cc3_in = [nc.dram_tensor(f"cc3i{j}", [2 * H, LC], F32) for j in range(4)]
    cc3_out = [nc.dram_tensor(f"cc3o{j}", [H, LC], F32) for j in range(4)]

    with ExitStack() as ctx:
        sb = lambda name, shape, dt_: ctx.enter_context(nc.sbuf_tensor(name, shape, dt_))
        cst = sb("cst", [128, NCST], F32)
        ones = sb("ones", [128, 1], F32)
        onesr = sb("onesr", [1, 128], F32)
        convw = sb("convw", [128, 64], F32)
        convb = sb("convb", [128, 16], F32)
        dtb = sb("dtb", [128, 16], F32)
        A_sb = sb("A_sb", [128, 256], F32)
        D_sb = sb("D_sb", [128, 16], F32)
        gs = sb("gs", [128, NIC], F32)
        us = sb("us", [128, NIC], F32)
        ds = sb("ds", [128, NHC], F32)
        hds = sb("hds", [128, NHC], F32)
        dtln = sb("dtln", [128, 1], F32)
        bln = sb("bln", [16, 1], F32)
        cln = sb("cln", [16, 1], F32)
        oneh = sb("oneh", [16, NN * 128], F32)
        wx_bf = sb("wx_bf", [128, MYKT * 160], BF16)
        wdt_bf = sb("wdt_bf", [128, IH], BF16)
        ts_n = sb("ts_n", [128, LG], BF16)
        state = sb("state", [128, MYKT * NN], F32)
        s_ring = sb("s_ring", [128, KB * (LC + 1)], F32)
        stat_row = sb("stat_row", [4, 512], F32)
        AR_N = 82000
        arena = Arena(sb("arena", [128, AR_N], BF16), AR_N)

        ps_t = [ctx.enter_context(nc.psum_tensor(f"ps{i}", [128, 512], F32))
                for i in range(8)]
        banks = Banks(ps_t)

        p = Prog()

        def ld(dst_ap, src_ap, after=(), q="sq", chan=None):
            return p.do(q, lambda e, d=dst_ap, s=src_ap: e.dma_start(out=d, in_=s),
                        after=after, dma=True, chan=chan)

        st_ = ld

        def cc(kind, op_, in_ap, out_ap, after, chan):
            return p.do("gq", lambda e, k=kind, o=op_, i=in_ap, u=out_ap:
                        e.collective_compute(k, o, replica_groups=PAIRS,
                                             ins=[i], outs=[u]),
                        after=after, cc=True, chan=chan)

        def mm(bank_i, lhsT, rhs, start, stop, after=(), base=0, m=128):
            out = banks.t[bank_i][base:base + m, :rhs.shape[-1]]
            extra = [banks.r[bank_i]] if start else []
            op = p.do("pe", lambda e, o=out, lh=lhsT, r=rhs, s1=start, s2=stop:
                      e.matmul(o, lh, r, start=s1, stop=s2),
                      after=list(after) + extra)
            banks.w[bank_i] = op
            return op

        def bank_read(bi_, op):
            banks.r[bi_] = op
            return op

        def act(out, in_, func, bias=0.0, scale=1.0, after=()):
            return p.do("act", lambda e, o=out, i=in_, f=func, b=bias, s=scale:
                        e.activation(o, i, f, bias=b, scale=s), after=after)

        def tt(out, a, b_, aop, after=(), eng="dve"):
            return p.do(eng, lambda e, o=out, x=a, y=b_, op_=aop:
                        e.tensor_tensor(o, x, y, op_), after=after)

        def ts(out, a, s1, s2, op0, op1=None, after=(), eng="dve"):
            if op1 is None:
                return p.do(eng, lambda e, o=out, x=a, u=s1, v=s2, p0=op0:
                            e.tensor_scalar(o, x, u, v, p0), after=after)
            return p.do(eng, lambda e, o=out, x=a, u=s1, v=s2, p0=op0, p1=op1:
                        e.tensor_scalar(o, x, u, v, p0, p1), after=after)

        def stt(out, in0, scal, in1, op0, op1, after=(), eng="dve"):
            return p.do(eng, lambda e, o=out, x=in0, s=scal, y=in1, p0=op0, p1=op1:
                        e.scalar_tensor_tensor(o, x, s, y, p0, p1), after=after)

        def cp(out, in_, after=(), eng="dve"):
            return p.do(eng, lambda e, o=out, i=in_: e.tensor_copy(o, i), after=after)

        def memset(ap, v, after=(), eng="dve"):
            return p.do(eng, lambda e, a=ap, vv=v: e.memset(a, vv), after=after)

        def scan(out, d0, d1, init, after=()):
            return p.do("dve", lambda e, o=out, a=d0, b_=d1, i=init:
                        e.tensor_tensor_scan(o, a, b_, i, ALU.mult, ALU.add),
                        after=after)

        cEPS1 = cst[0:1, C_EPS:C_EPS + 1]
        cNEGH1 = cst[0:1, C_NEGH:C_NEGH + 1]
        c151 = cst[0:1, C_15:C_15 + 1]
        cI2048 = cst[0:1, C_I2048:C_I2048 + 1]
        cI128 = cst[0:1, C_I128:C_I128 + 1]
        cI16 = cst[0:1, C_I16:C_I16 + 1]

        stat_busy = [None]

        def rstd_chain(src_row, prescale, rows, cols, after=()):
            row_v, row_r0, row_t = rows
            o1 = ts(row_v[:, 0:cols], src_row, prescale, cEPS1,
                    ALU.mult, ALU.add, after=after)
            o2 = act(row_r0[:, 0:cols], row_v[:, 0:cols], AF.Ln, after=[o1])
            o3 = act(row_r0[:, 0:cols], row_r0[:, 0:cols], AF.Exp, scale=cNEGH1,
                     after=[o2])
            o4 = tt(row_t[:, 0:cols], row_r0[:, 0:cols], row_r0[:, 0:cols],
                    ALU.mult, after=[o3])
            o5 = tt(row_t[:, 0:cols], row_t[:, 0:cols], row_v[:, 0:cols],
                    ALU.mult, after=[o4, o1])
            o6 = ts(row_t[:, 0:cols], row_t[:, 0:cols], cNEGH1, c151,
                    ALU.mult, ALU.add, after=[o5])
            o7 = tt(stat_row[0:1, 0:cols], row_r0[:, 0:cols], row_t[:, 0:cols],
                    ALU.mult, after=[o6, o3, stat_busy[0]])
            return o7

        lds = []
        for dst, src in ((cst, cst_in), (convw, convw_s), (convb, convb_s),
                         (dtb, dtb_s), (A_sb, A_s), (D_sb, D_s), (gs, gs_s),
                         (us, us_s), (ds, ds_s), (dtln, dtln_s)):
            lds.append(ld(dst[:], src.ap(), chan="p0c"))
        lds.append(ld(bln[0:16, :], bln_s.ap(), chan="p0c"))
        lds.append(ld(cln[0:16, :], cln_s.ap(), chan="p0c"))
        lds.append(ld(oneh[0:16, :], oneh_s.ap(), chan="p0c"))
        wx_src = bass.AP(wx_t, 0, [[160, 128], [128 * 160, MYKT], [1, 160]])
        lds.append(ld(wx_bf[:], wx_src, q="gq", chan="p0g"))
        lds.append(ld(wdt_bf[:], wdt_t.ap(), q="gq", chan="p0g"))

        memset(ones[:], 1.0)
        memset(onesr[0:1, :], 1.0)
        ts(hds[:], ds[:], cst[:, C_HALF:C_HALF + 1], None, ALU.mult, after=lds)
        memset(state[:], 0.0)

        m0 = arena.mark()
        wc_f = [arena.alloc(8192, F32) for _ in range(2)]
        wc_b = [arena.alloc(8192, BF16) for _ in range(2)]
        cast_jobs = []
        for ic0 in range(0, NIC, 4):
            n = min(4, NIC - ic0)
            for (src, dst) in ((wg_t, d_wg), (wu_t, d_wu)):
                ap_pat = [[H, 128], [128 * H, n], [1, H]]
                cast_jobs.append((bass.AP(src, ic0 * 128 * H, ap_pat),
                                  bass.AP(dst, ic0 * 128 * H, ap_pat), n * H))
        for hc in range(NHC):
            ap_pat = [[ICF, 128], [1, ICF]]
            cast_jobs.append((bass.AP(wd_t, hc * 128 * ICF, ap_pat),
                              bass.AP(d_wd, hc * 128 * ICF, ap_pat), ICF))
        for ot in range(0, NHC, 2):
            ap_pat = [[IH, 128], [128 * IH, 2], [1, IH]]
            cast_jobs.append((bass.AP(wout_t, ot * 128 * IH, ap_pat),
                              bass.AP(d_wout, ot * 128 * IH, ap_pat), 2 * IH))
        cast_c = [None, None]
        cast_s = [None, None]
        for j, (srcap, dstap, cols) in enumerate(cast_jobs):
            bi = j % 2
            l_ = ld(wc_f[bi][:, 0:cols], srcap, after=[cast_c[bi]],
                    chan=f"cstl{bi}")
            c_ = act(wc_b[bi][:, 0:cols], wc_f[bi][:, 0:cols], AF.Copy,
                     after=[l_, cast_s[bi]])
            s_ = st_(dstap, wc_b[bi][:, 0:cols], after=[c_], chan=f"csts{bi}")
            cast_c[bi] = c_
            cast_s[bi] = s_
        p.barrier()
        arena.reset(m0)

        def ln_phase(src_dram, dst_fn):
            m0 = arena.mark()
            x_sb = [arena.alloc(8192, F32) for _ in range(2)]
            sq_sb = [arena.alloc(512, F32) for _ in range(2)]
            rows = [arena.alloc(512, F32)[0:1, :] for _ in range(3)]
            row_mu = arena.alloc(512, F32)[0:1, :]
            row_oc = arena.alloc(512, F32)[0:1, :]
            t1 = arena.alloc(512, F32)
            xn_sb = [arena.alloc(8192, BF16) for _ in range(2)]
            lu_x = [None, None]
            lu_xn = [None, None]
            oc_busy = [None]
            sqmm = [None, None]
            for t in range(NTF):
                bi = t % 2
                bk_x = 4 + bi
                bk_s = 6 + bi
                bk_r = 0 + bi
                bk_c = 2 + bi
                ldx = []
                for g in range(4):
                    srcap = bass.AP(src_dram, g * 4 * 128 * LT + t * TFF,
                                    [[LT, 128], [128 * LT, 4], [1, TFF]])
                    ldx.append(ld(x_sb[bi][:, g * 2048:(g + 1) * 2048], srcap,
                                  after=[lu_x[bi]], chan=f"lnx{bi}"))
                mmx = None
                mms = None
                for hc in range(NHC):
                    xs = x_sb[bi][:, hc * 512:(hc + 1) * 512]
                    mmx = mm(bk_x, ones[:, 0:1], xs, start=(hc == 0),
                             stop=(hc == NHC - 1), after=ldx, base=0, m=1)
                    sq_ev = act(sq_sb[hc % 2], xs, AF.Square,
                                after=ldx + [sqmm[hc % 2]])
                    mms = mm(bk_s, ones[:, 0:1], sq_sb[hc % 2],
                             start=(hc == 0), stop=(hc == NHC - 1),
                             after=[sq_ev], base=0, m=1)
                    sqmm[hc % 2] = mms
                mu = act(row_mu, banks.t[bk_x][0:1, :], AF.Copy, scale=cI2048,
                         after=[mmx])
                bank_read(bk_x, mu)
                ex2 = act(rows[0][0:1, :], banks.t[bk_s][0:1, :], AF.Copy,
                          scale=cI2048, after=[mms])
                bank_read(bk_s, ex2)
                mu2 = act(rows[2][0:1, :], row_mu, AF.Square, after=[mu])
                var = tt(rows[0][0:1, :], rows[0][0:1, :], rows[2][0:1, :],
                         ALU.subtract, after=[ex2, mu2])
                r7 = rstd_chain(rows[0][0:1, :], cst[0:1, C_ONE:C_ONE + 1],
                                rows, 512, after=[var])
                oc = tt(row_oc, row_mu, stat_row[0:1, :], ALU.mult,
                        after=[r7, mu, oc_busy[0]])
                br = mm(bk_r, onesr[0:1, 0:128], stat_row[0:1, :],
                        start=True, stop=True, after=[r7])
                bc_ = mm(bk_c, onesr[0:1, 0:128], row_oc,
                         start=True, stop=True, after=[oc])
                stat_busy[0] = [br, oc]
                oc_busy[0] = bc_
                m2s = []
                for hc in range(NHC):
                    xs = x_sb[bi][:, hc * 512:(hc + 1) * 512]
                    m1 = tt(t1, xs, banks.t[bk_r][:], ALU.mult,
                            after=[br, lu_xn[bi], (m2s[-1] if m2s else None)])
                    m2 = tt(xn_sb[bi][:, hc * 512:(hc + 1) * 512], t1,
                            banks.t[bk_c][:], ALU.subtract, after=[m1, bc_])
                    m2s.append(m2)
                bank_read(bk_r, m2s[-1])
                bank_read(bk_c, m2s[-1])
                lu_x[bi] = m2s[-1]
                so = dst_fn(t, xn_sb[bi], [m2s[-1]], bi)
                lu_xn[bi] = so
                if paranoid & 1:
                    p.barrier()
            p.barrier()
            arena.reset(m0)

        def mk_dram_dst(dst_dram):
            def f(t, xn_tile, after, bi):
                outs = []
                for g in range(4):
                    dstap = bass.AP(dst_dram, g * 4 * 128 * LT + t * TFF,
                                    [[LT, 128], [128 * LT, 4], [1, TFF]])
                    outs.append(st_(dstap, xn_tile[:, g * 2048:(g + 1) * 2048],
                                    after=after, chan=f"lno{bi}"))
                return outs[-1]
            return f

        cc1_store_last = [[], []]

        def cc1_dst(t, xn_tile, after, bi):
            s = t // 2
            coff = (t % 2) * TFF
            outs = []
            for g in range(4):
                dstap = bass.AP(cc1_in[s], g * 4 * 128 * BLK + coff,
                                [[BLK, 128], [128 * BLK, 4], [1, TFF]])
                outs.append(st_(dstap, xn_tile[:, g * 2048:(g + 1) * 2048],
                                after=after, chan=f"lno{bi}"))
            cc1_store_last[s].append(outs[-1])
            return outs[-1]

        def ffn_phase(xn_dram, res_dram, out_dram):
            m0 = arena.mark()
            xn_T = [arena.alloc(8192, BF16) for _ in range(2)]
            hdn_T = arena.alloc(NIC * 512, BF16)
            wgb = [arena.alloc(2048, BF16) for _ in range(2)]
            wub = [arena.alloc(2048, BF16) for _ in range(2)]
            wdb = [arena.alloc(ICF, BF16) for _ in range(2)]
            gsl = [arena.alloc(512, F32) for _ in range(2)]
            x0b = [arena.alloc(512, F32) for _ in range(2)]
            x1b = [arena.alloc(512, F32) for _ in range(2)]
            lu_xn = [None, None]
            lu_wg = [None, None]; lu_wu = [None, None]; lu_wd = [None, None]
            lu_gs = [None, None]
            lu_x0 = [None, None]; lu_x1 = [None, None]
            hdn_reader = None
            for t in range(NTF):
                bi = t % 2
                ldx = []
                for g in range(4):
                    srcap = bass.AP(xn_dram, g * 4 * 128 * LT + t * TFF,
                                    [[LT, 128], [128 * LT, 4], [1, TFF]])
                    ldx.append(ld(xn_T[bi][:, g * 2048:(g + 1) * 2048], srcap,
                                  after=[lu_xn[bi]], chan=f"fxn{bi}"))
                hdn_ops = []
                for ic in range(NIC):
                    wi = ic % 2
                    bg = wi * 2
                    bu = wi * 2 + 1
                    lg = ld(wgb[wi][:], bass.AP(d_wg, ic * 128 * H, [[H, 128], [1, H]]),
                            after=[lu_wg[wi]], chan=f"fwg{wi}")
                    lu_ = ld(wub[wi][:], bass.AP(d_wu, ic * 128 * H, [[H, 128], [1, H]]),
                             after=[lu_wu[wi]], chan=f"fwu{wi}")
                    gm = um = None
                    for hc in range(NHC):
                        gm = mm(bg, wgb[wi][:, hc * 128:(hc + 1) * 128],
                                xn_T[bi][:, hc * 512:(hc + 1) * 512],
                                start=(hc == 0), stop=(hc == NHC - 1),
                                after=ldx + [lg])
                        um = mm(bu, wub[wi][:, hc * 128:(hc + 1) * 128],
                                xn_T[bi][:, hc * 512:(hc + 1) * 512],
                                start=(hc == 0), stop=(hc == NHC - 1),
                                after=ldx + [lu_])
                    lu_wg[wi] = gm
                    lu_wu[wi] = um
                    eg = act(gsl[wi], banks.t[bg][:], AF.Silu,
                             scale=gs[:, ic:ic + 1], after=[gm, lu_gs[wi]])
                    bank_read(bg, eg)
                    hm = stt(hdn_T[:, ic * 512:(ic + 1) * 512],
                             banks.t[bu][:], us[:, ic:ic + 1], gsl[wi],
                             ALU.mult, ALU.mult,
                             after=[um, eg, hdn_reader])
                    bank_read(bu, hm)
                    lu_gs[wi] = hm
                    hdn_ops.append(hm)
                lu_xn[bi] = hdn_ops[-1]
                ev_last = None
                for hc in range(NHC):
                    wi = hc % 2
                    bd = 4 + wi
                    lw = ld(wdb[wi][:], bass.AP(d_wd, hc * 128 * ICF,
                                                [[ICF, 128], [1, ICF]]),
                            after=[lu_wd[wi]], chan=f"fwd{wi}")
                    lx = ld(x0b[wi][:], bass.AP(res_dram, hc * 128 * LT + t * TFF,
                                                [[LT, 128], [1, TFF]]),
                            after=[lu_x0[wi]], chan=f"fx0{wi}")
                    dm = None
                    for ic in range(NIC):
                        dm = mm(bd, wdb[wi][:, ic * 128:(ic + 1) * 128],
                                hdn_T[:, ic * 512:(ic + 1) * 512],
                                start=(ic == 0), stop=(ic == NIC - 1),
                                after=[lw, hdn_ops[ic]])
                    lu_wd[wi] = dm
                    ev = stt(x1b[wi][:], banks.t[bd][:], hds[:, hc:hc + 1],
                             x0b[wi][:], ALU.mult, ALU.add,
                             after=[dm, lx, lu_x1[wi]])
                    bank_read(bd, ev)
                    lu_x0[wi] = ev
                    so = st_(bass.AP(out_dram, hc * 128 * LT + t * TFF,
                                     [[LT, 128], [1, TFF]]),
                             x1b[wi][:], after=[ev], chan=f"fx1{wi}")
                    lu_x1[wi] = so
                    ev_last = ev
                hdn_reader = ev_last
                if paranoid & 2:
                    p.barrier()
            p.barrier()
            arena.reset(m0)

        def pass1a():
            m0 = arena.mark()
            xnb = [arena.alloc(16 * BLK, BF16) for _ in range(2)]
            winb = [arena.alloc(2048, BF16) for _ in range(2)]
            hst = [arena.alloc(1024, F32) for _ in range(2)]
            lu_xn = [None, None]
            lu_w = [None, None]
            lu_h = [None, None]
            blocks = [(0, 0, 0), (2, 2048, 0), (1, 0, 1), (3, 2048, 1)]
            for bki, (blk, rowb, sup) in enumerate(blocks):
                bbi = bki % 2
                ldx = []
                for g in range(4):
                    srcap = bass.AP(cc1_out[sup], (rowb + g * 4 * 128) * BLK,
                                    [[BLK, 128], [128 * BLK, 4], [1, BLK]])
                    ldx.append(ld(xnb[bbi][:, g * 4 * BLK:(g + 1) * 4 * BLK], srcap,
                                  after=[lu_xn[bbi], cc_ops["cc1"][sup]],
                                  chan=f"paxn{bbi}"))
                for ot in range(NOT):
                    wi = ot % 2
                    lw = ld(winb[wi][:],
                            bass.AP(w_in_t, ot * 128 * H, [[H, 128], [1, H]]),
                            after=[lu_w[wi]], q="gq", chan=f"paw{wi}")
                    is_h = ot < MYKT
                    hi = wi
                    evs = []
                    for c4 in range(2):
                        bpj = 6 + (c4 % 2)
                        pm = None
                        for hc in range(NHC):
                            pm = mm(bpj, winb[wi][:, hc * 128:(hc + 1) * 128],
                                    xnb[bbi][:, hc * BLK + c4 * 512:
                                             hc * BLK + c4 * 512 + 512],
                                    start=(hc == 0), stop=(hc == NHC - 1),
                                    after=ldx + [lw])
                        if is_h:
                            ev = act(hst[hi][:, c4 * 512:(c4 + 1) * 512],
                                     banks.t[bpj][:], AF.Copy,
                                     after=[pm, lu_h[hi]])
                        else:
                            ev = act(hst[hi][:, c4 * 512:(c4 + 1) * 512],
                                     banks.t[bpj][:], AF.Silu,
                                     after=[pm, lu_h[hi]])
                        bank_read(bpj, ev)
                        evs.append(ev)
                    lu_w[wi] = evs[-1]
                    tgt = d_h if is_h else d_gate
                    kt = ot if is_h else ot - MYKT
                    so = st_(bass.AP(tgt, kt * 128 * LG + blk * BLK,
                                     [[LG, 128], [1, BLK]]),
                             hst[hi][:], after=evs, chan=f"pah{hi}")
                    lu_h[hi] = so
                if paranoid & 4:
                    p.barrier()
            p.barrier()
            arena.reset(m0)

        def pass1b():
            m0 = arena.mark()
            hbuf = [arena.alloc(LT + 3, F32) for _ in range(2)]
            acc_a = [arena.alloc(LT, F32) for _ in range(2)]
            acc_b = [arena.alloc(LT, F32) for _ in range(2)]
            ubf = [arena.alloc(LT, BF16) for _ in range(2)]
            tsev = arena.alloc(2048, F32)
            bcev = arena.alloc(2048, F32)
            lu_h = [None, None]
            lu_u = [None, None]
            acc_busy = [None, None]
            ev_stores = []
            for half in range(2):
                col0 = half * LT
                for ot in range(MYKT):
                    wi = ot % 2
                    if half == 0:
                        hz = memset(hbuf[wi][:, 0:3], 0.0, after=[lu_h[wi]])
                        lh = ld(hbuf[wi][:, 3:3 + LT],
                                bass.AP(d_h, ot * 128 * LG + col0,
                                        [[LG, 128], [1, LT]]),
                                after=[lu_h[wi]], chan=f"pbh{wi}")
                        lh = [lh, hz]
                    else:
                        lh = [ld(hbuf[wi][:, 0:3 + LT],
                                 bass.AP(d_h, ot * 128 * LG + col0 - 3,
                                         [[LG, 128], [1, LT + 3]]),
                                 after=[lu_h[wi]], chan=f"pbh{wi}")]
                    c0 = ts(acc_a[wi], hbuf[wi][:, 0:LT],
                            convw[:, ot * 4:ot * 4 + 1], None, ALU.mult,
                            after=lh + [acc_busy[wi]])
                    c1 = stt(acc_b[wi], hbuf[wi][:, 1:1 + LT],
                             convw[:, ot * 4 + 1:ot * 4 + 2], acc_a[wi],
                             ALU.mult, ALU.add, after=[c0])
                    c2 = stt(acc_a[wi], hbuf[wi][:, 2:2 + LT],
                             convw[:, ot * 4 + 2:ot * 4 + 3], acc_b[wi],
                             ALU.mult, ALU.add, after=[c1])
                    c3 = stt(acc_b[wi], hbuf[wi][:, 3:3 + LT],
                             convw[:, ot * 4 + 3:ot * 4 + 4], acc_a[wi],
                             ALU.mult, ALU.add, after=[c2])
                    lu_h[wi] = c3
                    sl = act(acc_a[wi], acc_b[wi], AF.Silu,
                             bias=convb[:, ot:ot + 1], after=[c3])
                    su = st_(bass.AP(d_u, ot * 128 * LG + col0,
                                     [[LG, 128], [1, LT]]),
                             acc_a[wi], after=[sl], chan=f"pbu{wi}")
                    cb = act(ubf[wi][:], acc_a[wi], AF.Copy, after=[sl, lu_u[wi]])
                    acc_busy[wi] = [su, cb]
                    mm_last = []
                    for c4 in range(4):
                        bts = c4
                        bbc = 4 + c4
                        u_sl = ubf[wi][:, c4 * 512:(c4 + 1) * 512]
                        m1 = mm(bts, wx_bf[:, ot * 160:ot * 160 + 128], u_sl,
                                start=(ot == 0), stop=(ot == MYKT - 1),
                                after=[cb])
                        m2 = mm(bbc, wx_bf[:, ot * 160 + 128:ot * 160 + 160],
                                u_sl, start=(ot == 0), stop=(ot == MYKT - 1),
                                after=[cb], m=32)
                        mm_last.append(m1)
                        mm_last.append(m2)
                    lu_u[wi] = mm_last
                evs = []
                for c4 in range(4):
                    e1 = act(tsev[:, c4 * 512:(c4 + 1) * 512], banks.t[c4][:],
                             AF.Copy, after=[banks.w[c4]] + ev_stores)
                    bank_read(c4, e1)
                    e2 = act(bcev[0:32, c4 * 512:(c4 + 1) * 512],
                             banks.t[4 + c4][0:32, :], AF.Copy,
                             after=[banks.w[4 + c4]] + ev_stores)
                    bank_read(4 + c4, e2)
                    evs += [e1, e2]
                s1 = st_(bass.AP(cc2_in[half], 0, [[LT, 128], [1, LT]]),
                         tsev[:], after=evs, chan="pbt")
                s2 = st_(bass.AP(cc2_in[half], 128 * LT, [[LT, 32], [1, LT]]),
                         bcev[0:32, :], after=evs, chan="pbt")
                ev_stores = [s1, s2]
                cc_ops["cc2"][half] = cc("AllReduce", ALU.add,
                                         cc2_in[half].ap(), cc2_out[half].ap(),
                                         after=[s1, s2], chan=f"cc2_{half}")
                if paranoid & 4:
                    p.barrier()
            p.barrier()
            arena.reset(m0)

        def rms_phase():
            m0 = arena.mark()
            tsb = [arena.alloc(512, F32) for _ in range(2)]
            bb_ = [arena.alloc(512, F32) for _ in range(2)]
            cb_ = [arena.alloc(512, F32) for _ in range(2)]
            sq = arena.alloc(512, F32)
            rows = [arena.alloc(512, F32)[0:1, :] for _ in range(3)]
            tf = arena.alloc(512, F32)
            lu = [None, None]
            lu_bc = [None, None]
            for c in range(NCHG):
                half = c // 4
                coff = (c % 4) * 512
                bi = c % 2
                lt_ = ld(tsb[bi][:], bass.AP(cc2_out[half], coff,
                                             [[LT, 128], [1, 512]]),
                         after=[lu[bi], cc_ops["cc2"][half]], chan=f"rmt{bi}")
                lb_ = ld(bb_[bi][0:16, :], bass.AP(cc2_out[half], 128 * LT + coff,
                                                   [[LT, 16], [1, 512]]),
                         after=[lu_bc[bi], cc_ops["cc2"][half]], chan=f"rmb{bi}")
                lc2 = ld(cb_[bi][0:16, :], bass.AP(cc2_out[half], 144 * LT + coff,
                                                   [[LT, 16], [1, 512]]),
                         after=[lu_bc[bi], cc_ops["cc2"][half]], chan=f"rmb{bi}")
                sq1 = act(sq, tsb[bi][:], AF.Square, after=[lt_])
                sm = mm(6, ones[:, 0:1], sq, start=True, stop=True,
                        after=[sq1], m=1)
                r7 = rstd_chain(banks.t[6][0:1, :], cI128, rows, 512, after=[sm])
                bank_read(6, r7)
                br = mm(7, onesr[0:1, 0:128], stat_row[0:1, :], start=True,
                        stop=True, after=[r7])
                stat_busy[0] = br
                t1_ = tt(tf, tsb[bi][:], banks.t[7][:], ALU.mult,
                         after=[br, sq1])
                t2_ = ts(ts_n[:, c * 512:(c + 1) * 512], tf, dtln[:, 0:1],
                         None, ALU.mult, after=[t1_])
                bank_read(7, t2_)
                lu[bi] = t2_
                prev = t2_
                for half2, w_ln in ((0, bln), (1, cln)):
                    src16 = bb_[bi][0:16, :] if half2 == 0 else cb_[bi][0:16, :]
                    sq2 = act(sq[0:16, :], src16, AF.Square,
                              after=[lb_, lc2, prev, sm])
                    sm2 = mm(6, ones[0:16, 0:1], sq[0:16, :], start=True,
                             stop=True, after=[sq2], m=1)
                    r72 = rstd_chain(banks.t[6][0:1, :], cI16, rows, 512,
                                     after=[sm2])
                    bank_read(6, r72)
                    br2 = mm(7, onesr[0:1, 0:128], stat_row[0:1, :],
                             start=True, stop=True, after=[r72])
                    stat_busy[0] = br2
                    n1 = tt(tf[0:16, :], src16, banks.t[7][0:16, :], ALU.mult,
                            after=[br2, sq2])
                    n2 = ts(src16, tf[0:16, :],
                            w_ln[0:16, 0:1], None, ALU.mult, after=[n1])
                    bank_read(7, n2)
                    sb2 = st_(bass.AP(d_bc, half2 * 16 * LG + c * 512,
                                      [[LG, 16], [1, 512]]),
                              src16, after=[n2], chan="rmo")
                    prev = sb2
                lu_bc[bi] = prev
            p.barrier()
            arena.reset(m0)

        def scan_phase():
            m0 = arena.mark()
            Bb = arena.alloc(NN * 512, F32)
            Cb = arena.alloc(NN * 512, F32)
            bst = arena.alloc(512, F32)
            cstg = arena.alloc(512, F32)
            ub = [arena.alloc(KB * 512, F32) for _ in range(2)]
            gat = [arena.alloc(KB * 512, F32)]
            delta = arena.alloc(KB * 512, F32)
            du = arena.alloc(KB * 512, F32)
            yb = arena.alloc(KB * 512, F32)
            dab = [arena.alloc(LC + 1, F32) for _ in range(2)]
            dbb = [arena.alloc(LC + 1, F32) for _ in range(2)]
            ymt_d = arena.alloc(512, F32)
            ymt_p = arena.alloc(512, F32)
            tmp_sp = [arena.alloc(512, F32) for _ in range(2)]
            y_g = arena.alloc(MYKT * 512, BF16)
            woutb = [arena.alloc(IH, BF16) for _ in range(2)]
            wev = [arena.alloc(512, F32)]
            pad_ok = [memset(dab[0][:, LC:LC + 1], 1.0),
                      memset(dab[1][:, LC:LC + 1], 1.0),
                      memset(dbb[0][:, LC:LC + 1], 0.0),
                      memset(dbb[1][:, LC:LC + 1], 0.0)]
            sring_v = s_ring[:].rearrange("p (k l) -> p k l", l=LC + 1)
            state_v = state[:].rearrange("p (k n) -> p k n", n=NN)
            tbl_busy = [None]
            lu_u = [None, None]; lu_g = [None, None]
            lu_w = [None, None]
            lu_wev = [None]
            delta_busy = None
            du_busy = None
            tmp_busy = [None, None]
            st_carry = None
            scan_carry = {}
            yg_last = {"dve": None, "pool": None}
            wout_done = None
            wout_stores = []
            for c in range(NCHG):
                lbst = ld(bst[0:16, :], bass.AP(d_bc, c * 512,
                                                [[LG, 16], [1, 512]]),
                          after=[tbl_busy[0]], chan="stb")
                lcst = ld(cstg[0:16, :], bass.AP(d_bc, 16 * LG + c * 512,
                                                 [[LG, 16], [1, 512]]),
                          after=[tbl_busy[0]], chan="stc")
                tev = []
                for n in range(NN):
                    bsel = n % 2
                    mb = mm(bsel, oneh[0:16, n * 128:(n + 1) * 128],
                            bst[0:16, :], start=True,
                            stop=True, after=[tbl_busy[0], lbst])
                    eb = act(Bb[:, n * 512:(n + 1) * 512], banks.t[bsel][:],
                             AF.Copy, after=[mb, tbl_busy[0]])
                    bank_read(bsel, eb)
                    mc = mm(bsel, oneh[0:16, n * 128:(n + 1) * 128],
                            cstg[0:16, :], start=True,
                            stop=True, after=[eb, lcst])
                    ec = act(Cb[:, n * 512:(n + 1) * 512], banks.t[bsel][:],
                             AF.Copy, after=[mc])
                    bank_read(bsel, ec)
                    tev.append(ec)
                tev_last = tev[-1]
                for kb in range(NKB):
                    bi = kb % 2
                    d1_ops = []
                    y0_ops = []
                    lg_ops = []
                    delta_ops = []
                    for kk in range(KB):
                        k = kb * KB + kk
                        lh = ld(ub[bi][:, kk * 512:kk * 512 + LC],
                                bass.AP(d_u, k * 128 * LG + c * LC,
                                        [[LG, 128], [1, LC]]),
                                after=[lu_u[bi]], chan=f"su{bi}")
                        lg = ld(gat[0][:, kk * 512:kk * 512 + LC],
                                bass.AP(d_gate, k * 128 * LG + c * LC,
                                        [[LG, 128], [1, LC]]),
                                after=[lu_g[0]], chan="sg0")
                        bdt = 6 + (kk % 2)
                        dmm = mm(bdt, wdt_bf[:, k * 128:(k + 1) * 128],
                                 ts_n[:, c * LC:(c + 1) * LC], start=True,
                                 stop=True, after=[])
                        dl = delta[:, kk * 512:kk * 512 + LC]
                        sp1 = act(tmp_sp[kk % 2], banks.t[bdt][:, 0:LC],
                                  AF.Abs, bias=dtb[:, k:k + 1],
                                  after=[dmm, tmp_busy[kk % 2]])
                        sp2 = act(tmp_sp[kk % 2], tmp_sp[kk % 2], AF.Exp,
                                  scale=cst[:, C_NEG1:C_NEG1 + 1], after=[sp1])
                        sp3 = act(tmp_sp[kk % 2], tmp_sp[kk % 2], AF.Ln,
                                  bias=cst[:, C_ONE:C_ONE + 1], after=[sp2])
                        sp4 = act(dl, banks.t[bdt][:, 0:LC], AF.Relu,
                                  bias=dtb[:, k:k + 1],
                                  after=[sp3, delta_busy])
                        bank_read(bdt, sp4)
                        sp5 = tt(dl, dl, tmp_sp[kk % 2], ALU.add,
                                 after=[sp4, delta_busy])
                        tmp_busy[kk % 2] = sp5
                        delta_ops.append(sp5)
                        d1 = tt(du[:, kk * 512:kk * 512 + LC], dl,
                                ub[bi][:, kk * 512:kk * 512 + LC], ALU.mult,
                                after=[sp5, lh, du_busy])
                        y0 = ts(yb[:, kk * 512:kk * 512 + LC],
                                ub[bi][:, kk * 512:kk * 512 + LC],
                                D_sb[:, k:k + 1], None, ALU.mult,
                                after=[lh, yg_last["dve"], yg_last["pool"]])
                        d1_ops.append(d1)
                        y0_ops.append(y0)
                        lg_ops.append(lg)
                    lu_u[bi] = [d1_ops[-1], y0_ops[-1]]
                    scan_prev = dict(scan_carry)
                    yop_prev = dict(enumerate(y0_ops))
                    st_prev = st_carry
                    last_ex = None
                    last_db = None
                    for n in range(NN):
                        for kk in range(KB):
                            k = kb * KB + kk
                            abi = kk % 2
                            dl = delta[:, kk * 512:kk * 512 + LC]
                            ex = act(dab[abi][:, 0:LC], dl, AF.Exp,
                                     scale=A_sb[:, k * NN + n:k * NN + n + 1],
                                     after=[scan_prev.get(abi), delta_ops[kk]]
                                     + pad_ok)
                            last_ex = ex
                            db_ = tt(dbb[abi][:, 0:LC],
                                     du[:, kk * 512:kk * 512 + LC],
                                     Bb[:, n * 512:(n + 1) * 512], ALU.mult,
                                     after=[tev_last, scan_prev.get(abi),
                                            d1_ops[kk]])
                            last_db = db_
                            sc = scan(sring_v[:, kk, :], dab[abi][:],
                                      dbb[abi][:],
                                      state[:, k * NN + n:k * NN + n + 1],
                                      after=[ex, db_, yop_prev.get(kk), st_prev])
                            scan_prev[abi] = sc
                            yeng = "dve" if kk < 1 else "pool"
                            ymt = ymt_d if kk < 1 else ymt_p
                            y1 = tt(ymt, sring_v[:, kk, 0:LC],
                                    Cb[:, n * 512:(n + 1) * 512], ALU.mult,
                                    after=[sc, tev_last, yg_last[yeng]],
                                    eng=yeng)
                            y2 = tt(yb[:, kk * 512:kk * 512 + LC],
                                    yb[:, kk * 512:kk * 512 + LC], ymt,
                                    ALU.add, after=[y1, yop_prev.get(kk)],
                                    eng=yeng)
                            yop_prev[kk] = y2
                            yg_last[yeng] = y2
                        scan_ops = [scan_prev[0], scan_prev[1]]
                        st_prev = act(state_v[:, kb * KB:(kb + 1) * KB, n],
                                      sring_v[:, :, LC], AF.Copy,
                                      after=scan_ops + [yop_prev[KB - 1],
                                                        yop_prev[0]])
                    st_carry = st_prev
                    scan_carry = dict(scan_prev)
                    delta_busy = [last_ex, d1_ops[-1]]
                    du_busy = last_db
                    for kk in range(KB):
                        k = kb * KB + kk
                        yeng = "dve" if kk < 1 else "pool"
                        yg = tt(y_g[:, k * 512:k * 512 + LC],
                                yb[:, kk * 512:kk * 512 + LC],
                                gat[0][:, kk * 512:kk * 512 + LC], ALU.mult,
                                after=[yop_prev[kk], st_prev, lg_ops[kk],
                                       yg_last[yeng], wout_done], eng=yeng)
                        yg_last[yeng] = yg
                    lu_g[0] = [yg_last["dve"], yg_last["pool"]]
                wev_ops = []
                for ot in range(NHC):
                    wi = ot % 2
                    bw = 2 + wi
                    lw = ld(woutb[wi][:],
                            bass.AP(d_wout, ot * 128 * IH, [[IH, 128], [1, IH]]),
                            after=[lu_w[wi]], chan=f"sw{wi}")
                    wm = None
                    for kt in range(MYKT):
                        wm = mm(bw, woutb[wi][:, kt * 128:(kt + 1) * 128],
                                y_g[:, kt * 512:(kt + 1) * 512],
                                start=(kt == 0), stop=(kt == MYKT - 1),
                                after=[lw, yg_last["dve"], yg_last["pool"]])
                    lu_w[wi] = wm
                    ev = act(wev[0][:], banks.t[bw][:, 0:LC], AF.Copy,
                             after=[wm, lu_wev[0]])
                    bank_read(bw, ev)
                    j = c % 4
                    rowb = 0 if c < 4 else H
                    so = st_(bass.AP(cc3_in[j], (rowb + ot * 128) * LC,
                                     [[LC, 128], [1, LC]]),
                             wev[0][:], after=[ev], chan="sv0")
                    lu_wev[0] = so
                    wev_ops.append(so)
                    wev_ops.append(ev)
                wout_done = wev_ops[-1]
                tbl_busy[0] = wev_ops[-1]
                wout_stores.append(wev_ops)
                if c >= 4:
                    j = c - 4
                    cc_ops["cc3"][j] = cc("ReduceScatter", ALU.add,
                                          cc3_in[j].ap(), cc3_out[j].ap(),
                                          after=wout_stores[j] + wout_stores[c],
                                          chan=f"cc3_{j}")
                if paranoid & 8:
                    p.barrier()
            p.barrier()
            arena.reset(m0)

        def x2_phase():
            m0 = arena.mark()
            a_ = [arena.alloc(512, F32) for _ in range(2)]
            b_ = [arena.alloc(512, F32) for _ in range(2)]
            lu = [None, None]
            for j in range(4):
                for hc in range(NHC):
                    bi = (j * NHC + hc) % 2
                    l1 = ld(a_[bi][:], bass.AP(d_x1, hc * 128 * LT + j * LC,
                                               [[LT, 128], [1, LC]]),
                            after=[lu[bi]], chan=f"xa{bi}")
                    l2 = ld(b_[bi][:], bass.AP(cc3_out[j], hc * 128 * LC,
                                               [[LC, 128], [1, LC]]),
                            after=[lu[bi], cc_ops["cc3"][j]], chan=f"xb{bi}")
                    ad = tt(a_[bi][:], a_[bi][:], b_[bi][:], ALU.add,
                            after=[l1, l2])
                    so = st_(bass.AP(d_x2, hc * 128 * LT + j * LC,
                                     [[LT, 128], [1, LC]]),
                             a_[bi][:], after=[ad], chan=f"xc{bi}")
                    lu[bi] = [ad, so]
            p.barrier()
            arena.reset(m0)

        cc_ops = {"cc1": [None, None], "cc2": [None, None],
                  "cc3": [None, None, None, None]}

        ln_phase(xT, mk_dram_dst(d_xn1))
        ffn_phase(d_xn1, xT, d_x1)
        ln_phase(d_x1, cc1_dst)
        for s in range(2):
            cc_ops["cc1"][s] = cc("AllGather", ALU.bypass, cc1_in[s].ap(),
                                  cc1_out[s].ap(), after=cc1_store_last[s],
                                  chan=f"cc1_{s}")
        pass1a()
        pass1b()
        rms_phase()
        scan_phase()
        x2_phase()
        ln_phase(d_x2, mk_dram_dst(d_xn3))
        ffn_phase(d_xn3, d_x2, o_out)

        if debug:
            p.barrier()
            dbg_map = {"d_xn1": (d_xn1, [H, LT], BF16), "d_x1": (d_x1, [H, LT], F32),
                       "d_h": (d_h, [MYKT, 128, LG], F32),
                       "d_u": (d_u, [MYKT, 128, LG], F32),
                       "d_gate": (d_gate, [MYKT, 128, LG], F32),
                       "d_x2": (d_x2, [H, LT], F32),
                       "cc1o0": (cc1_out[0], [2 * H, BLK], BF16),
                       "cc1o1": (cc1_out[1], [2 * H, BLK], BF16),
                       "cc2o0": (cc2_out[0], [160, LT], F32),
                       "cc2o1": (cc2_out[1], [160, LT], F32),
                       "cc3o0": (cc3_out[0], [H, LC], F32)}
            for name, (src, shape, dt_) in dbg_map.items():
                dbg = nc.dram_tensor("dbg_" + name, shape, dt_,
                                     kind="ExternalOutput")
                ld(dbg.ap(), src.ap(), chan="dbg")
            dbg_tsn = nc.dram_tensor("dbg_tsn", [128, LG], BF16,
                                     kind="ExternalOutput")
            dbg_bn = nc.dram_tensor("dbg_bn", [16, LG], F32,
                                    kind="ExternalOutput")
            dbg_cn = nc.dram_tensor("dbg_cn", [16, LG], F32,
                                    kind="ExternalOutput")
            ld(dbg_tsn.ap(), ts_n[:], chan="dbg")
            ld(dbg_bn.ap(), b_n[0:16, :], chan="dbg")
            ld(dbg_cn.ap(), c_n[0:16, :], chan="dbg")

        p.do("sq", None, after=list(p.chan_last.values()))

        per_stream, keys = p.finalize()
        sems = {k: ctx.enter_context(nc.semaphore(f"sem_{k}")) for k in keys}
        hw = {}
        block = ctx.enter_context(nc.Block())

        @block.sync
        def _(sync):
            Prog.emit_stream(sync, "sync", per_stream["sync"], sems, hw)

        @block.tensor
        def _(tensor):
            Prog.emit_stream(tensor, "tensor", per_stream["tensor"], sems, hw)

        @block.scalar
        def _(scalar):
            Prog.emit_stream(scalar, "scalar", per_stream["scalar"], sems, hw)

        @block.vector
        def _(vector):
            Prog.emit_stream(vector, "vector", per_stream["vector"], sems, hw)

        @block.gpsimd
        def _(gpsimd):
            Prog.emit_stream(gpsimd, "gpsimd", per_stream["gpsimd"], sems, hw)

    return nc


def _prep_shared(inp):
    gw = inp["L0_ffn_gate_w"]
    uw = inp["L0_ffn_up_w"]
    dw = inp["L0_ffn_down_w"]
    d = {}
    d["wg_t"] = np.ascontiguousarray(
        np.asarray(gw).reshape(NIC, 128, NHC, 128).transpose(0, 3, 2, 1)
        .reshape(NIC, 128, H))
    d["wu_t"] = np.ascontiguousarray(
        np.asarray(uw).reshape(NIC, 128, NHC, 128).transpose(0, 3, 2, 1)
        .reshape(NIC, 128, H))
    d["wd_t"] = np.ascontiguousarray(
        np.asarray(dw).reshape(NHC, 128, NIC, 128).transpose(0, 3, 2, 1)
        .reshape(NHC, 128, ICF))
    d["gs_s"] = np.ascontiguousarray(
        np.asarray(inp["L0_ffn_gate_s"]).reshape(NIC, 128).T)
    d["us_s"] = np.ascontiguousarray(
        np.asarray(inp["L0_ffn_up_s"]).reshape(NIC, 128).T)
    d["ds_s"] = np.ascontiguousarray(
        np.asarray(inp["L0_ffn_down_s"]).reshape(NHC, 128).T)
    d["dtln_s"] = np.ascontiguousarray(
        np.asarray(inp["L0_attn_dt_ln_w"]).reshape(128, 1))
    d["bln_s"] = np.ascontiguousarray(
        np.asarray(inp["L0_attn_b_ln_w"]).reshape(16, 1))
    d["cln_s"] = np.ascontiguousarray(
        np.asarray(inp["L0_attn_c_ln_w"]).reshape(16, 1))
    oneh = np.zeros((16, NN * 128), np.float32)
    for n in range(NN):
        oneh[n, n * 128:(n + 1) * 128] = 1.0
    d["oneh_s"] = oneh
    cvals = np.zeros(NCST, np.float32)
    cvals[C_EPS] = EPS
    cvals[C_ONE] = 1.0
    cvals[C_NEG1] = -1.0
    cvals[C_NEGH] = -0.5
    cvals[C_15] = 1.5
    cvals[C_HALF] = 0.5
    cvals[C_I2048] = 1.0 / 2048.0
    cvals[C_I128] = 1.0 / 128.0
    cvals[C_I16] = 1.0 / 16.0
    d["cst_in"] = np.ascontiguousarray(
        np.broadcast_to(cvals.astype(np.float32), (128, NCST)))
    return d


def _prep_core(inp, half):
    ch0 = half * IH
    W_in = np.asarray(inp["L0_attn_W_in"])
    Wx = np.asarray(inp["L0_attn_Wx"])
    Wdt = np.asarray(inp["L0_attn_Wdt"])
    Wout = np.asarray(inp["L0_attn_Wout"])
    d = {}
    w_in_rows = np.concatenate([W_in[ch0:ch0 + IH], W_in[I + ch0:I + ch0 + IH]],
                               axis=0)
    d["w_in_t"] = np.ascontiguousarray(
        w_in_rows.reshape(NOT, 128, NHC, 128).transpose(0, 3, 2, 1)
        .reshape(NOT, 128, H))
    d["wx_t"] = np.ascontiguousarray(
        Wx[:, ch0:ch0 + IH].T.reshape(MYKT, 128, 160))
    d["wdt_t"] = np.ascontiguousarray(Wdt[ch0:ch0 + IH].T)
    d["wout_t"] = np.ascontiguousarray(
        Wout[:, ch0:ch0 + IH].reshape(NHC, 128, MYKT, 128)
        .transpose(0, 3, 2, 1).reshape(NHC, 128, IH))
    d["convw_s"] = np.ascontiguousarray(
        np.asarray(inp["L0_attn_conv_w"])[ch0:ch0 + IH].reshape(16, 128, 4)
        .transpose(1, 0, 2).reshape(128, 64))
    d["convb_s"] = np.ascontiguousarray(
        np.asarray(inp["L0_attn_conv_b"])[ch0:ch0 + IH].reshape(16, 128).T)
    d["dtb_s"] = np.ascontiguousarray(
        np.asarray(inp["L0_attn_dt_bias"])[ch0:ch0 + IH].reshape(16, 128).T)
    d["A_s"] = np.ascontiguousarray(
        np.asarray(inp["L0_attn_A"])[ch0:ch0 + IH].reshape(16, 128, 16)
        .transpose(1, 0, 2).reshape(128, 256))
    d["D_s"] = np.ascontiguousarray(
        np.asarray(inp["L0_attn_D"])[ch0:ch0 + IH].reshape(16, 128).T)
    return d


def kernel(**inputs):
    shared = _prep_shared(inputs)
    x = np.asarray(inputs["L0_attn_x"])
    nc = build_nc()
    in_maps = []
    core_prep = [_prep_core(inputs, 0), _prep_core(inputs, 1)]
    for core in range(8):
        b = core // 2
        half = core % 2
        m = dict(shared)
        m.update(core_prep[half])
        m["xT"] = np.ascontiguousarray(x[b].T[:, half * LT:(half + 1) * LT])
        in_maps.append(m)
    res = run_bass_kernel_spmd(nc, in_maps, core_ids=list(range(8)))
    out = np.empty((B, LG, H), np.float32)
    for b in range(B):
        for half in range(2):
            out[b, half * LT:(half + 1) * LT] = \
                np.asarray(res.results[2 * b + half]["o_out"]).T
    return out


# version 08
# speedup vs baseline: 1.7507x; 1.0167x over previous
import numpy as np
import concourse.bass as bass
import concourse.mybir as mybir
from concourse.bass_utils import run_bass_kernel_spmd
from contextlib import ExitStack

F32 = mybir.dt.float32
BF16 = mybir.dt.bfloat16
AF = mybir.ActivationFunctionType
ALU = mybir.AluOpType

B = 4
LG = 4096
LT = 2048
H = 2048
I = 4096
IH = 2048
MYKT = 16
NOT = 32
NN = 16
R = 128
ICF = 5632
NIC = 44
NHC = 16
LC = 512
NCHG = LG // LC
TFF = 512
NTF = LT // TFF
BLK = 1024
KB = 4
NKB = MYKT // KB
EPS = 1e-5
PAIRS = [[0, 1], [2, 3], [4, 5], [6, 7]]

C_EPS, C_ONE, C_NEG1, C_NEGH, C_15, C_HALF, C_I2048, C_I128, C_I16, C_ZERO = range(10)
NCST = 10

STREAM_OF = {"pe": "tensor", "act": "scalar", "dve": "vector", "pool": "gpsimd",
             "sq": "sync", "aq": "scalar", "gq": "gpsimd"}
COMPUTE_ENGS = {"pe", "act", "dve", "pool"}
ENG_OF_STREAM = {"tensor": "pe", "scalar": "act", "vector": "dve",
                 "gpsimd": "pool", "sync": "sq"}


class Op:
    __slots__ = ("eng", "emit", "afters", "inc", "flag", "val", "seq", "key", "cc")

    def __init__(self, eng, emit, afters, inc, seq, key, cc=False):
        self.eng = eng
        self.emit = emit
        self.afters = afters
        self.inc = inc
        self.flag = False
        self.val = None
        self.seq = seq
        self.key = key
        self.cc = cc


def _flat(after):
    out = []
    for a in after:
        if a is None:
            continue
        if isinstance(a, (list, tuple)):
            out.extend(x for x in a if x is not None)
        else:
            out.append(a)
    return out


class Prog:

    def __init__(self):
        self.ops = []
        self.last_real = {}
        self.chan_last = {}

    def do(self, eng, emit, after=(), dma=False, chan=None, cc=False):
        key = chan if (dma or cc) else eng
        if dma or cc:
            assert chan is not None
        op = Op(eng, emit, _flat(after), 1 if cc else (16 if dma else 1),
                len(self.ops), key, cc)
        self.ops.append(op)
        if emit is not None:
            self.last_real[STREAM_OF[eng]] = op
            if dma or cc:
                self.chan_last[chan] = op
        return op

    def barrier(self):
        lasts = list(self.last_real.items())
        chans = list(self.chan_last.values())
        for st in ("tensor", "scalar", "vector", "gpsimd", "sync"):
            afters = [op for s2, op in lasts if s2 != st] + chans
            if afters:
                self.do(ENG_OF_STREAM[st], None, after=afters)

    def finalize(self):
        keys = []
        seen = set()
        for op in self.ops:
            if op.key not in seen:
                seen.add(op.key)
                keys.append(op.key)
        for op in self.ops:
            for a in op.afters:
                a.flag = True
        counters = {k: 0 for k in keys}
        for op in self.ops:
            if op.flag:
                assert op.emit is not None, "cannot wait on a wait-only op"
            if op.flag or (op.inc == 16 and op.emit is not None) or op.cc:
                op.flag = True
                counters[op.key] += op.inc
                op.val = counters[op.key]
        per_stream = {s: [] for s in ("tensor", "scalar", "vector", "gpsimd", "sync")}
        for op in self.ops:
            per_stream[STREAM_OF[op.eng]].append(op)
        return per_stream, keys

    @staticmethod
    def emit_stream(eng_obj, stream, ops, sems, hw):
        for op in ops:
            for a in op.afters:
                if (STREAM_OF[a.eng] == stream and a.eng in COMPUTE_ENGS
                        and not a.cc and a.seq < op.seq):
                    continue
                if hw.get((stream, a.key), 0) >= a.val:
                    continue
                eng_obj.wait_ge(sems[a.key], a.val)
                hw[(stream, a.key)] = a.val
            if op.emit is not None:
                instr = op.emit(eng_obj)
                if op.flag:
                    if op.cc:
                        instr.then_inc(sems[op.key])
                    else:
                        instr.then_inc(sems[op.key], op.inc)


class Arena:

    def __init__(self, tensor, ncols_bf16):
        self.t = tensor
        self.n = ncols_bf16
        self.cur = 0

    def alloc(self, cols, dtype):
        units = cols * (2 if dtype == F32 else 1)
        units = (units + 31) & ~31
        assert self.cur + units <= self.n, f"arena overflow {self.cur}+{units}>{self.n}"
        ap = self.t[:, self.cur:self.cur + units]
        self.cur += units
        if dtype == F32:
            ap = ap.bitcast(F32)
        return ap[:, 0:cols]

    def mark(self):
        return self.cur

    def reset(self, mark):
        self.cur = mark


class Banks:
    def __init__(self, tensors):
        self.t = tensors
        self.w = [None] * len(tensors)
        self.r = [None] * len(tensors)


def build_nc(debug=False, paranoid=0):
    nc = bass.Bass("TRN2", target_bir_lowering=False)

    xT = nc.dram_tensor("xT", [NTF, NHC, 128, TFF], F32, kind="ExternalInput")
    w_in_t = nc.dram_tensor("w_in_t", [NOT, 128, H], F32, kind="ExternalInput")
    wx_t = nc.dram_tensor("wx_t", [MYKT, 128, 160], F32, kind="ExternalInput")
    wdt_t = nc.dram_tensor("wdt_t", [R, IH], F32, kind="ExternalInput")
    wout_t = nc.dram_tensor("wout_t", [NHC, 128, IH], F32, kind="ExternalInput")
    wg_t = nc.dram_tensor("wg_t", [NIC, 128, H], F32, kind="ExternalInput")
    wu_t = nc.dram_tensor("wu_t", [NIC, 128, H], F32, kind="ExternalInput")
    wd_t = nc.dram_tensor("wd_t", [NHC, 128, ICF], F32, kind="ExternalInput")
    convw_s = nc.dram_tensor("convw_s", [128, 64], F32, kind="ExternalInput")
    convb_s = nc.dram_tensor("convb_s", [128, 16], F32, kind="ExternalInput")
    dtb_s = nc.dram_tensor("dtb_s", [128, 16], F32, kind="ExternalInput")
    A_s = nc.dram_tensor("A_s", [128, 256], F32, kind="ExternalInput")
    D_s = nc.dram_tensor("D_s", [128, 16], F32, kind="ExternalInput")
    gs_s = nc.dram_tensor("gs_s", [128, NIC], F32, kind="ExternalInput")
    us_s = nc.dram_tensor("us_s", [128, NIC], F32, kind="ExternalInput")
    ds_s = nc.dram_tensor("ds_s", [128, NHC], F32, kind="ExternalInput")
    dtln_s = nc.dram_tensor("dtln_s", [128, 1], F32, kind="ExternalInput")
    bln_s = nc.dram_tensor("bln_s", [16, 1], F32, kind="ExternalInput")
    cln_s = nc.dram_tensor("cln_s", [16, 1], F32, kind="ExternalInput")
    oneh_s = nc.dram_tensor("oneh_s", [16, NN * 128], F32, kind="ExternalInput")
    cst_in = nc.dram_tensor("cst_in", [128, NCST], F32, kind="ExternalInput")

    o_out = nc.dram_tensor("o_out", [NTF, NHC, 128, TFF], F32, kind="ExternalOutput")

    d_xn1 = nc.dram_tensor("d_xn1", [NTF, NHC, 128, TFF], BF16)
    d_x1 = nc.dram_tensor("d_x1", [NTF, NHC, 128, TFF], F32)
    d_x2 = nc.dram_tensor("d_x2", [NTF, NHC, 128, TFF], F32)
    d_xn3 = nc.dram_tensor("d_xn3", [NTF, NHC, 128, TFF], BF16)
    d_h = nc.dram_tensor("d_h", [NCHG, MYKT, 128, LC], F32)
    d_u = nc.dram_tensor("d_u", [NCHG, MYKT, 128, LC], F32)
    d_gate = nc.dram_tensor("d_gate", [NCHG, MYKT, 128, LC], F32)
    d_bc = nc.dram_tensor("d_bc", [NCHG, 32, LC], F32)
    d_wg = nc.dram_tensor("d_wg", [NIC, 128, H], BF16)
    d_wu = nc.dram_tensor("d_wu", [NIC, 128, H], BF16)
    d_wd = nc.dram_tensor("d_wd", [NHC, 128, ICF], BF16)
    d_wout = nc.dram_tensor("d_wout", [NHC, 128, IH], BF16)
    cc1_in = [nc.dram_tensor(f"cc1i{s}", [2, NHC, 128, TFF], BF16) for s in range(2)]
    cc1_out = [nc.dram_tensor(f"cc1o{s}", [2, 2, NHC, 128, TFF], BF16) for s in range(2)]
    cc2_in = [nc.dram_tensor(f"cc2i{s}", [4, 160, LC], F32) for s in range(2)]
    cc2_out = [nc.dram_tensor(f"cc2o{s}", [4, 160, LC], F32) for s in range(2)]
    cc3_in = [nc.dram_tensor(f"cc3i{j}", [2 * H, LC], F32) for j in range(4)]
    cc3_out = [nc.dram_tensor(f"cc3o{j}", [H, LC], F32) for j in range(4)]

    with ExitStack() as ctx:
        sb = lambda name, shape, dt_: ctx.enter_context(nc.sbuf_tensor(name, shape, dt_))
        cst = sb("cst", [128, NCST], F32)
        ones = sb("ones", [128, 1], F32)
        onesr = sb("onesr", [1, 128], F32)
        convw = sb("convw", [128, 64], F32)
        convb = sb("convb", [128, 16], F32)
        dtb = sb("dtb", [128, 16], F32)
        A_sb = sb("A_sb", [128, 256], F32)
        D_sb = sb("D_sb", [128, 16], F32)
        gs = sb("gs", [128, NIC], F32)
        us = sb("us", [128, NIC], F32)
        ds = sb("ds", [128, NHC], F32)
        hds = sb("hds", [128, NHC], F32)
        dtln = sb("dtln", [128, 1], F32)
        bln = sb("bln", [16, 1], F32)
        cln = sb("cln", [16, 1], F32)
        oneh = sb("oneh", [16, NN * 128], F32)
        wx_bf = sb("wx_bf", [128, MYKT * 160], BF16)
        wdt_bf = sb("wdt_bf", [128, IH], BF16)
        ts_n = sb("ts_n", [128, LG], BF16)
        state = sb("state", [128, MYKT * NN], F32)
        s_ring = sb("s_ring", [128, 2 * KB * (LC + 1)], F32)
        stat_row = sb("stat_row", [4, 512], F32)
        AR_N = 82000
        arena = Arena(sb("arena", [128, AR_N], BF16), AR_N)

        ps_t = [ctx.enter_context(nc.psum_tensor(f"ps{i}", [128, 512], F32))
                for i in range(8)]
        banks = Banks(ps_t)

        p = Prog()

        def ld(dst_ap, src_ap, after=(), q="sq", chan=None):
            return p.do(q, lambda e, d=dst_ap, s=src_ap: e.dma_start(out=d, in_=s),
                        after=after, dma=True, chan=chan)

        st_ = ld

        def cc(kind, op_, in_ap, out_ap, after, chan):
            return p.do("gq", lambda e, k=kind, o=op_, i=in_ap, u=out_ap:
                        e.collective_compute(k, o, replica_groups=PAIRS,
                                             ins=[i], outs=[u]),
                        after=after, cc=True, chan=chan)

        def mm(bank_i, lhsT, rhs, start, stop, after=(), base=0, m=128):
            out = banks.t[bank_i][base:base + m, :rhs.shape[-1]]
            extra = [banks.r[bank_i]] if start else []
            op = p.do("pe", lambda e, o=out, lh=lhsT, r=rhs, s1=start, s2=stop:
                      e.matmul(o, lh, r, start=s1, stop=s2),
                      after=list(after) + extra)
            banks.w[bank_i] = op
            return op

        def bank_read(bi_, op):
            banks.r[bi_] = op
            return op

        def act(out, in_, func, bias=0.0, scale=1.0, after=()):
            return p.do("act", lambda e, o=out, i=in_, f=func, b=bias, s=scale:
                        e.activation(o, i, f, bias=b, scale=s), after=after)

        def tt(out, a, b_, aop, after=(), eng="dve"):
            return p.do(eng, lambda e, o=out, x=a, y=b_, op_=aop:
                        e.tensor_tensor(o, x, y, op_), after=after)

        def ts(out, a, s1, s2, op0, op1=None, after=(), eng="dve"):
            if op1 is None:
                return p.do(eng, lambda e, o=out, x=a, u=s1, v=s2, p0=op0:
                            e.tensor_scalar(o, x, u, v, p0), after=after)
            return p.do(eng, lambda e, o=out, x=a, u=s1, v=s2, p0=op0, p1=op1:
                        e.tensor_scalar(o, x, u, v, p0, p1), after=after)

        def stt(out, in0, scal, in1, op0, op1, after=(), eng="dve"):
            return p.do(eng, lambda e, o=out, x=in0, s=scal, y=in1, p0=op0, p1=op1:
                        e.scalar_tensor_tensor(o, x, s, y, p0, p1), after=after)

        def cp(out, in_, after=(), eng="dve"):
            return p.do(eng, lambda e, o=out, i=in_: e.tensor_copy(o, i), after=after)

        def memset(ap, v, after=(), eng="dve"):
            return p.do(eng, lambda e, a=ap, vv=v: e.memset(a, vv), after=after)

        def scan(out, d0, d1, init, after=()):
            return p.do("dve", lambda e, o=out, a=d0, b_=d1, i=init:
                        e.tensor_tensor_scan(o, a, b_, i, ALU.mult, ALU.add),
                        after=after)

        cEPS1 = cst[0:1, C_EPS:C_EPS + 1]
        cNEGH1 = cst[0:1, C_NEGH:C_NEGH + 1]
        c151 = cst[0:1, C_15:C_15 + 1]
        cI2048 = cst[0:1, C_I2048:C_I2048 + 1]
        cI128 = cst[0:1, C_I128:C_I128 + 1]
        cI16 = cst[0:1, C_I16:C_I16 + 1]

        stat_busy = [None]

        def rstd_chain(src_row, prescale, rows, cols, after=()):
            row_v, row_r0, row_t = rows
            o1 = ts(row_v[:, 0:cols], src_row, prescale, cEPS1,
                    ALU.mult, ALU.add, after=after)
            o2 = act(row_r0[:, 0:cols], row_v[:, 0:cols], AF.Ln, after=[o1])
            o3 = act(row_r0[:, 0:cols], row_r0[:, 0:cols], AF.Exp, scale=cNEGH1,
                     after=[o2])
            o4 = tt(row_t[:, 0:cols], row_r0[:, 0:cols], row_r0[:, 0:cols],
                    ALU.mult, after=[o3])
            o5 = tt(row_t[:, 0:cols], row_t[:, 0:cols], row_v[:, 0:cols],
                    ALU.mult, after=[o4, o1])
            o6 = ts(row_t[:, 0:cols], row_t[:, 0:cols], cNEGH1, c151,
                    ALU.mult, ALU.add, after=[o5])
            o7 = tt(stat_row[0:1, 0:cols], row_r0[:, 0:cols], row_t[:, 0:cols],
                    ALU.mult, after=[o6, o3, stat_busy[0]])
            return o7

        lds = []
        for dst, src in ((cst, cst_in), (convw, convw_s), (convb, convb_s),
                         (dtb, dtb_s), (A_sb, A_s), (D_sb, D_s), (gs, gs_s),
                         (us, us_s), (ds, ds_s), (dtln, dtln_s)):
            lds.append(ld(dst[:], src.ap(), chan="p0c"))
        lds.append(ld(bln[0:16, :], bln_s.ap(), chan="p0c"))
        lds.append(ld(cln[0:16, :], cln_s.ap(), chan="p0c"))
        lds.append(ld(oneh[0:16, :], oneh_s.ap(), chan="p0c"))
        wx_src = bass.AP(wx_t, 0, [[160, 128], [128 * 160, MYKT], [1, 160]])
        lds.append(ld(wx_bf[:], wx_src, q="gq", chan="p0g"))
        lds.append(ld(wdt_bf[:], wdt_t.ap(), q="gq", chan="p0g"))

        memset(ones[:], 1.0)
        memset(onesr[0:1, :], 1.0)
        ts(hds[:], ds[:], cst[:, C_HALF:C_HALF + 1], None, ALU.mult, after=lds)
        memset(state[:], 0.0)

        cast_m0 = arena.mark()
        wc_f = [arena.alloc(4096, F32) for _ in range(2)]
        wc_b = [arena.alloc(4096, BF16) for _ in range(2)]
        cast_jobs = []
        for ic0 in range(0, NIC, 2):
            n = min(2, NIC - ic0)
            for (src, dst) in ((wg_t, d_wg), (wu_t, d_wu)):
                ap_pat = [[H, 128], [128 * H, n], [1, H]]
                cast_jobs.append((bass.AP(src, ic0 * 128 * H, ap_pat),
                                  bass.AP(dst, ic0 * 128 * H, ap_pat), n * H))
        for hc in range(NHC):
            for piece in range(2):
                w = ICF // 2
                ap_pat = [[ICF, 128], [1, w]]
                cast_jobs.append((bass.AP(wd_t, hc * 128 * ICF + piece * w, ap_pat),
                                  bass.AP(d_wd, hc * 128 * ICF + piece * w, ap_pat),
                                  w))
        for ot in range(0, NHC, 2):
            ap_pat = [[IH, 128], [128 * IH, 2], [1, IH]]
            cast_jobs.append((bass.AP(wout_t, ot * 128 * IH, ap_pat),
                              bass.AP(d_wout, ot * 128 * IH, ap_pat), 2 * IH))
        cast_c = [None, None]
        cast_s = [None, None]
        for j, (srcap, dstap, cols) in enumerate(cast_jobs):
            bi = j % 2
            l_ = ld(wc_f[bi][:, 0:cols], srcap, after=[cast_c[bi]],
                    chan=f"cstl{bi}")
            c_ = act(wc_b[bi][:, 0:cols], wc_f[bi][:, 0:cols], AF.Copy,
                     after=[l_, cast_s[bi]])
            s_ = st_(dstap, wc_b[bi][:, 0:cols], after=[c_], chan=f"csts{bi}")
            cast_c[bi] = c_
            cast_s[bi] = s_

        def ln_phase(src_dram, dst_fn):
            m0 = arena.mark()
            x_sb = [arena.alloc(8192, F32) for _ in range(2)]
            sq_sb = [arena.alloc(512, F32) for _ in range(2)]
            rows = [arena.alloc(512, F32)[0:1, :] for _ in range(3)]
            row_mu = arena.alloc(512, F32)[0:1, :]
            row_oc = arena.alloc(512, F32)[0:1, :]
            t1 = arena.alloc(512, F32)
            xn_sb = [arena.alloc(8192, BF16) for _ in range(2)]
            lu_x = [None, None]
            lu_xn = [None, None]
            oc_busy = [None]
            sqmm = [None, None]
            for t in range(NTF):
                bi = t % 2
                bk_x = 4 + bi
                bk_s = 6 + bi
                bk_r = 0 + bi
                bk_c = 2 + bi
                srcap = bass.AP(src_dram, t * 16 * 128 * TFF,
                                [[TFF, 128], [128 * TFF, 16], [1, TFF]])
                ldx = [ld(x_sb[bi][:], srcap, after=[lu_x[bi]], chan=f"lnx{bi}")]
                mmx = None
                mms = None
                for hc in range(NHC):
                    xs = x_sb[bi][:, hc * 512:(hc + 1) * 512]
                    mmx = mm(bk_x, ones[:, 0:1], xs, start=(hc == 0),
                             stop=(hc == NHC - 1), after=ldx, base=0, m=1)
                    sq_ev = act(sq_sb[hc % 2], xs, AF.Square,
                                after=ldx + [sqmm[hc % 2]])
                    mms = mm(bk_s, ones[:, 0:1], sq_sb[hc % 2],
                             start=(hc == 0), stop=(hc == NHC - 1),
                             after=[sq_ev], base=0, m=1)
                    sqmm[hc % 2] = mms
                mu = act(row_mu, banks.t[bk_x][0:1, :], AF.Copy, scale=cI2048,
                         after=[mmx])
                bank_read(bk_x, mu)
                ex2 = act(rows[0][0:1, :], banks.t[bk_s][0:1, :], AF.Copy,
                          scale=cI2048, after=[mms])
                bank_read(bk_s, ex2)
                mu2 = act(rows[2][0:1, :], row_mu, AF.Square, after=[mu])
                var = tt(rows[0][0:1, :], rows[0][0:1, :], rows[2][0:1, :],
                         ALU.subtract, after=[ex2, mu2])
                r7 = rstd_chain(rows[0][0:1, :], cst[0:1, C_ONE:C_ONE + 1],
                                rows, 512, after=[var])
                oc = tt(row_oc, row_mu, stat_row[0:1, :], ALU.mult,
                        after=[r7, mu, oc_busy[0]])
                br = mm(bk_r, onesr[0:1, 0:128], stat_row[0:1, :],
                        start=True, stop=True, after=[r7])
                bc_ = mm(bk_c, onesr[0:1, 0:128], row_oc,
                         start=True, stop=True, after=[oc])
                stat_busy[0] = [br, oc]
                oc_busy[0] = bc_
                m2s = []
                for hc in range(NHC):
                    xs = x_sb[bi][:, hc * 512:(hc + 1) * 512]
                    m1 = tt(t1, xs, banks.t[bk_r][:], ALU.mult,
                            after=[br, lu_xn[bi], (m2s[-1] if m2s else None)])
                    m2 = tt(xn_sb[bi][:, hc * 512:(hc + 1) * 512], t1,
                            banks.t[bk_c][:], ALU.subtract, after=[m1, bc_])
                    m2s.append(m2)
                bank_read(bk_r, m2s[-1])
                bank_read(bk_c, m2s[-1])
                lu_x[bi] = m2s[-1]
                so = dst_fn(t, xn_sb[bi], [m2s[-1]], bi)
                lu_xn[bi] = so
                if paranoid & 1:
                    p.barrier()
            p.barrier()
            arena.reset(m0)

        def mk_dram_dst(dst_dram):
            def f(t, xn_tile, after, bi):
                dstap = bass.AP(dst_dram, t * 16 * 128 * TFF,
                                [[TFF, 128], [128 * TFF, 16], [1, TFF]])
                return st_(dstap, xn_tile[:], after=after, chan=f"lno{bi}")
            return f

        cc1_store_last = [[], []]

        def cc1_dst(t, xn_tile, after, bi):
            s = t // 2
            dstap = bass.AP(cc1_in[s], (t % 2) * 16 * 128 * TFF,
                            [[TFF, 128], [128 * TFF, 16], [1, TFF]])
            so = st_(dstap, xn_tile[:], after=after, chan=f"lno{bi}")
            cc1_store_last[s].append(so)
            return so

        def ffn_phase(xn_dram, res_dram, out_dram):
            m0 = arena.mark()
            xn_T = [arena.alloc(8192, BF16) for _ in range(2)]
            hdn_T = arena.alloc(NIC * 512, BF16)
            wgb = [arena.alloc(2048, BF16) for _ in range(2)]
            wub = [arena.alloc(2048, BF16) for _ in range(2)]
            wdb = [arena.alloc(ICF, BF16) for _ in range(2)]
            gsl = [arena.alloc(512, F32) for _ in range(2)]
            x0b = [arena.alloc(512, F32) for _ in range(2)]
            x1b = [arena.alloc(512, F32) for _ in range(2)]
            lu_xn = [None, None]
            lu_wg = [None, None]; lu_wu = [None, None]; lu_wd = [None, None]
            lu_gs = [None, None]
            lu_x0 = [None, None]; lu_x1 = [None, None]
            hdn_reader = None
            for t in range(NTF):
                bi = t % 2
                srcap = bass.AP(xn_dram, t * 16 * 128 * TFF,
                                [[TFF, 128], [128 * TFF, 16], [1, TFF]])
                ldx = [ld(xn_T[bi][:], srcap, after=[lu_xn[bi]], chan=f"fxn{bi}")]
                hdn_ops = []
                for ic in range(NIC):
                    wi = ic % 2
                    bg = wi * 2
                    bu = wi * 2 + 1
                    lg = ld(wgb[wi][:], bass.AP(d_wg, ic * 128 * H, [[H, 128], [1, H]]),
                            after=[lu_wg[wi]], chan=f"fwg{wi}")
                    lu_ = ld(wub[wi][:], bass.AP(d_wu, ic * 128 * H, [[H, 128], [1, H]]),
                             after=[lu_wu[wi]], chan=f"fwu{wi}")
                    gm = um = None
                    for hc in range(NHC):
                        gm = mm(bg, wgb[wi][:, hc * 128:(hc + 1) * 128],
                                xn_T[bi][:, hc * 512:(hc + 1) * 512],
                                start=(hc == 0), stop=(hc == NHC - 1),
                                after=ldx + [lg])
                        um = mm(bu, wub[wi][:, hc * 128:(hc + 1) * 128],
                                xn_T[bi][:, hc * 512:(hc + 1) * 512],
                                start=(hc == 0), stop=(hc == NHC - 1),
                                after=ldx + [lu_])
                    lu_wg[wi] = gm
                    lu_wu[wi] = um
                    eg = act(gsl[wi], banks.t[bg][:], AF.Silu,
                             scale=gs[:, ic:ic + 1], after=[gm, lu_gs[wi]])
                    bank_read(bg, eg)
                    hm = stt(hdn_T[:, ic * 512:(ic + 1) * 512],
                             banks.t[bu][:], us[:, ic:ic + 1], gsl[wi],
                             ALU.mult, ALU.mult,
                             after=[um, eg, hdn_reader])
                    bank_read(bu, hm)
                    lu_gs[wi] = hm
                    hdn_ops.append(hm)
                lu_xn[bi] = hdn_ops[-1]
                ev_last = None
                for hc in range(NHC):
                    wi = hc % 2
                    bd = 4 + wi
                    lw = ld(wdb[wi][:], bass.AP(d_wd, hc * 128 * ICF,
                                                [[ICF, 128], [1, ICF]]),
                            after=[lu_wd[wi]], chan=f"fwd{wi}")
                    lx = ld(x0b[wi][:], bass.AP(res_dram, (t * 16 + hc) * 128 * TFF,
                                                [[TFF, 128], [1, TFF]]),
                            after=[lu_x0[wi]], chan=f"fx0{wi}")
                    dm = None
                    for ic in range(NIC):
                        dm = mm(bd, wdb[wi][:, ic * 128:(ic + 1) * 128],
                                hdn_T[:, ic * 512:(ic + 1) * 512],
                                start=(ic == 0), stop=(ic == NIC - 1),
                                after=[lw, hdn_ops[ic]])
                    lu_wd[wi] = dm
                    ev = stt(x1b[wi][:], banks.t[bd][:], hds[:, hc:hc + 1],
                             x0b[wi][:], ALU.mult, ALU.add,
                             after=[dm, lx, lu_x1[wi]])
                    bank_read(bd, ev)
                    lu_x0[wi] = ev
                    so = st_(bass.AP(out_dram, (t * 16 + hc) * 128 * TFF,
                                     [[TFF, 128], [1, TFF]]),
                             x1b[wi][:], after=[ev], chan=f"fx1{wi}")
                    lu_x1[wi] = so
                    ev_last = ev
                hdn_reader = ev_last
                if paranoid & 2:
                    p.barrier()
            p.barrier()
            arena.reset(m0)

        def pass1a():
            m0 = arena.mark()
            xnb = [arena.alloc(16 * BLK, BF16) for _ in range(2)]
            winb = [arena.alloc(2048, BF16) for _ in range(2)]
            hst = [arena.alloc(1024, F32) for _ in range(2)]
            lu_xn = [None, None]
            lu_w = [None, None]
            lu_h = [None, None]
            blocks = [(0, 0, 0), (2, 2048, 0), (1, 0, 1), (3, 2048, 1)]
            for bki, (blk, rowb, sup) in enumerate(blocks):
                bbi = bki % 2
                srcap = bass.AP(cc1_out[sup], rowb * BLK,
                                [[TFF, 128], [128 * TFF, 32], [1, TFF]])
                ldx = [ld(xnb[bbi][:], srcap,
                          after=[lu_xn[bbi], cc_ops["cc1"][sup]],
                          chan=f"paxn{bbi}")]
                for ot in range(NOT):
                    wi = ot % 2
                    lw = ld(winb[wi][:],
                            bass.AP(w_in_t, ot * 128 * H, [[H, 128], [1, H]]),
                            after=[lu_w[wi]], q="gq", chan=f"paw{wi}")
                    is_h = ot < MYKT
                    hi = wi
                    evs = []
                    for c4 in range(2):
                        bpj = 6 + (c4 % 2)
                        pm = None
                        for hc in range(NHC):
                            pm = mm(bpj, winb[wi][:, hc * 128:(hc + 1) * 128],
                                    xnb[bbi][:, (c4 * 16 + hc) * 512:
                                             (c4 * 16 + hc) * 512 + 512],
                                    start=(hc == 0), stop=(hc == NHC - 1),
                                    after=ldx + [lw])
                        if is_h:
                            ev = act(hst[hi][:, c4 * 512:(c4 + 1) * 512],
                                     banks.t[bpj][:], AF.Copy,
                                     after=[pm, lu_h[hi]])
                        else:
                            ev = act(hst[hi][:, c4 * 512:(c4 + 1) * 512],
                                     banks.t[bpj][:], AF.Silu,
                                     after=[pm, lu_h[hi]])
                        bank_read(bpj, ev)
                        evs.append(ev)
                    lu_w[wi] = evs[-1]
                    tgt = d_h if is_h else d_gate
                    kt = ot if is_h else ot - MYKT
                    sos = []
                    for c4 in range(2):
                        sos.append(st_(
                            bass.AP(tgt, ((2 * blk + c4) * MYKT + kt) * 128 * LC,
                                    [[LC, 128], [1, LC]]),
                            hst[hi][:, c4 * 512:(c4 + 1) * 512],
                            after=evs, chan=f"pah{hi}"))
                    lu_h[hi] = sos[-1]
                if paranoid & 4:
                    p.barrier()
            p.barrier()
            arena.reset(m0)

        def pass1b():
            m0 = arena.mark()
            hbuf = [arena.alloc(LT + 3, F32) for _ in range(2)]
            acc_a = [arena.alloc(LT, F32) for _ in range(2)]
            acc_b = [arena.alloc(LT, F32) for _ in range(2)]
            ubf = [arena.alloc(LT, BF16) for _ in range(2)]
            tsev = arena.alloc(2048, F32)
            bcev = arena.alloc(2048, F32)
            lu_h = [None, None]
            lu_u = [None, None]
            acc_busy = [None, None]
            ev_stores = []
            for half in range(2):
                col0 = half * LT
                for ot in range(MYKT):
                    wi = ot % 2
                    lh = []
                    if half == 0:
                        lh.append(memset(hbuf[wi][:, 0:3], 0.0,
                                         after=[lu_h[wi]]))
                    else:
                        lh.append(ld(hbuf[wi][:, 0:3],
                                     bass.AP(d_h, (3 * MYKT + ot) * 128 * LC
                                             + LC - 3,
                                             [[LC, 128], [1, 3]]),
                                     after=[lu_h[wi]], chan=f"pbh{wi}"))
                    for cc4 in range(4):
                        gch = half * 4 + cc4
                        lh.append(ld(hbuf[wi][:, 3 + cc4 * LC:3 + (cc4 + 1) * LC],
                                     bass.AP(d_h, (gch * MYKT + ot) * 128 * LC,
                                             [[LC, 128], [1, LC]]),
                                     after=[lu_h[wi]], chan=f"pbh{wi}"))
                    c0 = ts(acc_a[wi], hbuf[wi][:, 0:LT],
                            convw[:, ot * 4:ot * 4 + 1], None, ALU.mult,
                            after=lh + [acc_busy[wi]])
                    c1 = stt(acc_b[wi], hbuf[wi][:, 1:1 + LT],
                             convw[:, ot * 4 + 1:ot * 4 + 2], acc_a[wi],
                             ALU.mult, ALU.add, after=[c0])
                    c2 = stt(acc_a[wi], hbuf[wi][:, 2:2 + LT],
                             convw[:, ot * 4 + 2:ot * 4 + 3], acc_b[wi],
                             ALU.mult, ALU.add, after=[c1])
                    c3 = stt(acc_b[wi], hbuf[wi][:, 3:3 + LT],
                             convw[:, ot * 4 + 3:ot * 4 + 4], acc_a[wi],
                             ALU.mult, ALU.add, after=[c2])
                    lu_h[wi] = c3
                    sl = act(acc_a[wi], acc_b[wi], AF.Silu,
                             bias=convb[:, ot:ot + 1], after=[c3])
                    sus = []
                    for cc4 in range(4):
                        gch = half * 4 + cc4
                        sus.append(st_(
                            bass.AP(d_u, (gch * MYKT + ot) * 128 * LC,
                                    [[LC, 128], [1, LC]]),
                            acc_a[wi][:, cc4 * LC:(cc4 + 1) * LC],
                            after=[sl], chan=f"pbu{wi}"))
                    su = sus[-1]
                    cb = act(ubf[wi][:], acc_a[wi], AF.Copy, after=[sl, lu_u[wi]])
                    acc_busy[wi] = [su, cb]
                    mm_last = []
                    for c4 in range(4):
                        bts = c4
                        bbc = 4 + c4
                        u_sl = ubf[wi][:, c4 * 512:(c4 + 1) * 512]
                        m1 = mm(bts, wx_bf[:, ot * 160:ot * 160 + 128], u_sl,
                                start=(ot == 0), stop=(ot == MYKT - 1),
                                after=[cb])
                        m2 = mm(bbc, wx_bf[:, ot * 160 + 128:ot * 160 + 160],
                                u_sl, start=(ot == 0), stop=(ot == MYKT - 1),
                                after=[cb], m=32)
                        mm_last.append(m1)
                        mm_last.append(m2)
                    lu_u[wi] = mm_last
                evs = []
                for c4 in range(4):
                    e1 = act(tsev[:, c4 * 512:(c4 + 1) * 512], banks.t[c4][:],
                             AF.Copy, after=[banks.w[c4]] + ev_stores)
                    bank_read(c4, e1)
                    e2 = act(bcev[0:32, c4 * 512:(c4 + 1) * 512],
                             banks.t[4 + c4][0:32, :], AF.Copy,
                             after=[banks.w[4 + c4]] + ev_stores)
                    bank_read(4 + c4, e2)
                    evs += [e1, e2]
                s1 = st_(bass.AP(cc2_in[half], 0,
                                 [[LC, 128], [160 * LC, 4], [1, LC]]),
                         tsev[:], after=evs, chan="pbt")
                s2 = st_(bass.AP(cc2_in[half], 128 * LC,
                                 [[LC, 32], [160 * LC, 4], [1, LC]]),
                         bcev[0:32, :], after=evs, chan="pbt")
                ev_stores = [s1, s2]
                cc_ops["cc2"][half] = cc("AllReduce", ALU.add,
                                         cc2_in[half].ap(), cc2_out[half].ap(),
                                         after=[s1, s2], chan=f"cc2_{half}")
                if paranoid & 4:
                    p.barrier()
            p.barrier()
            arena.reset(m0)

        def rms_phase():
            m0 = arena.mark()
            tsb = [arena.alloc(512, F32) for _ in range(2)]
            bb_ = [arena.alloc(512, F32) for _ in range(2)]
            cb_ = [arena.alloc(512, F32) for _ in range(2)]
            sq = arena.alloc(512, F32)
            rows = [arena.alloc(512, F32)[0:1, :] for _ in range(3)]
            tf = arena.alloc(512, F32)
            lu = [None, None]
            lu_bc = [None, None]
            for c in range(NCHG):
                half = c // 4
                coff = (c % 4) * 512
                bi = c % 2
                c4g = c % 4
                lt_ = ld(tsb[bi][:], bass.AP(cc2_out[half], c4g * 160 * LC,
                                             [[LC, 128], [1, LC]]),
                         after=[lu[bi], cc_ops["cc2"][half]], chan=f"rmt{bi}")
                lb_ = ld(bb_[bi][0:16, :],
                         bass.AP(cc2_out[half], c4g * 160 * LC + 128 * LC,
                                 [[LC, 16], [1, LC]]),
                         after=[lu_bc[bi], cc_ops["cc2"][half]], chan=f"rmb{bi}")
                lc2 = ld(cb_[bi][0:16, :],
                         bass.AP(cc2_out[half], c4g * 160 * LC + 144 * LC,
                                 [[LC, 16], [1, LC]]),
                         after=[lu_bc[bi], cc_ops["cc2"][half]], chan=f"rmb{bi}")
                sq1 = act(sq, tsb[bi][:], AF.Square, after=[lt_])
                sm = mm(6, ones[:, 0:1], sq, start=True, stop=True,
                        after=[sq1], m=1)
                r7 = rstd_chain(banks.t[6][0:1, :], cI128, rows, 512, after=[sm])
                bank_read(6, r7)
                br = mm(7, onesr[0:1, 0:128], stat_row[0:1, :], start=True,
                        stop=True, after=[r7])
                stat_busy[0] = br
                t1_ = tt(tf, tsb[bi][:], banks.t[7][:], ALU.mult,
                         after=[br, sq1])
                t2_ = ts(ts_n[:, c * 512:(c + 1) * 512], tf, dtln[:, 0:1],
                         None, ALU.mult, after=[t1_])
                bank_read(7, t2_)
                lu[bi] = t2_
                prev = t2_
                for half2, w_ln in ((0, bln), (1, cln)):
                    src16 = bb_[bi][0:16, :] if half2 == 0 else cb_[bi][0:16, :]
                    sq2 = act(sq[0:16, :], src16, AF.Square,
                              after=[lb_, lc2, prev, sm])
                    sm2 = mm(6, ones[0:16, 0:1], sq[0:16, :], start=True,
                             stop=True, after=[sq2], m=1)
                    r72 = rstd_chain(banks.t[6][0:1, :], cI16, rows, 512,
                                     after=[sm2])
                    bank_read(6, r72)
                    br2 = mm(7, onesr[0:1, 0:128], stat_row[0:1, :],
                             start=True, stop=True, after=[r72])
                    stat_busy[0] = br2
                    n1 = tt(tf[0:16, :], src16, banks.t[7][0:16, :], ALU.mult,
                            after=[br2, sq2])
                    n2 = ts(src16, tf[0:16, :],
                            w_ln[0:16, 0:1], None, ALU.mult, after=[n1])
                    bank_read(7, n2)
                    sb2 = st_(bass.AP(d_bc, (c * 32 + half2 * 16) * LC,
                                      [[LC, 16], [1, LC]]),
                              src16, after=[n2], chan="rmo")
                    prev = sb2
                lu_bc[bi] = prev
            p.barrier()
            arena.reset(m0)

        def scan_phase():
            m0 = arena.mark()
            Bb = arena.alloc(NN * 512, F32)
            Cb = arena.alloc(NN * 512, F32)
            bst = arena.alloc(512, F32)
            cstg = arena.alloc(512, F32)
            ub = [arena.alloc(KB * 512, F32)]
            gat = [arena.alloc(KB * 512, F32)]
            delta = arena.alloc(KB * 512, F32)
            du = arena.alloc(KB * 512, F32)
            yb = arena.alloc(KB * 512, F32)
            dab = [arena.alloc(LC + 1, F32) for _ in range(4)]
            dbb = [arena.alloc(LC + 1, F32) for _ in range(4)]
            ymt_d = arena.alloc(512, F32)
            ymt_p = arena.alloc(512, F32)
            tmp_sp = [arena.alloc(512, F32) for _ in range(2)]
            y_g = arena.alloc(MYKT * 512, BF16)
            woutb = [arena.alloc(IH, BF16) for _ in range(2)]
            wev = [arena.alloc(512, F32)]
            pad_ok = ([memset(dab[i][:, LC:LC + 1], 1.0) for i in range(4)]
                      + [memset(dbb[i][:, LC:LC + 1], 0.0) for i in range(4)])
            sring_v = s_ring[:].rearrange("p (q k l) -> p q k l", l=LC + 1, k=KB)
            state_v = state[:].rearrange("p (k n) -> p k n", n=NN)
            tbl_busy = [None]
            db_last = [None, None]
            lu_u = [None]; lu_g = [None]
            lu_w = [None, None]
            lu_wev = [None]
            delta_busy = None
            du_busy = None
            tmp_busy = [None, None]
            st_by_par = {0: None, 1: None}
            scan_hist = {}
            y2_hist = {}
            yg_last = {"dve": None, "pool": None}
            wout_done = None
            wout_stores = []
            for c in range(NCHG):
                lbst = ld(bst[0:16, :], bass.AP(d_bc, c * 32 * LC,
                                                [[LC, 16], [1, LC]]),
                          after=[tbl_busy[0]], chan="stb")
                lcst = ld(cstg[0:16, :], bass.AP(d_bc, (c * 32 + 16) * LC,
                                                 [[LC, 16], [1, LC]]),
                          after=[tbl_busy[0]], chan="stc")
                tev = []
                for n in range(NN):
                    bsel = n % 2
                    mb = mm(bsel, oneh[0:16, n * 128:(n + 1) * 128],
                            bst[0:16, :], start=True,
                            stop=True, after=[tbl_busy[0], lbst])
                    eb = act(Bb[:, n * 512:(n + 1) * 512], banks.t[bsel][:],
                             AF.Copy, after=[mb, tbl_busy[0]])
                    bank_read(bsel, eb)
                    mc = mm(bsel, oneh[0:16, n * 128:(n + 1) * 128],
                            cstg[0:16, :], start=True,
                            stop=True, after=[eb, lcst])
                    ec = act(Cb[:, n * 512:(n + 1) * 512], banks.t[bsel][:],
                             AF.Copy, after=[mc])
                    bank_read(bsel, ec)
                    tev.append(ec)
                tev_last = tev[-1]
                for kb in range(NKB):
                    d1_ops = []
                    y0_ops = []
                    lg_ops = []
                    delta_ops = []
                    for kk in range(KB):
                        k = kb * KB + kk
                        lh = ld(ub[0][:, kk * 512:kk * 512 + LC],
                                bass.AP(d_u, (c * MYKT + k) * 128 * LC,
                                        [[LC, 128], [1, LC]]),
                                after=[lu_u[0]], chan="su0")
                        lg = ld(gat[0][:, kk * 512:kk * 512 + LC],
                                bass.AP(d_gate, (c * MYKT + k) * 128 * LC,
                                        [[LC, 128], [1, LC]]),
                                after=[lu_g[0]], chan="sg0")
                        bdt = 6 + (kk % 2)
                        dmm = mm(bdt, wdt_bf[:, k * 128:(k + 1) * 128],
                                 ts_n[:, c * LC:(c + 1) * LC], start=True,
                                 stop=True, after=[])
                        dl = delta[:, kk * 512:kk * 512 + LC]
                        sp1 = act(tmp_sp[kk % 2], banks.t[bdt][:, 0:LC],
                                  AF.Abs, bias=dtb[:, k:k + 1],
                                  after=[dmm, tmp_busy[kk % 2]])
                        sp2 = act(tmp_sp[kk % 2], tmp_sp[kk % 2], AF.Exp,
                                  scale=cst[:, C_NEG1:C_NEG1 + 1], after=[sp1])
                        sp3 = act(tmp_sp[kk % 2], tmp_sp[kk % 2], AF.Ln,
                                  bias=cst[:, C_ONE:C_ONE + 1], after=[sp2])
                        sp4 = act(dl, banks.t[bdt][:, 0:LC], AF.Relu,
                                  bias=dtb[:, k:k + 1],
                                  after=[sp3, delta_busy])
                        bank_read(bdt, sp4)
                        sp5 = tt(dl, dl, tmp_sp[kk % 2], ALU.add,
                                 after=[sp4, delta_busy])
                        tmp_busy[kk % 2] = sp5
                        delta_ops.append(sp5)
                        d1 = tt(du[:, kk * 512:kk * 512 + LC], dl,
                                ub[0][:, kk * 512:kk * 512 + LC], ALU.mult,
                                after=[sp5, lh, du_busy])
                        y0 = ts(yb[:, kk * 512:kk * 512 + LC],
                                ub[0][:, kk * 512:kk * 512 + LC],
                                D_sb[:, k:k + 1], None, ALU.mult,
                                after=[lh, yg_last["dve"], yg_last["pool"]])
                        d1_ops.append(d1)
                        y0_ops.append(y0)
                        lg_ops.append(lg)
                    lu_u[0] = [d1_ops[-1], y0_ops[-1]]
                    yop_prev = dict(enumerate(y0_ops))
                    last_ex = None
                    last_db = None
                    st_ops = []
                    for n in range(NN):
                        par = n % 2
                        for kk in range(KB):
                            k = kb * KB + kk
                            dl = delta[:, kk * 512:kk * 512 + LC]
                            ex = act(dab[kk][:, 0:LC], dl, AF.Exp,
                                     scale=A_sb[:, k * NN + n:k * NN + n + 1],
                                     after=[scan_hist.get((kk, 1 - par)),
                                            delta_ops[kk]] + pad_ok)
                            last_ex = ex
                            db_ = tt(dbb[kk][:, 0:LC],
                                     du[:, kk * 512:kk * 512 + LC],
                                     Bb[:, n * 512:(n + 1) * 512], ALU.mult,
                                     after=[tev_last, scan_hist.get((kk, 1 - par)),
                                            d1_ops[kk], db_last[n % 2]],
                                     eng=("dve" if n % 2 == 0 else "pool"))
                            db_last[n % 2] = db_
                            last_db = db_
                            sc = scan(sring_v[:, par, kk, :], dab[kk][:],
                                      dbb[kk][:],
                                      state[:, k * NN + n:k * NN + n + 1],
                                      after=[ex, db_, y2_hist.get((kk, par)),
                                             st_by_par[par]])
                            scan_hist[(kk, par)] = sc
                            yeng = "dve" if kk < 2 else "pool"
                            ymt = ymt_d if kk < 2 else ymt_p
                            y1 = tt(ymt, sring_v[:, par, kk, 0:LC],
                                    Cb[:, n * 512:(n + 1) * 512], ALU.mult,
                                    after=[sc, tev_last, yg_last[yeng]],
                                    eng=yeng)
                            y2 = tt(yb[:, kk * 512:kk * 512 + LC],
                                    yb[:, kk * 512:kk * 512 + LC], ymt,
                                    ALU.add, after=[y1, yop_prev.get(kk)],
                                    eng=yeng)
                            yop_prev[kk] = y2
                            y2_hist[(kk, par)] = y2
                            yg_last[yeng] = y2
                        stc = cp(state_v[:, kb * KB:(kb + 1) * KB, n],
                                 sring_v[:, par, :, LC],
                                 after=[scan_hist[(kk2, par)] for kk2 in range(KB)]
                                 + [st_by_par[par]], eng="pool")
                        st_by_par[par] = stc
                        st_ops.append(stc)
                    delta_busy = [last_ex, d1_ops[-1]]
                    du_busy = last_db
                    for kk in range(KB):
                        k = kb * KB + kk
                        yeng = "dve" if kk < 2 else "pool"
                        yg = tt(y_g[:, k * 512:k * 512 + LC],
                                yb[:, kk * 512:kk * 512 + LC],
                                gat[0][:, kk * 512:kk * 512 + LC], ALU.mult,
                                after=[yop_prev[kk], st_ops[-1], st_ops[-2],
                                       lg_ops[kk],
                                       yg_last[yeng], wout_done], eng=yeng)
                        yg_last[yeng] = yg
                    lu_g[0] = [yg_last["dve"], yg_last["pool"]]
                wev_ops = []
                for ot in range(NHC):
                    wi = ot % 2
                    bw = 2 + wi
                    lw = ld(woutb[wi][:],
                            bass.AP(d_wout, ot * 128 * IH, [[IH, 128], [1, IH]]),
                            after=[lu_w[wi]], chan=f"sw{wi}")
                    wm = None
                    for kt in range(MYKT):
                        wm = mm(bw, woutb[wi][:, kt * 128:(kt + 1) * 128],
                                y_g[:, kt * 512:(kt + 1) * 512],
                                start=(kt == 0), stop=(kt == MYKT - 1),
                                after=[lw, yg_last["dve"], yg_last["pool"]])
                    lu_w[wi] = wm
                    ev = act(wev[0][:], banks.t[bw][:, 0:LC], AF.Copy,
                             after=[wm, lu_wev[0]])
                    bank_read(bw, ev)
                    j = c % 4
                    rowb = 0 if c < 4 else H
                    so = st_(bass.AP(cc3_in[j], (rowb + ot * 128) * LC,
                                     [[LC, 128], [1, LC]]),
                             wev[0][:], after=[ev], chan="sv0")
                    lu_wev[0] = so
                    wev_ops.append(so)
                    wev_ops.append(ev)
                wout_done = wev_ops[-1]
                tbl_busy[0] = wev_ops[-1]
                wout_stores.append(wev_ops)
                if c >= 4:
                    j = c - 4
                    cc_ops["cc3"][j] = cc("ReduceScatter", ALU.add,
                                          cc3_in[j].ap(), cc3_out[j].ap(),
                                          after=wout_stores[j] + wout_stores[c],
                                          chan=f"cc3_{j}")
                if paranoid & 8:
                    p.barrier()
            p.barrier()
            arena.reset(m0)

        def x2_phase():
            m0 = arena.mark()
            a_ = [arena.alloc(512, F32) for _ in range(2)]
            b_ = [arena.alloc(512, F32) for _ in range(2)]
            lu = [None, None]
            for j in range(4):
                for hc in range(NHC):
                    bi = (j * NHC + hc) % 2
                    l1 = ld(a_[bi][:], bass.AP(d_x1, (j * 16 + hc) * 128 * LC,
                                               [[LC, 128], [1, LC]]),
                            after=[lu[bi]], chan=f"xa{bi}")
                    l2 = ld(b_[bi][:], bass.AP(cc3_out[j], hc * 128 * LC,
                                               [[LC, 128], [1, LC]]),
                            after=[lu[bi], cc_ops["cc3"][j]], chan=f"xb{bi}")
                    ad = tt(a_[bi][:], a_[bi][:], b_[bi][:], ALU.add,
                            after=[l1, l2])
                    so = st_(bass.AP(d_x2, (j * 16 + hc) * 128 * LC,
                                     [[LC, 128], [1, LC]]),
                             a_[bi][:], after=[ad], chan=f"xc{bi}")
                    lu[bi] = [ad, so]
            p.barrier()
            arena.reset(m0)

        cc_ops = {"cc1": [None, None], "cc2": [None, None],
                  "cc3": [None, None, None, None]}

        ln_phase(xT, mk_dram_dst(d_xn1))
        arena.reset(cast_m0)
        ffn_phase(d_xn1, xT, d_x1)
        ln_phase(d_x1, cc1_dst)
        for s in range(2):
            cc_ops["cc1"][s] = cc("AllGather", ALU.bypass, cc1_in[s].ap(),
                                  cc1_out[s].ap(), after=cc1_store_last[s],
                                  chan=f"cc1_{s}")
        pass1a()
        pass1b()
        rms_phase()
        scan_phase()
        x2_phase()
        ln_phase(d_x2, mk_dram_dst(d_xn3))
        ffn_phase(d_xn3, d_x2, o_out)

        if debug:
            p.barrier()
            dbg_map = {"d_xn1": (d_xn1, [H, LT], BF16), "d_x1": (d_x1, [H, LT], F32),
                       "d_h": (d_h, [MYKT, 128, LG], F32),
                       "d_u": (d_u, [MYKT, 128, LG], F32),
                       "d_gate": (d_gate, [MYKT, 128, LG], F32),
                       "d_x2": (d_x2, [H, LT], F32),
                       "cc1o0": (cc1_out[0], [2 * H, BLK], BF16),
                       "cc1o1": (cc1_out[1], [2 * H, BLK], BF16),
                       "cc2o0": (cc2_out[0], [160, LT], F32),
                       "cc2o1": (cc2_out[1], [160, LT], F32),
                       "cc3o0": (cc3_out[0], [H, LC], F32)}
            for name, (src, shape, dt_) in dbg_map.items():
                dbg = nc.dram_tensor("dbg_" + name, shape, dt_,
                                     kind="ExternalOutput")
                ld(dbg.ap(), src.ap(), chan="dbg")
            dbg_tsn = nc.dram_tensor("dbg_tsn", [128, LG], BF16,
                                     kind="ExternalOutput")
            dbg_bn = nc.dram_tensor("dbg_bn", [16, LG], F32,
                                    kind="ExternalOutput")
            dbg_cn = nc.dram_tensor("dbg_cn", [16, LG], F32,
                                    kind="ExternalOutput")
            ld(dbg_tsn.ap(), ts_n[:], chan="dbg")
            ld(dbg_bn.ap(), b_n[0:16, :], chan="dbg")
            ld(dbg_cn.ap(), c_n[0:16, :], chan="dbg")

        p.do("sq", None, after=list(p.chan_last.values()))

        per_stream, keys = p.finalize()
        sems = {k: ctx.enter_context(nc.semaphore(f"sem_{k}")) for k in keys}
        hw = {}
        block = ctx.enter_context(nc.Block())

        @block.sync
        def _(sync):
            Prog.emit_stream(sync, "sync", per_stream["sync"], sems, hw)

        @block.tensor
        def _(tensor):
            Prog.emit_stream(tensor, "tensor", per_stream["tensor"], sems, hw)

        @block.scalar
        def _(scalar):
            Prog.emit_stream(scalar, "scalar", per_stream["scalar"], sems, hw)

        @block.vector
        def _(vector):
            Prog.emit_stream(vector, "vector", per_stream["vector"], sems, hw)

        @block.gpsimd
        def _(gpsimd):
            Prog.emit_stream(gpsimd, "gpsimd", per_stream["gpsimd"], sems, hw)

    return nc


def _prep_shared(inp):
    gw = inp["L0_ffn_gate_w"]
    uw = inp["L0_ffn_up_w"]
    dw = inp["L0_ffn_down_w"]
    d = {}
    d["wg_t"] = np.ascontiguousarray(
        np.asarray(gw).reshape(NIC, 128, NHC, 128).transpose(0, 3, 2, 1)
        .reshape(NIC, 128, H))
    d["wu_t"] = np.ascontiguousarray(
        np.asarray(uw).reshape(NIC, 128, NHC, 128).transpose(0, 3, 2, 1)
        .reshape(NIC, 128, H))
    d["wd_t"] = np.ascontiguousarray(
        np.asarray(dw).reshape(NHC, 128, NIC, 128).transpose(0, 3, 2, 1)
        .reshape(NHC, 128, ICF))
    d["gs_s"] = np.ascontiguousarray(
        np.asarray(inp["L0_ffn_gate_s"]).reshape(NIC, 128).T)
    d["us_s"] = np.ascontiguousarray(
        np.asarray(inp["L0_ffn_up_s"]).reshape(NIC, 128).T)
    d["ds_s"] = np.ascontiguousarray(
        np.asarray(inp["L0_ffn_down_s"]).reshape(NHC, 128).T)
    d["dtln_s"] = np.ascontiguousarray(
        np.asarray(inp["L0_attn_dt_ln_w"]).reshape(128, 1))
    d["bln_s"] = np.ascontiguousarray(
        np.asarray(inp["L0_attn_b_ln_w"]).reshape(16, 1))
    d["cln_s"] = np.ascontiguousarray(
        np.asarray(inp["L0_attn_c_ln_w"]).reshape(16, 1))
    oneh = np.zeros((16, NN * 128), np.float32)
    for n in range(NN):
        oneh[n, n * 128:(n + 1) * 128] = 1.0
    d["oneh_s"] = oneh
    cvals = np.zeros(NCST, np.float32)
    cvals[C_EPS] = EPS
    cvals[C_ONE] = 1.0
    cvals[C_NEG1] = -1.0
    cvals[C_NEGH] = -0.5
    cvals[C_15] = 1.5
    cvals[C_HALF] = 0.5
    cvals[C_I2048] = 1.0 / 2048.0
    cvals[C_I128] = 1.0 / 128.0
    cvals[C_I16] = 1.0 / 16.0
    d["cst_in"] = np.ascontiguousarray(
        np.broadcast_to(cvals.astype(np.float32), (128, NCST)))
    return d


def _prep_core(inp, half):
    ch0 = half * IH
    W_in = np.asarray(inp["L0_attn_W_in"])
    Wx = np.asarray(inp["L0_attn_Wx"])
    Wdt = np.asarray(inp["L0_attn_Wdt"])
    Wout = np.asarray(inp["L0_attn_Wout"])
    d = {}
    w_in_rows = np.concatenate([W_in[ch0:ch0 + IH], W_in[I + ch0:I + ch0 + IH]],
                               axis=0)
    d["w_in_t"] = np.ascontiguousarray(
        w_in_rows.reshape(NOT, 128, NHC, 128).transpose(0, 3, 2, 1)
        .reshape(NOT, 128, H))
    d["wx_t"] = np.ascontiguousarray(
        Wx[:, ch0:ch0 + IH].T.reshape(MYKT, 128, 160))
    d["wdt_t"] = np.ascontiguousarray(Wdt[ch0:ch0 + IH].T)
    d["wout_t"] = np.ascontiguousarray(
        Wout[:, ch0:ch0 + IH].reshape(NHC, 128, MYKT, 128)
        .transpose(0, 3, 2, 1).reshape(NHC, 128, IH))
    d["convw_s"] = np.ascontiguousarray(
        np.asarray(inp["L0_attn_conv_w"])[ch0:ch0 + IH].reshape(16, 128, 4)
        .transpose(1, 0, 2).reshape(128, 64))
    d["convb_s"] = np.ascontiguousarray(
        np.asarray(inp["L0_attn_conv_b"])[ch0:ch0 + IH].reshape(16, 128).T)
    d["dtb_s"] = np.ascontiguousarray(
        np.asarray(inp["L0_attn_dt_bias"])[ch0:ch0 + IH].reshape(16, 128).T)
    d["A_s"] = np.ascontiguousarray(
        np.asarray(inp["L0_attn_A"])[ch0:ch0 + IH].reshape(16, 128, 16)
        .transpose(1, 0, 2).reshape(128, 256))
    d["D_s"] = np.ascontiguousarray(
        np.asarray(inp["L0_attn_D"])[ch0:ch0 + IH].reshape(16, 128).T)
    return d


def kernel(**inputs):
    shared = _prep_shared(inputs)
    x = np.asarray(inputs["L0_attn_x"])
    nc = build_nc()
    in_maps = []
    core_prep = [_prep_core(inputs, 0), _prep_core(inputs, 1)]
    for core in range(8):
        b = core // 2
        half = core % 2
        m = dict(shared)
        m.update(core_prep[half])
        xr = x[b].T[:, half * LT:(half + 1) * LT]
        m["xT"] = np.ascontiguousarray(
            xr.reshape(NHC, 128, NTF, TFF).transpose(2, 0, 1, 3))
        in_maps.append(m)
    res = run_bass_kernel_spmd(nc, in_maps, core_ids=list(range(8)))
    out = np.empty((B, LG, H), np.float32)
    for b in range(B):
        for half in range(2):
            o = np.asarray(res.results[2 * b + half]["o_out"])
            o = o.transpose(1, 2, 0, 3).reshape(H, LT)
            out[b, half * LT:(half + 1) * LT] = o.T
    return out


# version 09
# speedup vs baseline: 1.7830x; 1.0184x over previous
import numpy as np
import concourse.bass as bass
import concourse.mybir as mybir
from concourse.bass_utils import run_bass_kernel_spmd
from contextlib import ExitStack

F32 = mybir.dt.float32
BF16 = mybir.dt.bfloat16
AF = mybir.ActivationFunctionType
ALU = mybir.AluOpType

B = 4
LG = 4096
LT = 2048
H = 2048
I = 4096
IH = 2048
MYKT = 16
NOT = 32
NN = 16
R = 128
ICF = 5632
NIC = 44
NHC = 16
LC = 512
NCHG = LG // LC
TFF = 512
NTF = LT // TFF
BLK = 1024
KB = 4
NKB = MYKT // KB
EPS = 1e-5
PAIRS = [[0, 1], [2, 3], [4, 5], [6, 7]]

C_EPS, C_ONE, C_NEG1, C_NEGH, C_15, C_HALF, C_I2048, C_I128, C_I16, C_ZERO = range(10)
NCST = 10

STREAM_OF = {"pe": "tensor", "act": "scalar", "dve": "vector", "pool": "gpsimd",
             "sq": "sync", "aq": "scalar", "gq": "gpsimd"}
COMPUTE_ENGS = {"pe", "act", "dve", "pool"}
ENG_OF_STREAM = {"tensor": "pe", "scalar": "act", "vector": "dve",
                 "gpsimd": "pool", "sync": "sq"}


class Op:
    __slots__ = ("eng", "emit", "afters", "inc", "flag", "val", "seq", "key", "cc")

    def __init__(self, eng, emit, afters, inc, seq, key, cc=False):
        self.eng = eng
        self.emit = emit
        self.afters = afters
        self.inc = inc
        self.flag = False
        self.val = None
        self.seq = seq
        self.key = key
        self.cc = cc


def _flat(after):
    out = []
    for a in after:
        if a is None:
            continue
        if isinstance(a, (list, tuple)):
            out.extend(x for x in a if x is not None)
        else:
            out.append(a)
    return out


class Prog:

    def __init__(self):
        self.ops = []
        self.last_real = {}
        self.chan_last = {}

    def do(self, eng, emit, after=(), dma=False, chan=None, cc=False):
        key = chan if (dma or cc) else eng
        if dma or cc:
            assert chan is not None
        op = Op(eng, emit, _flat(after), 1 if cc else (16 if dma else 1),
                len(self.ops), key, cc)
        self.ops.append(op)
        if emit is not None:
            self.last_real[STREAM_OF[eng]] = op
            if dma or cc:
                self.chan_last[chan] = op
        return op

    def barrier(self):
        lasts = list(self.last_real.items())
        chans = list(self.chan_last.values())
        for st in ("tensor", "scalar", "vector", "gpsimd", "sync"):
            afters = [op for s2, op in lasts if s2 != st] + chans
            if afters:
                self.do(ENG_OF_STREAM[st], None, after=afters)

    def finalize(self):
        keys = []
        seen = set()
        for op in self.ops:
            if op.key not in seen:
                seen.add(op.key)
                keys.append(op.key)
        for op in self.ops:
            for a in op.afters:
                a.flag = True
        counters = {k: 0 for k in keys}
        for op in self.ops:
            if op.flag:
                assert op.emit is not None, "cannot wait on a wait-only op"
            if op.flag or (op.inc == 16 and op.emit is not None) or op.cc:
                op.flag = True
                counters[op.key] += op.inc
                op.val = counters[op.key]
        per_stream = {s: [] for s in ("tensor", "scalar", "vector", "gpsimd", "sync")}
        for op in self.ops:
            per_stream[STREAM_OF[op.eng]].append(op)
        return per_stream, keys

    @staticmethod
    def emit_stream(eng_obj, stream, ops, sems, hw):
        for op in ops:
            for a in op.afters:
                if (STREAM_OF[a.eng] == stream and a.eng in COMPUTE_ENGS
                        and not a.cc and a.seq < op.seq):
                    continue
                if hw.get((stream, a.key), 0) >= a.val:
                    continue
                eng_obj.wait_ge(sems[a.key], a.val)
                hw[(stream, a.key)] = a.val
            if op.emit is not None:
                instr = op.emit(eng_obj)
                if op.flag:
                    if op.cc:
                        instr.then_inc(sems[op.key])
                    else:
                        instr.then_inc(sems[op.key], op.inc)


class Arena:

    def __init__(self, tensor, ncols_bf16):
        self.t = tensor
        self.n = ncols_bf16
        self.cur = 0

    def alloc(self, cols, dtype):
        units = cols * (2 if dtype == F32 else 1)
        units = (units + 31) & ~31
        assert self.cur + units <= self.n, f"arena overflow {self.cur}+{units}>{self.n}"
        ap = self.t[:, self.cur:self.cur + units]
        self.cur += units
        if dtype == F32:
            ap = ap.bitcast(F32)
        return ap[:, 0:cols]

    def mark(self):
        return self.cur

    def reset(self, mark):
        self.cur = mark


class Banks:
    def __init__(self, tensors):
        self.t = tensors
        self.w = [None] * len(tensors)
        self.r = [None] * len(tensors)


def build_nc(debug=False, paranoid=0):
    nc = bass.Bass("TRN2", target_bir_lowering=False)

    xT = nc.dram_tensor("xT", [NTF, NHC, 128, TFF], F32, kind="ExternalInput")
    w_in_t = nc.dram_tensor("w_in_t", [NOT, 128, H], F32, kind="ExternalInput")
    wx_t = nc.dram_tensor("wx_t", [MYKT, 128, 160], F32, kind="ExternalInput")
    wdt_t = nc.dram_tensor("wdt_t", [R, IH], F32, kind="ExternalInput")
    wout_t = nc.dram_tensor("wout_t", [NHC, 128, IH], F32, kind="ExternalInput")
    wg_t = nc.dram_tensor("wg_t", [NIC, 128, H], F32, kind="ExternalInput")
    wu_t = nc.dram_tensor("wu_t", [NIC, 128, H], F32, kind="ExternalInput")
    wd_t = nc.dram_tensor("wd_t", [NHC, 128, ICF], F32, kind="ExternalInput")
    convw_s = nc.dram_tensor("convw_s", [128, 64], F32, kind="ExternalInput")
    convb_s = nc.dram_tensor("convb_s", [128, 16], F32, kind="ExternalInput")
    dtb_s = nc.dram_tensor("dtb_s", [128, 16], F32, kind="ExternalInput")
    A_s = nc.dram_tensor("A_s", [128, 256], F32, kind="ExternalInput")
    D_s = nc.dram_tensor("D_s", [128, 16], F32, kind="ExternalInput")
    gs_s = nc.dram_tensor("gs_s", [128, NIC], F32, kind="ExternalInput")
    us_s = nc.dram_tensor("us_s", [128, NIC], F32, kind="ExternalInput")
    ds_s = nc.dram_tensor("ds_s", [128, NHC], F32, kind="ExternalInput")
    dtln_s = nc.dram_tensor("dtln_s", [128, 1], F32, kind="ExternalInput")
    bln_s = nc.dram_tensor("bln_s", [16, 1], F32, kind="ExternalInput")
    cln_s = nc.dram_tensor("cln_s", [16, 1], F32, kind="ExternalInput")
    oneh_s = nc.dram_tensor("oneh_s", [16, NN * 128], F32, kind="ExternalInput")
    cst_in = nc.dram_tensor("cst_in", [128, NCST], F32, kind="ExternalInput")

    o_out = nc.dram_tensor("o_out", [NTF, NHC, 128, TFF], F32, kind="ExternalOutput")

    d_xn1 = nc.dram_tensor("d_xn1", [NTF, NHC, 128, TFF], BF16)
    d_x1 = nc.dram_tensor("d_x1", [NTF, NHC, 128, TFF], F32)
    d_x2 = nc.dram_tensor("d_x2", [NTF, NHC, 128, TFF], F32)
    d_xn3 = nc.dram_tensor("d_xn3", [NTF, NHC, 128, TFF], BF16)
    d_h = nc.dram_tensor("d_h", [NCHG, MYKT, 128, LC], F32)
    d_u = nc.dram_tensor("d_u", [NCHG, MYKT, 128, LC], F32)
    d_gate = nc.dram_tensor("d_gate", [NCHG, MYKT, 128, LC], F32)
    d_bc = nc.dram_tensor("d_bc", [NCHG, 32, LC], F32)
    d_wg = nc.dram_tensor("d_wg", [NIC, 128, H], BF16)
    d_wu = nc.dram_tensor("d_wu", [NIC, 128, H], BF16)
    d_wd = nc.dram_tensor("d_wd", [NHC, 128, ICF], BF16)
    d_wout = nc.dram_tensor("d_wout", [NHC, 128, IH], BF16)
    cc1_in = [nc.dram_tensor(f"cc1i{s}", [2, NHC, 128, TFF], BF16) for s in range(2)]
    cc1_out = [nc.dram_tensor(f"cc1o{s}", [2, 2, NHC, 128, TFF], BF16) for s in range(2)]
    cc2_in = [nc.dram_tensor(f"cc2i{s}", [4, 160, LC], F32) for s in range(2)]
    cc2_out = [nc.dram_tensor(f"cc2o{s}", [4, 160, LC], F32) for s in range(2)]
    cc3_in = [nc.dram_tensor(f"cc3i{j}", [2 * H, LC], F32) for j in range(4)]
    cc3_out = [nc.dram_tensor(f"cc3o{j}", [H, LC], F32) for j in range(4)]

    with ExitStack() as ctx:
        sb = lambda name, shape, dt_: ctx.enter_context(nc.sbuf_tensor(name, shape, dt_))
        cst = sb("cst", [128, NCST], F32)
        ones = sb("ones", [128, 1], F32)
        onesr = sb("onesr", [1, 128], F32)
        convw = sb("convw", [128, 64], F32)
        convb = sb("convb", [128, 16], F32)
        dtb = sb("dtb", [128, 16], F32)
        A_sb = sb("A_sb", [128, 256], F32)
        D_sb = sb("D_sb", [128, 16], F32)
        gs = sb("gs", [128, NIC], F32)
        us = sb("us", [128, NIC], F32)
        ds = sb("ds", [128, NHC], F32)
        hds = sb("hds", [128, NHC], F32)
        dtln = sb("dtln", [128, 1], F32)
        bln = sb("bln", [16, 1], F32)
        cln = sb("cln", [16, 1], F32)
        oneh = sb("oneh", [16, NN * 128], F32)
        wx_bf = sb("wx_bf", [128, MYKT * 160], BF16)
        wdt_bf = sb("wdt_bf", [128, IH], BF16)
        ts_n = sb("ts_n", [128, LG], BF16)
        state = sb("state", [128, MYKT * NN], F32)
        s_ring = sb("s_ring", [128, 2 * KB * (LC + 1)], F32)
        stat_row = sb("stat_row", [4, 512], F32)
        AR_N = 82000
        arena = Arena(sb("arena", [128, AR_N], BF16), AR_N)

        ps_t = [ctx.enter_context(nc.psum_tensor(f"ps{i}", [128, 512], F32))
                for i in range(8)]
        banks = Banks(ps_t)

        p = Prog()

        def ld(dst_ap, src_ap, after=(), q="sq", chan=None):
            return p.do(q, lambda e, d=dst_ap, s=src_ap: e.dma_start(out=d, in_=s),
                        after=after, dma=True, chan=chan)

        st_ = ld

        def cc(kind, op_, in_ap, out_ap, after, chan):
            return p.do("gq", lambda e, k=kind, o=op_, i=in_ap, u=out_ap:
                        e.collective_compute(k, o, replica_groups=PAIRS,
                                             ins=[i], outs=[u]),
                        after=after, cc=True, chan=chan)

        def mm(bank_i, lhsT, rhs, start, stop, after=(), base=0, m=128):
            out = banks.t[bank_i][base:base + m, :rhs.shape[-1]]
            extra = [banks.r[bank_i]] if start else []
            op = p.do("pe", lambda e, o=out, lh=lhsT, r=rhs, s1=start, s2=stop:
                      e.matmul(o, lh, r, start=s1, stop=s2),
                      after=list(after) + extra)
            banks.w[bank_i] = op
            return op

        def bank_read(bi_, op):
            banks.r[bi_] = op
            return op

        def act(out, in_, func, bias=0.0, scale=1.0, after=()):
            return p.do("act", lambda e, o=out, i=in_, f=func, b=bias, s=scale:
                        e.activation(o, i, f, bias=b, scale=s), after=after)

        def tt(out, a, b_, aop, after=(), eng="dve"):
            return p.do(eng, lambda e, o=out, x=a, y=b_, op_=aop:
                        e.tensor_tensor(o, x, y, op_), after=after)

        def ts(out, a, s1, s2, op0, op1=None, after=(), eng="dve"):
            if op1 is None:
                return p.do(eng, lambda e, o=out, x=a, u=s1, v=s2, p0=op0:
                            e.tensor_scalar(o, x, u, v, p0), after=after)
            return p.do(eng, lambda e, o=out, x=a, u=s1, v=s2, p0=op0, p1=op1:
                        e.tensor_scalar(o, x, u, v, p0, p1), after=after)

        def stt(out, in0, scal, in1, op0, op1, after=(), eng="dve"):
            return p.do(eng, lambda e, o=out, x=in0, s=scal, y=in1, p0=op0, p1=op1:
                        e.scalar_tensor_tensor(o, x, s, y, p0, p1), after=after)

        def cp(out, in_, after=(), eng="dve"):
            return p.do(eng, lambda e, o=out, i=in_: e.tensor_copy(o, i), after=after)

        def memset(ap, v, after=(), eng="dve"):
            return p.do(eng, lambda e, a=ap, vv=v: e.memset(a, vv), after=after)

        def scan(out, d0, d1, init, after=()):
            return p.do("dve", lambda e, o=out, a=d0, b_=d1, i=init:
                        e.tensor_tensor_scan(o, a, b_, i, ALU.mult, ALU.add),
                        after=after)

        cEPS1 = cst[0:1, C_EPS:C_EPS + 1]
        cNEGH1 = cst[0:1, C_NEGH:C_NEGH + 1]
        c151 = cst[0:1, C_15:C_15 + 1]
        cI2048 = cst[0:1, C_I2048:C_I2048 + 1]
        cI128 = cst[0:1, C_I128:C_I128 + 1]
        cI16 = cst[0:1, C_I16:C_I16 + 1]

        stat_busy = [None]

        def rstd_chain(src_row, prescale, rows, cols, after=()):
            row_v, row_r0, row_t = rows
            o1 = ts(row_v[:, 0:cols], src_row, prescale, cEPS1,
                    ALU.mult, ALU.add, after=after)
            o2 = act(row_r0[:, 0:cols], row_v[:, 0:cols], AF.Ln, after=[o1])
            o3 = act(row_r0[:, 0:cols], row_r0[:, 0:cols], AF.Exp, scale=cNEGH1,
                     after=[o2])
            o4 = tt(row_t[:, 0:cols], row_r0[:, 0:cols], row_r0[:, 0:cols],
                    ALU.mult, after=[o3])
            o5 = tt(row_t[:, 0:cols], row_t[:, 0:cols], row_v[:, 0:cols],
                    ALU.mult, after=[o4, o1])
            o6 = ts(row_t[:, 0:cols], row_t[:, 0:cols], cNEGH1, c151,
                    ALU.mult, ALU.add, after=[o5])
            o7 = tt(stat_row[0:1, 0:cols], row_r0[:, 0:cols], row_t[:, 0:cols],
                    ALU.mult, after=[o6, o3, stat_busy[0]])
            return o7

        lds = []
        for dst, src in ((cst, cst_in), (convw, convw_s), (convb, convb_s),
                         (dtb, dtb_s), (A_sb, A_s), (D_sb, D_s), (gs, gs_s),
                         (us, us_s), (ds, ds_s), (dtln, dtln_s)):
            lds.append(ld(dst[:], src.ap(), chan="p0c"))
        lds.append(ld(bln[0:16, :], bln_s.ap(), chan="p0c"))
        lds.append(ld(cln[0:16, :], cln_s.ap(), chan="p0c"))
        lds.append(ld(oneh[0:16, :], oneh_s.ap(), chan="p0c"))
        wx_src = bass.AP(wx_t, 0, [[160, 128], [128 * 160, MYKT], [1, 160]])
        lds.append(ld(wx_bf[:], wx_src, q="gq", chan="p0g"))
        lds.append(ld(wdt_bf[:], wdt_t.ap(), q="gq", chan="p0g"))

        memset(ones[:], 1.0)
        memset(onesr[0:1, :], 1.0)
        ts(hds[:], ds[:], cst[:, C_HALF:C_HALF + 1], None, ALU.mult, after=lds)
        memset(state[:], 0.0)

        cast_m0 = arena.mark()
        wc_f = [arena.alloc(4096, F32) for _ in range(2)]
        wc_b = [arena.alloc(4096, BF16) for _ in range(2)]
        cast_jobs = []
        for ic0 in range(0, NIC, 2):
            n = min(2, NIC - ic0)
            for (src, dst) in ((wg_t, d_wg), (wu_t, d_wu)):
                ap_pat = [[H, 128], [128 * H, n], [1, H]]
                cast_jobs.append((bass.AP(src, ic0 * 128 * H, ap_pat),
                                  bass.AP(dst, ic0 * 128 * H, ap_pat), n * H))
        for hc in range(NHC):
            for piece in range(2):
                w = ICF // 2
                ap_pat = [[ICF, 128], [1, w]]
                cast_jobs.append((bass.AP(wd_t, hc * 128 * ICF + piece * w, ap_pat),
                                  bass.AP(d_wd, hc * 128 * ICF + piece * w, ap_pat),
                                  w))
        for ot in range(0, NHC, 2):
            ap_pat = [[IH, 128], [128 * IH, 2], [1, IH]]
            cast_jobs.append((bass.AP(wout_t, ot * 128 * IH, ap_pat),
                              bass.AP(d_wout, ot * 128 * IH, ap_pat), 2 * IH))
        cast_c = [None, None]
        cast_s = [None, None]
        for j, (srcap, dstap, cols) in enumerate(cast_jobs):
            bi = j % 2
            l_ = ld(wc_f[bi][:, 0:cols], srcap, after=[cast_c[bi]],
                    chan=f"cstl{bi}")
            c_ = act(wc_b[bi][:, 0:cols], wc_f[bi][:, 0:cols], AF.Copy,
                     after=[l_, cast_s[bi]])
            s_ = st_(dstap, wc_b[bi][:, 0:cols], after=[c_], chan=f"csts{bi}")
            cast_c[bi] = c_
            cast_s[bi] = s_

        def ln_phase(src_dram, dst_fn):
            m0 = arena.mark()
            x_sb = [arena.alloc(8192, F32) for _ in range(2)]
            sq_sb = [arena.alloc(512, F32) for _ in range(2)]
            rows = [arena.alloc(512, F32)[0:1, :] for _ in range(3)]
            row_mu = arena.alloc(512, F32)[0:1, :]
            row_oc = arena.alloc(512, F32)[0:1, :]
            t1 = arena.alloc(512, F32)
            xn_sb = [arena.alloc(8192, BF16) for _ in range(2)]
            lu_x = [None, None]
            lu_xn = [None, None]
            oc_busy = [None]
            sqmm = [None, None]
            for t in range(NTF):
                bi = t % 2
                bk_x = 4 + bi
                bk_s = 6 + bi
                bk_r = 0 + bi
                bk_c = 2 + bi
                srcap = bass.AP(src_dram, t * 16 * 128 * TFF,
                                [[TFF, 128], [128 * TFF, 16], [1, TFF]])
                ldx = [ld(x_sb[bi][:], srcap, after=[lu_x[bi]], chan=f"lnx{bi}")]
                mmx = None
                mms = None
                for hc in range(NHC):
                    xs = x_sb[bi][:, hc * 512:(hc + 1) * 512]
                    mmx = mm(bk_x, ones[:, 0:1], xs, start=(hc == 0),
                             stop=(hc == NHC - 1), after=ldx, base=0, m=1)
                    sq_ev = act(sq_sb[hc % 2], xs, AF.Square,
                                after=ldx + [sqmm[hc % 2]])
                    mms = mm(bk_s, ones[:, 0:1], sq_sb[hc % 2],
                             start=(hc == 0), stop=(hc == NHC - 1),
                             after=[sq_ev], base=0, m=1)
                    sqmm[hc % 2] = mms
                mu = act(row_mu, banks.t[bk_x][0:1, :], AF.Copy, scale=cI2048,
                         after=[mmx])
                bank_read(bk_x, mu)
                ex2 = act(rows[0][0:1, :], banks.t[bk_s][0:1, :], AF.Copy,
                          scale=cI2048, after=[mms])
                bank_read(bk_s, ex2)
                mu2 = act(rows[2][0:1, :], row_mu, AF.Square, after=[mu])
                var = tt(rows[0][0:1, :], rows[0][0:1, :], rows[2][0:1, :],
                         ALU.subtract, after=[ex2, mu2])
                r7 = rstd_chain(rows[0][0:1, :], cst[0:1, C_ONE:C_ONE + 1],
                                rows, 512, after=[var])
                oc = tt(row_oc, row_mu, stat_row[0:1, :], ALU.mult,
                        after=[r7, mu, oc_busy[0]])
                br = mm(bk_r, onesr[0:1, 0:128], stat_row[0:1, :],
                        start=True, stop=True, after=[r7])
                bc_ = mm(bk_c, onesr[0:1, 0:128], row_oc,
                         start=True, stop=True, after=[oc])
                stat_busy[0] = [br, oc]
                oc_busy[0] = bc_
                m2s = []
                for hc in range(NHC):
                    xs = x_sb[bi][:, hc * 512:(hc + 1) * 512]
                    m1 = tt(t1, xs, banks.t[bk_r][:], ALU.mult,
                            after=[br, lu_xn[bi], (m2s[-1] if m2s else None)])
                    m2 = tt(xn_sb[bi][:, hc * 512:(hc + 1) * 512], t1,
                            banks.t[bk_c][:], ALU.subtract, after=[m1, bc_])
                    m2s.append(m2)
                bank_read(bk_r, m2s[-1])
                bank_read(bk_c, m2s[-1])
                lu_x[bi] = m2s[-1]
                so = dst_fn(t, xn_sb[bi], [m2s[-1]], bi)
                lu_xn[bi] = so
                if paranoid & 1:
                    p.barrier()
            p.barrier()
            arena.reset(m0)

        def mk_dram_dst(dst_dram):
            def f(t, xn_tile, after, bi):
                dstap = bass.AP(dst_dram, t * 16 * 128 * TFF,
                                [[TFF, 128], [128 * TFF, 16], [1, TFF]])
                return st_(dstap, xn_tile[:], after=after, chan=f"lno{bi}")
            return f

        cc1_store_last = [[], []]

        def cc1_dst(t, xn_tile, after, bi):
            s = t // 2
            dstap = bass.AP(cc1_in[s], (t % 2) * 16 * 128 * TFF,
                            [[TFF, 128], [128 * TFF, 16], [1, TFF]])
            so = st_(dstap, xn_tile[:], after=after, chan=f"lno{bi}")
            cc1_store_last[s].append(so)
            return so

        def ffn_phase(xn_dram, res_dram, out_dram):
            m0 = arena.mark()
            xn_T = [arena.alloc(8192, BF16) for _ in range(2)]
            hdn_T = arena.alloc(NIC * 512, BF16)
            wgb = [arena.alloc(2048, BF16) for _ in range(2)]
            wub = [arena.alloc(2048, BF16) for _ in range(2)]
            wdb = [arena.alloc(ICF, BF16) for _ in range(2)]
            gsl = [arena.alloc(512, F32) for _ in range(2)]
            x0b = [arena.alloc(512, F32) for _ in range(2)]
            x1b = [arena.alloc(512, F32) for _ in range(2)]
            lu_xn = [None, None]
            lu_wg = [None, None]; lu_wu = [None, None]; lu_wd = [None, None]
            lu_gs = [None, None]
            lu_x0 = [None, None]; lu_x1 = [None, None]
            hdn_reader = None
            for t in range(NTF):
                bi = t % 2
                srcap = bass.AP(xn_dram, t * 16 * 128 * TFF,
                                [[TFF, 128], [128 * TFF, 16], [1, TFF]])
                ldx = [ld(xn_T[bi][:], srcap, after=[lu_xn[bi]], chan=f"fxn{bi}")]
                hdn_ops = []
                for ic in range(NIC):
                    wi = ic % 2
                    bg = wi * 2
                    bu = wi * 2 + 1
                    lg = ld(wgb[wi][:], bass.AP(d_wg, ic * 128 * H, [[H, 128], [1, H]]),
                            after=[lu_wg[wi]], chan=f"fwg{wi}")
                    lu_ = ld(wub[wi][:], bass.AP(d_wu, ic * 128 * H, [[H, 128], [1, H]]),
                             after=[lu_wu[wi]], chan=f"fwu{wi}")
                    gm = um = None
                    for hc in range(NHC):
                        gm = mm(bg, wgb[wi][:, hc * 128:(hc + 1) * 128],
                                xn_T[bi][:, hc * 512:(hc + 1) * 512],
                                start=(hc == 0), stop=(hc == NHC - 1),
                                after=ldx + [lg])
                        um = mm(bu, wub[wi][:, hc * 128:(hc + 1) * 128],
                                xn_T[bi][:, hc * 512:(hc + 1) * 512],
                                start=(hc == 0), stop=(hc == NHC - 1),
                                after=ldx + [lu_])
                    lu_wg[wi] = gm
                    lu_wu[wi] = um
                    eg = act(gsl[wi], banks.t[bg][:], AF.Silu,
                             scale=gs[:, ic:ic + 1], after=[gm, lu_gs[wi]])
                    bank_read(bg, eg)
                    hm = stt(hdn_T[:, ic * 512:(ic + 1) * 512],
                             banks.t[bu][:], us[:, ic:ic + 1], gsl[wi],
                             ALU.mult, ALU.mult,
                             after=[um, eg, hdn_reader])
                    bank_read(bu, hm)
                    lu_gs[wi] = hm
                    hdn_ops.append(hm)
                lu_xn[bi] = hdn_ops[-1]
                ev_last = None
                for hc in range(NHC):
                    wi = hc % 2
                    bd = 4 + wi
                    lw = ld(wdb[wi][:], bass.AP(d_wd, hc * 128 * ICF,
                                                [[ICF, 128], [1, ICF]]),
                            after=[lu_wd[wi]], chan=f"fwd{wi}")
                    lx = ld(x0b[wi][:], bass.AP(res_dram, (t * 16 + hc) * 128 * TFF,
                                                [[TFF, 128], [1, TFF]]),
                            after=[lu_x0[wi]], chan=f"fx0{wi}")
                    dm = None
                    for ic in range(NIC):
                        dm = mm(bd, wdb[wi][:, ic * 128:(ic + 1) * 128],
                                hdn_T[:, ic * 512:(ic + 1) * 512],
                                start=(ic == 0), stop=(ic == NIC - 1),
                                after=[lw, hdn_ops[ic]])
                    lu_wd[wi] = dm
                    ev = stt(x1b[wi][:], banks.t[bd][:], hds[:, hc:hc + 1],
                             x0b[wi][:], ALU.mult, ALU.add,
                             after=[dm, lx, lu_x1[wi]])
                    bank_read(bd, ev)
                    lu_x0[wi] = ev
                    so = st_(bass.AP(out_dram, (t * 16 + hc) * 128 * TFF,
                                     [[TFF, 128], [1, TFF]]),
                             x1b[wi][:], after=[ev], chan=f"fx1{wi}")
                    lu_x1[wi] = so
                    ev_last = ev
                hdn_reader = ev_last
                if paranoid & 2:
                    p.barrier()
            p.barrier()
            arena.reset(m0)

        def pass1a():
            m0 = arena.mark()
            xnb = [arena.alloc(16 * BLK, BF16) for _ in range(2)]
            winb = [arena.alloc(2048, BF16) for _ in range(2)]
            hst = [arena.alloc(1024, F32) for _ in range(2)]
            lu_xn = [None, None]
            lu_w = [None, None]
            lu_h = [None, None]
            blocks = [(0, 0, 0), (2, 2048, 0), (1, 0, 1), (3, 2048, 1)]
            for bki, (blk, rowb, sup) in enumerate(blocks):
                bbi = bki % 2
                srcap = bass.AP(cc1_out[sup], rowb * BLK,
                                [[TFF, 128], [128 * TFF, 32], [1, TFF]])
                ldx = [ld(xnb[bbi][:], srcap,
                          after=[lu_xn[bbi], cc_ops["cc1"][sup]],
                          chan=f"paxn{bbi}")]
                for ot in range(NOT):
                    wi = ot % 2
                    lw = ld(winb[wi][:],
                            bass.AP(w_in_t, ot * 128 * H, [[H, 128], [1, H]]),
                            after=[lu_w[wi]], q="gq", chan=f"paw{wi}")
                    is_h = ot < MYKT
                    hi = wi
                    evs = []
                    for c4 in range(2):
                        bpj = 6 + (c4 % 2)
                        pm = None
                        for hc in range(NHC):
                            pm = mm(bpj, winb[wi][:, hc * 128:(hc + 1) * 128],
                                    xnb[bbi][:, (c4 * 16 + hc) * 512:
                                             (c4 * 16 + hc) * 512 + 512],
                                    start=(hc == 0), stop=(hc == NHC - 1),
                                    after=ldx + [lw])
                        if is_h:
                            ev = act(hst[hi][:, c4 * 512:(c4 + 1) * 512],
                                     banks.t[bpj][:], AF.Copy,
                                     after=[pm, lu_h[hi]])
                        else:
                            ev = act(hst[hi][:, c4 * 512:(c4 + 1) * 512],
                                     banks.t[bpj][:], AF.Silu,
                                     after=[pm, lu_h[hi]])
                        bank_read(bpj, ev)
                        evs.append(ev)
                    lu_w[wi] = evs[-1]
                    tgt = d_h if is_h else d_gate
                    kt = ot if is_h else ot - MYKT
                    sos = []
                    for c4 in range(2):
                        sos.append(st_(
                            bass.AP(tgt, ((2 * blk + c4) * MYKT + kt) * 128 * LC,
                                    [[LC, 128], [1, LC]]),
                            hst[hi][:, c4 * 512:(c4 + 1) * 512],
                            after=evs, chan=f"pah{hi}"))
                    lu_h[hi] = sos[-1]
                if paranoid & 4:
                    p.barrier()
            p.barrier()
            arena.reset(m0)

        def pass1b():
            m0 = arena.mark()
            hbuf = [arena.alloc(LT + 3, F32) for _ in range(2)]
            acc_a = [arena.alloc(LT, F32) for _ in range(2)]
            acc_b = [arena.alloc(LT, F32) for _ in range(2)]
            ubf = [arena.alloc(LT, BF16) for _ in range(2)]
            tsev = arena.alloc(2048, F32)
            bcev = arena.alloc(2048, F32)
            lu_h = [None, None]
            lu_u = [None, None]
            acc_busy = [None, None]
            ev_stores = []
            for half in range(2):
                col0 = half * LT
                for ot in range(MYKT):
                    wi = ot % 2
                    lh = []
                    if half == 0:
                        lh.append(memset(hbuf[wi][:, 0:3], 0.0,
                                         after=[lu_h[wi]]))
                    else:
                        lh.append(ld(hbuf[wi][:, 0:3],
                                     bass.AP(d_h, (3 * MYKT + ot) * 128 * LC
                                             + LC - 3,
                                             [[LC, 128], [1, 3]]),
                                     after=[lu_h[wi]], chan=f"pbh{wi}"))
                    for cc4 in range(4):
                        gch = half * 4 + cc4
                        lh.append(ld(hbuf[wi][:, 3 + cc4 * LC:3 + (cc4 + 1) * LC],
                                     bass.AP(d_h, (gch * MYKT + ot) * 128 * LC,
                                             [[LC, 128], [1, LC]]),
                                     after=[lu_h[wi]], chan=f"pbh{wi}"))
                    c0 = ts(acc_a[wi], hbuf[wi][:, 0:LT],
                            convw[:, ot * 4:ot * 4 + 1], None, ALU.mult,
                            after=lh + [acc_busy[wi]])
                    c1 = stt(acc_b[wi], hbuf[wi][:, 1:1 + LT],
                             convw[:, ot * 4 + 1:ot * 4 + 2], acc_a[wi],
                             ALU.mult, ALU.add, after=[c0])
                    c2 = stt(acc_a[wi], hbuf[wi][:, 2:2 + LT],
                             convw[:, ot * 4 + 2:ot * 4 + 3], acc_b[wi],
                             ALU.mult, ALU.add, after=[c1])
                    c3 = stt(acc_b[wi], hbuf[wi][:, 3:3 + LT],
                             convw[:, ot * 4 + 3:ot * 4 + 4], acc_a[wi],
                             ALU.mult, ALU.add, after=[c2])
                    lu_h[wi] = c3
                    sl = act(acc_a[wi], acc_b[wi], AF.Silu,
                             bias=convb[:, ot:ot + 1], after=[c3])
                    sus = []
                    for cc4 in range(4):
                        gch = half * 4 + cc4
                        sus.append(st_(
                            bass.AP(d_u, (gch * MYKT + ot) * 128 * LC,
                                    [[LC, 128], [1, LC]]),
                            acc_a[wi][:, cc4 * LC:(cc4 + 1) * LC],
                            after=[sl], chan=f"pbu{wi}"))
                    su = sus[-1]
                    cb = act(ubf[wi][:], acc_a[wi], AF.Copy, after=[sl, lu_u[wi]])
                    acc_busy[wi] = [su, cb]
                    mm_last = []
                    for c4 in range(4):
                        bts = c4
                        bbc = 4 + c4
                        u_sl = ubf[wi][:, c4 * 512:(c4 + 1) * 512]
                        m1 = mm(bts, wx_bf[:, ot * 160:ot * 160 + 128], u_sl,
                                start=(ot == 0), stop=(ot == MYKT - 1),
                                after=[cb])
                        m2 = mm(bbc, wx_bf[:, ot * 160 + 128:ot * 160 + 160],
                                u_sl, start=(ot == 0), stop=(ot == MYKT - 1),
                                after=[cb], m=32)
                        mm_last.append(m1)
                        mm_last.append(m2)
                    lu_u[wi] = mm_last
                evs = []
                for c4 in range(4):
                    e1 = act(tsev[:, c4 * 512:(c4 + 1) * 512], banks.t[c4][:],
                             AF.Copy, after=[banks.w[c4]] + ev_stores)
                    bank_read(c4, e1)
                    e2 = act(bcev[0:32, c4 * 512:(c4 + 1) * 512],
                             banks.t[4 + c4][0:32, :], AF.Copy,
                             after=[banks.w[4 + c4]] + ev_stores)
                    bank_read(4 + c4, e2)
                    evs += [e1, e2]
                s1 = st_(bass.AP(cc2_in[half], 0,
                                 [[LC, 128], [160 * LC, 4], [1, LC]]),
                         tsev[:], after=evs, chan="pbt")
                s2 = st_(bass.AP(cc2_in[half], 128 * LC,
                                 [[LC, 32], [160 * LC, 4], [1, LC]]),
                         bcev[0:32, :], after=evs, chan="pbt")
                ev_stores = [s1, s2]
                cc_ops["cc2"][half] = cc("AllReduce", ALU.add,
                                         cc2_in[half].ap(), cc2_out[half].ap(),
                                         after=[s1, s2], chan=f"cc2_{half}")
                if paranoid & 4:
                    p.barrier()
            p.barrier()
            arena.reset(m0)

        def rms_phase():
            m0 = arena.mark()
            tsb = [arena.alloc(512, F32) for _ in range(2)]
            bb_ = [arena.alloc(512, F32) for _ in range(2)]
            cb_ = [arena.alloc(512, F32) for _ in range(2)]
            sq = arena.alloc(512, F32)
            rows = [arena.alloc(512, F32)[0:1, :] for _ in range(3)]
            tf = arena.alloc(512, F32)
            lu = [None, None]
            lu_bc = [None, None]
            for c in range(NCHG):
                half = c // 4
                coff = (c % 4) * 512
                bi = c % 2
                c4g = c % 4
                lt_ = ld(tsb[bi][:], bass.AP(cc2_out[half], c4g * 160 * LC,
                                             [[LC, 128], [1, LC]]),
                         after=[lu[bi], cc_ops["cc2"][half]], chan=f"rmt{bi}")
                lb_ = ld(bb_[bi][0:16, :],
                         bass.AP(cc2_out[half], c4g * 160 * LC + 128 * LC,
                                 [[LC, 16], [1, LC]]),
                         after=[lu_bc[bi], cc_ops["cc2"][half]], chan=f"rmb{bi}")
                lc2 = ld(cb_[bi][0:16, :],
                         bass.AP(cc2_out[half], c4g * 160 * LC + 144 * LC,
                                 [[LC, 16], [1, LC]]),
                         after=[lu_bc[bi], cc_ops["cc2"][half]], chan=f"rmb{bi}")
                sq1 = act(sq, tsb[bi][:], AF.Square, after=[lt_])
                sm = mm(6, ones[:, 0:1], sq, start=True, stop=True,
                        after=[sq1], m=1)
                r7 = rstd_chain(banks.t[6][0:1, :], cI128, rows, 512, after=[sm])
                bank_read(6, r7)
                br = mm(7, onesr[0:1, 0:128], stat_row[0:1, :], start=True,
                        stop=True, after=[r7])
                stat_busy[0] = br
                t1_ = tt(tf, tsb[bi][:], banks.t[7][:], ALU.mult,
                         after=[br, sq1])
                t2_ = ts(ts_n[:, c * 512:(c + 1) * 512], tf, dtln[:, 0:1],
                         None, ALU.mult, after=[t1_])
                bank_read(7, t2_)
                lu[bi] = t2_
                prev = t2_
                for half2, w_ln in ((0, bln), (1, cln)):
                    src16 = bb_[bi][0:16, :] if half2 == 0 else cb_[bi][0:16, :]
                    sq2 = act(sq[0:16, :], src16, AF.Square,
                              after=[lb_, lc2, prev, sm])
                    sm2 = mm(6, ones[0:16, 0:1], sq[0:16, :], start=True,
                             stop=True, after=[sq2], m=1)
                    r72 = rstd_chain(banks.t[6][0:1, :], cI16, rows, 512,
                                     after=[sm2])
                    bank_read(6, r72)
                    br2 = mm(7, onesr[0:1, 0:128], stat_row[0:1, :],
                             start=True, stop=True, after=[r72])
                    stat_busy[0] = br2
                    n1 = tt(tf[0:16, :], src16, banks.t[7][0:16, :], ALU.mult,
                            after=[br2, sq2])
                    n2 = ts(src16, tf[0:16, :],
                            w_ln[0:16, 0:1], None, ALU.mult, after=[n1])
                    bank_read(7, n2)
                    sb2 = st_(bass.AP(d_bc, (c * 32 + half2 * 16) * LC,
                                      [[LC, 16], [1, LC]]),
                              src16, after=[n2], chan="rmo")
                    prev = sb2
                lu_bc[bi] = prev
            p.barrier()
            arena.reset(m0)

        def scan_phase():
            m0 = arena.mark()
            Bb = arena.alloc(NN * 512, F32)
            Cb = arena.alloc(NN * 512, F32)
            bst = arena.alloc(512, F32)
            cstg = arena.alloc(512, F32)
            ub = [arena.alloc(KB * 512, F32)]
            gat = [arena.alloc(KB * 512, F32)]
            delta = arena.alloc(KB * 512, F32)
            du = arena.alloc(KB * 512, F32)
            yb = arena.alloc(KB * 512, F32)
            dab = [arena.alloc(LC + 1, F32) for _ in range(4)]
            dbb = [arena.alloc(LC + 1, F32) for _ in range(4)]
            ymt_d = arena.alloc(512, F32)
            ymt_p = arena.alloc(512, F32)
            tmp_sp = [arena.alloc(512, F32) for _ in range(2)]
            y_g = arena.alloc(MYKT * 512, BF16)
            woutb = [arena.alloc(IH, BF16) for _ in range(2)]
            wev = [arena.alloc(512, F32)]
            pad_ok = ([memset(dab[i][:, LC:LC + 1], 1.0) for i in range(4)]
                      + [memset(dbb[i][:, LC:LC + 1], 0.0) for i in range(4)])
            sring_v = s_ring[:].rearrange("p (q k l) -> p q k l", l=LC + 1, k=KB)
            state_v = state[:].rearrange("p (k n) -> p k n", n=NN)
            tbl_busy = [None]
            db_last = [None, None]
            lu_u = [None]; lu_g = [None]
            lu_w = [None, None]
            lu_wev = [None]
            delta_busy = None
            du_busy = None
            tmp_busy = [None, None]
            st_by_par = {0: None, 1: None}
            scan_hist = {}
            y2_hist = {}
            yg_last = {"dve": None, "pool": None}
            wout_done = None
            wout_stores = []
            for c in range(NCHG):
                lbst = ld(bst[0:16, :], bass.AP(d_bc, c * 32 * LC,
                                                [[LC, 16], [1, LC]]),
                          after=[tbl_busy[0]], chan="stb")
                lcst = ld(cstg[0:16, :], bass.AP(d_bc, (c * 32 + 16) * LC,
                                                 [[LC, 16], [1, LC]]),
                          after=[tbl_busy[0]], chan="stc")
                tev = []
                for n in range(NN):
                    bsel = n % 2
                    mb = mm(bsel, oneh[0:16, n * 128:(n + 1) * 128],
                            bst[0:16, :], start=True,
                            stop=True, after=[tbl_busy[0], lbst])
                    eb = act(Bb[:, n * 512:(n + 1) * 512], banks.t[bsel][:],
                             AF.Copy, after=[mb, tbl_busy[0]])
                    bank_read(bsel, eb)
                    mc = mm(bsel, oneh[0:16, n * 128:(n + 1) * 128],
                            cstg[0:16, :], start=True,
                            stop=True, after=[eb, lcst])
                    ec = act(Cb[:, n * 512:(n + 1) * 512], banks.t[bsel][:],
                             AF.Copy, after=[mc])
                    bank_read(bsel, ec)
                    tev.append(ec)
                tev_last = tev[-1]
                for kb in range(NKB):
                    d1_ops = []
                    y0_ops = []
                    lg_ops = []
                    delta_ops = []
                    for kk in range(KB):
                        k = kb * KB + kk
                        lh = ld(ub[0][:, kk * 512:kk * 512 + LC],
                                bass.AP(d_u, (c * MYKT + k) * 128 * LC,
                                        [[LC, 128], [1, LC]]),
                                after=[lu_u[0]], chan="su0")
                        lg = ld(gat[0][:, kk * 512:kk * 512 + LC],
                                bass.AP(d_gate, (c * MYKT + k) * 128 * LC,
                                        [[LC, 128], [1, LC]]),
                                after=[lu_g[0]], chan="sg0")
                        bdt = 6 + (kk % 2)
                        dmm = mm(bdt, wdt_bf[:, k * 128:(k + 1) * 128],
                                 ts_n[:, c * LC:(c + 1) * LC], start=True,
                                 stop=True, after=[])
                        dl = delta[:, kk * 512:kk * 512 + LC]
                        sp1 = act(tmp_sp[kk % 2], banks.t[bdt][:, 0:LC],
                                  AF.Abs, bias=dtb[:, k:k + 1],
                                  after=[dmm, tmp_busy[kk % 2]])
                        sp2 = act(tmp_sp[kk % 2], tmp_sp[kk % 2], AF.Exp,
                                  scale=cst[:, C_NEG1:C_NEG1 + 1], after=[sp1])
                        sp3 = act(tmp_sp[kk % 2], tmp_sp[kk % 2], AF.Ln,
                                  bias=cst[:, C_ONE:C_ONE + 1], after=[sp2])
                        sp4 = act(dl, banks.t[bdt][:, 0:LC], AF.Relu,
                                  bias=dtb[:, k:k + 1],
                                  after=[sp3, delta_busy])
                        bank_read(bdt, sp4)
                        sp5 = tt(dl, dl, tmp_sp[kk % 2], ALU.add,
                                 after=[sp4, delta_busy])
                        tmp_busy[kk % 2] = sp5
                        delta_ops.append(sp5)
                        d1 = tt(du[:, kk * 512:kk * 512 + LC], dl,
                                ub[0][:, kk * 512:kk * 512 + LC], ALU.mult,
                                after=[sp5, lh, du_busy])
                        y0 = ts(yb[:, kk * 512:kk * 512 + LC],
                                ub[0][:, kk * 512:kk * 512 + LC],
                                D_sb[:, k:k + 1], None, ALU.mult,
                                after=[lh, yg_last["dve"], yg_last["pool"]])
                        d1_ops.append(d1)
                        y0_ops.append(y0)
                        lg_ops.append(lg)
                    lu_u[0] = [d1_ops[-1], y0_ops[-1]]
                    yop_prev = dict(enumerate(y0_ops))
                    last_ex = None
                    last_db = None
                    st_ops = []
                    for n in range(NN):
                        par = n % 2
                        for kk in range(KB):
                            k = kb * KB + kk
                            dl = delta[:, kk * 512:kk * 512 + LC]
                            ex = act(dab[kk][:, 0:LC], dl, AF.Exp,
                                     scale=A_sb[:, k * NN + n:k * NN + n + 1],
                                     after=[scan_hist.get((kk, 1 - par)),
                                            delta_ops[kk]] + pad_ok)
                            last_ex = ex
                            db_ = tt(dbb[kk][:, 0:LC],
                                     du[:, kk * 512:kk * 512 + LC],
                                     Bb[:, n * 512:(n + 1) * 512], ALU.mult,
                                     after=[tev_last, scan_hist.get((kk, 1 - par)),
                                            d1_ops[kk]])
                            last_db = db_
                            sc = scan(sring_v[:, par, kk, :], dab[kk][:],
                                      dbb[kk][:],
                                      state[:, k * NN + n:k * NN + n + 1],
                                      after=[ex, db_, y2_hist.get((kk, par)),
                                             st_by_par[par]])
                            scan_hist[(kk, par)] = sc
                            yeng = "dve" if kk < 1 else "pool"
                            ymt = ymt_d if kk < 1 else ymt_p
                            y1 = tt(ymt, sring_v[:, par, kk, 0:LC],
                                    Cb[:, n * 512:(n + 1) * 512], ALU.mult,
                                    after=[sc, tev_last, yg_last[yeng]],
                                    eng=yeng)
                            y2 = tt(yb[:, kk * 512:kk * 512 + LC],
                                    yb[:, kk * 512:kk * 512 + LC], ymt,
                                    ALU.add, after=[y1, yop_prev.get(kk)],
                                    eng=yeng)
                            yop_prev[kk] = y2
                            y2_hist[(kk, par)] = y2
                            yg_last[yeng] = y2
                        stc = cp(state_v[:, kb * KB:(kb + 1) * KB, n],
                                 sring_v[:, par, :, LC],
                                 after=[scan_hist[(kk2, par)] for kk2 in range(KB)]
                                 + [st_by_par[par]], eng="pool")
                        st_by_par[par] = stc
                        st_ops.append(stc)
                    delta_busy = [last_ex, d1_ops[-1]]
                    du_busy = last_db
                    for kk in range(KB):
                        k = kb * KB + kk
                        yeng = "dve" if kk < 1 else "pool"
                        yg = tt(y_g[:, k * 512:k * 512 + LC],
                                yb[:, kk * 512:kk * 512 + LC],
                                gat[0][:, kk * 512:kk * 512 + LC], ALU.mult,
                                after=[yop_prev[kk], st_ops[-1], st_ops[-2],
                                       lg_ops[kk],
                                       yg_last[yeng], wout_done], eng=yeng)
                        yg_last[yeng] = yg
                    lu_g[0] = [yg_last["dve"], yg_last["pool"]]
                wev_ops = []
                for ot in range(NHC):
                    wi = ot % 2
                    bw = 2 + wi
                    lw = ld(woutb[wi][:],
                            bass.AP(d_wout, ot * 128 * IH, [[IH, 128], [1, IH]]),
                            after=[lu_w[wi]], chan=f"sw{wi}")
                    wm = None
                    for kt in range(MYKT):
                        wm = mm(bw, woutb[wi][:, kt * 128:(kt + 1) * 128],
                                y_g[:, kt * 512:(kt + 1) * 512],
                                start=(kt == 0), stop=(kt == MYKT - 1),
                                after=[lw, yg_last["dve"], yg_last["pool"]])
                    lu_w[wi] = wm
                    ev = act(wev[0][:], banks.t[bw][:, 0:LC], AF.Copy,
                             after=[wm, lu_wev[0]])
                    bank_read(bw, ev)
                    j = c % 4
                    rowb = 0 if c < 4 else H
                    so = st_(bass.AP(cc3_in[j], (rowb + ot * 128) * LC,
                                     [[LC, 128], [1, LC]]),
                             wev[0][:], after=[ev], chan="sv0")
                    lu_wev[0] = so
                    wev_ops.append(so)
                    wev_ops.append(ev)
                wout_done = wev_ops[-1]
                tbl_busy[0] = wev_ops[-1]
                wout_stores.append(wev_ops)
                if c >= 4:
                    j = c - 4
                    cc_ops["cc3"][j] = cc("ReduceScatter", ALU.add,
                                          cc3_in[j].ap(), cc3_out[j].ap(),
                                          after=wout_stores[j] + wout_stores[c],
                                          chan=f"cc3_{j}")
                if paranoid & 8:
                    p.barrier()
            p.barrier()
            arena.reset(m0)

        def x2_phase():
            m0 = arena.mark()
            a_ = [arena.alloc(512, F32) for _ in range(2)]
            b_ = [arena.alloc(512, F32) for _ in range(2)]
            lu = [None, None]
            for j in range(4):
                for hc in range(NHC):
                    bi = (j * NHC + hc) % 2
                    l1 = ld(a_[bi][:], bass.AP(d_x1, (j * 16 + hc) * 128 * LC,
                                               [[LC, 128], [1, LC]]),
                            after=[lu[bi]], chan=f"xa{bi}")
                    l2 = ld(b_[bi][:], bass.AP(cc3_out[j], hc * 128 * LC,
                                               [[LC, 128], [1, LC]]),
                            after=[lu[bi], cc_ops["cc3"][j]], chan=f"xb{bi}")
                    ad = tt(a_[bi][:], a_[bi][:], b_[bi][:], ALU.add,
                            after=[l1, l2])
                    so = st_(bass.AP(d_x2, (j * 16 + hc) * 128 * LC,
                                     [[LC, 128], [1, LC]]),
                             a_[bi][:], after=[ad], chan=f"xc{bi}")
                    lu[bi] = [ad, so]
            p.barrier()
            arena.reset(m0)

        cc_ops = {"cc1": [None, None], "cc2": [None, None],
                  "cc3": [None, None, None, None]}

        ln_phase(xT, mk_dram_dst(d_xn1))
        arena.reset(cast_m0)
        ffn_phase(d_xn1, xT, d_x1)
        ln_phase(d_x1, cc1_dst)
        for s in range(2):
            cc_ops["cc1"][s] = cc("AllGather", ALU.bypass, cc1_in[s].ap(),
                                  cc1_out[s].ap(), after=cc1_store_last[s],
                                  chan=f"cc1_{s}")
        pass1a()
        pass1b()
        rms_phase()
        scan_phase()
        x2_phase()
        ln_phase(d_x2, mk_dram_dst(d_xn3))
        ffn_phase(d_xn3, d_x2, o_out)

        if debug:
            p.barrier()
            dbg_map = {"d_xn1": (d_xn1, [H, LT], BF16), "d_x1": (d_x1, [H, LT], F32),
                       "d_h": (d_h, [MYKT, 128, LG], F32),
                       "d_u": (d_u, [MYKT, 128, LG], F32),
                       "d_gate": (d_gate, [MYKT, 128, LG], F32),
                       "d_x2": (d_x2, [H, LT], F32),
                       "cc1o0": (cc1_out[0], [2 * H, BLK], BF16),
                       "cc1o1": (cc1_out[1], [2 * H, BLK], BF16),
                       "cc2o0": (cc2_out[0], [160, LT], F32),
                       "cc2o1": (cc2_out[1], [160, LT], F32),
                       "cc3o0": (cc3_out[0], [H, LC], F32)}
            for name, (src, shape, dt_) in dbg_map.items():
                dbg = nc.dram_tensor("dbg_" + name, shape, dt_,
                                     kind="ExternalOutput")
                ld(dbg.ap(), src.ap(), chan="dbg")
            dbg_tsn = nc.dram_tensor("dbg_tsn", [128, LG], BF16,
                                     kind="ExternalOutput")
            dbg_bn = nc.dram_tensor("dbg_bn", [16, LG], F32,
                                    kind="ExternalOutput")
            dbg_cn = nc.dram_tensor("dbg_cn", [16, LG], F32,
                                    kind="ExternalOutput")
            ld(dbg_tsn.ap(), ts_n[:], chan="dbg")
            ld(dbg_bn.ap(), b_n[0:16, :], chan="dbg")
            ld(dbg_cn.ap(), c_n[0:16, :], chan="dbg")

        p.do("sq", None, after=list(p.chan_last.values()))

        per_stream, keys = p.finalize()
        sems = {k: ctx.enter_context(nc.semaphore(f"sem_{k}")) for k in keys}
        hw = {}
        block = ctx.enter_context(nc.Block())

        @block.sync
        def _(sync):
            Prog.emit_stream(sync, "sync", per_stream["sync"], sems, hw)

        @block.tensor
        def _(tensor):
            Prog.emit_stream(tensor, "tensor", per_stream["tensor"], sems, hw)

        @block.scalar
        def _(scalar):
            Prog.emit_stream(scalar, "scalar", per_stream["scalar"], sems, hw)

        @block.vector
        def _(vector):
            Prog.emit_stream(vector, "vector", per_stream["vector"], sems, hw)

        @block.gpsimd
        def _(gpsimd):
            Prog.emit_stream(gpsimd, "gpsimd", per_stream["gpsimd"], sems, hw)

    return nc


def _prep_shared(inp):
    gw = inp["L0_ffn_gate_w"]
    uw = inp["L0_ffn_up_w"]
    dw = inp["L0_ffn_down_w"]
    d = {}
    d["wg_t"] = np.ascontiguousarray(
        np.asarray(gw).reshape(NIC, 128, NHC, 128).transpose(0, 3, 2, 1)
        .reshape(NIC, 128, H))
    d["wu_t"] = np.ascontiguousarray(
        np.asarray(uw).reshape(NIC, 128, NHC, 128).transpose(0, 3, 2, 1)
        .reshape(NIC, 128, H))
    d["wd_t"] = np.ascontiguousarray(
        np.asarray(dw).reshape(NHC, 128, NIC, 128).transpose(0, 3, 2, 1)
        .reshape(NHC, 128, ICF))
    d["gs_s"] = np.ascontiguousarray(
        np.asarray(inp["L0_ffn_gate_s"]).reshape(NIC, 128).T)
    d["us_s"] = np.ascontiguousarray(
        np.asarray(inp["L0_ffn_up_s"]).reshape(NIC, 128).T)
    d["ds_s"] = np.ascontiguousarray(
        np.asarray(inp["L0_ffn_down_s"]).reshape(NHC, 128).T)
    d["dtln_s"] = np.ascontiguousarray(
        np.asarray(inp["L0_attn_dt_ln_w"]).reshape(128, 1))
    d["bln_s"] = np.ascontiguousarray(
        np.asarray(inp["L0_attn_b_ln_w"]).reshape(16, 1))
    d["cln_s"] = np.ascontiguousarray(
        np.asarray(inp["L0_attn_c_ln_w"]).reshape(16, 1))
    oneh = np.zeros((16, NN * 128), np.float32)
    for n in range(NN):
        oneh[n, n * 128:(n + 1) * 128] = 1.0
    d["oneh_s"] = oneh
    cvals = np.zeros(NCST, np.float32)
    cvals[C_EPS] = EPS
    cvals[C_ONE] = 1.0
    cvals[C_NEG1] = -1.0
    cvals[C_NEGH] = -0.5
    cvals[C_15] = 1.5
    cvals[C_HALF] = 0.5
    cvals[C_I2048] = 1.0 / 2048.0
    cvals[C_I128] = 1.0 / 128.0
    cvals[C_I16] = 1.0 / 16.0
    d["cst_in"] = np.ascontiguousarray(
        np.broadcast_to(cvals.astype(np.float32), (128, NCST)))
    return d


def _prep_core(inp, half):
    ch0 = half * IH
    W_in = np.asarray(inp["L0_attn_W_in"])
    Wx = np.asarray(inp["L0_attn_Wx"])
    Wdt = np.asarray(inp["L0_attn_Wdt"])
    Wout = np.asarray(inp["L0_attn_Wout"])
    d = {}
    w_in_rows = np.concatenate([W_in[ch0:ch0 + IH], W_in[I + ch0:I + ch0 + IH]],
                               axis=0)
    d["w_in_t"] = np.ascontiguousarray(
        w_in_rows.reshape(NOT, 128, NHC, 128).transpose(0, 3, 2, 1)
        .reshape(NOT, 128, H))
    d["wx_t"] = np.ascontiguousarray(
        Wx[:, ch0:ch0 + IH].T.reshape(MYKT, 128, 160))
    d["wdt_t"] = np.ascontiguousarray(Wdt[ch0:ch0 + IH].T)
    d["wout_t"] = np.ascontiguousarray(
        Wout[:, ch0:ch0 + IH].reshape(NHC, 128, MYKT, 128)
        .transpose(0, 3, 2, 1).reshape(NHC, 128, IH))
    d["convw_s"] = np.ascontiguousarray(
        np.asarray(inp["L0_attn_conv_w"])[ch0:ch0 + IH].reshape(16, 128, 4)
        .transpose(1, 0, 2).reshape(128, 64))
    d["convb_s"] = np.ascontiguousarray(
        np.asarray(inp["L0_attn_conv_b"])[ch0:ch0 + IH].reshape(16, 128).T)
    d["dtb_s"] = np.ascontiguousarray(
        np.asarray(inp["L0_attn_dt_bias"])[ch0:ch0 + IH].reshape(16, 128).T)
    d["A_s"] = np.ascontiguousarray(
        np.asarray(inp["L0_attn_A"])[ch0:ch0 + IH].reshape(16, 128, 16)
        .transpose(1, 0, 2).reshape(128, 256))
    d["D_s"] = np.ascontiguousarray(
        np.asarray(inp["L0_attn_D"])[ch0:ch0 + IH].reshape(16, 128).T)
    return d


def kernel(**inputs):
    shared = _prep_shared(inputs)
    x = np.asarray(inputs["L0_attn_x"])
    nc = build_nc()
    in_maps = []
    core_prep = [_prep_core(inputs, 0), _prep_core(inputs, 1)]
    for core in range(8):
        b = core // 2
        half = core % 2
        m = dict(shared)
        m.update(core_prep[half])
        xr = x[b].T[:, half * LT:(half + 1) * LT]
        m["xT"] = np.ascontiguousarray(
            xr.reshape(NHC, 128, NTF, TFF).transpose(2, 0, 1, 3))
        in_maps.append(m)
    res = run_bass_kernel_spmd(nc, in_maps, core_ids=list(range(8)))
    out = np.empty((B, LG, H), np.float32)
    for b in range(B):
        for half in range(2):
            o = np.asarray(res.results[2 * b + half]["o_out"])
            o = o.transpose(1, 2, 0, 3).reshape(H, LT)
            out[b, half * LT:(half + 1) * LT] = o.T
    return out


# version 10
# speedup vs baseline: 1.9122x; 1.0725x over previous
import numpy as np
import concourse.bass as bass
import concourse.mybir as mybir
from concourse.bass_utils import run_bass_kernel_spmd
from contextlib import ExitStack

F32 = mybir.dt.float32
BF16 = mybir.dt.bfloat16
AF = mybir.ActivationFunctionType
ALU = mybir.AluOpType

B = 4
LG = 4096
LT = 2048
H = 2048
I = 4096
IH = 2048
MYKT = 16
NOT = 32
NN = 16
R = 128
ICF = 5632
NIC = 44
NHC = 16
LC = 512
NCHG = LG // LC
TFF = 512
NTF = LT // TFF
BLK = 1024
KB = 4
NKB = MYKT // KB
EPS = 1e-5
PAIRS = [[0, 1], [2, 3], [4, 5], [6, 7]]

C_EPS, C_ONE, C_NEG1, C_NEGH, C_15, C_HALF, C_I2048, C_I128, C_I16, C_ZERO = range(10)
NCST = 10

STREAM_OF = {"pe": "tensor", "act": "scalar", "dve": "vector", "pool": "gpsimd",
             "sq": "sync", "aq": "scalar", "gq": "gpsimd"}
COMPUTE_ENGS = {"pe", "act", "dve", "pool"}
ENG_OF_STREAM = {"tensor": "pe", "scalar": "act", "vector": "dve",
                 "gpsimd": "pool", "sync": "sq"}


class Op:
    __slots__ = ("eng", "emit", "afters", "inc", "flag", "val", "seq", "key", "cc")

    def __init__(self, eng, emit, afters, inc, seq, key, cc=False):
        self.eng = eng
        self.emit = emit
        self.afters = afters
        self.inc = inc
        self.flag = False
        self.val = None
        self.seq = seq
        self.key = key
        self.cc = cc


def _flat(after):
    out = []
    for a in after:
        if a is None:
            continue
        if isinstance(a, (list, tuple)):
            out.extend(x for x in a if x is not None)
        else:
            out.append(a)
    return out


class Prog:

    def __init__(self):
        self.ops = []
        self.last_real = {}
        self.chan_last = {}

    def do(self, eng, emit, after=(), dma=False, chan=None, cc=False):
        key = chan if (dma or cc) else eng
        if dma or cc:
            assert chan is not None
        op = Op(eng, emit, _flat(after), 1 if cc else (16 if dma else 1),
                len(self.ops), key, cc)
        self.ops.append(op)
        if emit is not None:
            self.last_real[STREAM_OF[eng]] = op
            if dma or cc:
                self.chan_last[chan] = op
        return op

    def barrier(self):
        lasts = list(self.last_real.items())
        chans = list(self.chan_last.values())
        for st in ("tensor", "scalar", "vector", "gpsimd", "sync"):
            afters = [op for s2, op in lasts if s2 != st] + chans
            if afters:
                self.do(ENG_OF_STREAM[st], None, after=afters)

    def finalize(self):
        keys = []
        seen = set()
        for op in self.ops:
            if op.key not in seen:
                seen.add(op.key)
                keys.append(op.key)
        for op in self.ops:
            for a in op.afters:
                a.flag = True
        counters = {k: 0 for k in keys}
        for op in self.ops:
            if op.flag:
                assert op.emit is not None, "cannot wait on a wait-only op"
            if op.flag or (op.inc == 16 and op.emit is not None) or op.cc:
                op.flag = True
                counters[op.key] += op.inc
                op.val = counters[op.key]
        per_stream = {s: [] for s in ("tensor", "scalar", "vector", "gpsimd", "sync")}
        for op in self.ops:
            per_stream[STREAM_OF[op.eng]].append(op)
        return per_stream, keys

    @staticmethod
    def emit_stream(eng_obj, stream, ops, sems, hw):
        for op in ops:
            for a in op.afters:
                if (STREAM_OF[a.eng] == stream and a.eng in COMPUTE_ENGS
                        and not a.cc and a.seq < op.seq):
                    continue
                if hw.get((stream, a.key), 0) >= a.val:
                    continue
                eng_obj.wait_ge(sems[a.key], a.val)
                hw[(stream, a.key)] = a.val
            if op.emit is not None:
                instr = op.emit(eng_obj)
                if op.flag:
                    if op.cc:
                        instr.then_inc(sems[op.key])
                    else:
                        instr.then_inc(sems[op.key], op.inc)


class Arena:

    def __init__(self, tensor, ncols_bf16):
        self.t = tensor
        self.n = ncols_bf16
        self.cur = 0

    def alloc(self, cols, dtype):
        units = cols * (2 if dtype == F32 else 1)
        units = (units + 31) & ~31
        assert self.cur + units <= self.n, f"arena overflow {self.cur}+{units}>{self.n}"
        ap = self.t[:, self.cur:self.cur + units]
        self.cur += units
        if dtype == F32:
            ap = ap.bitcast(F32)
        return ap[:, 0:cols]

    def mark(self):
        return self.cur

    def reset(self, mark):
        self.cur = mark


class Banks:
    def __init__(self, tensors):
        self.t = tensors
        self.w = [None] * len(tensors)
        self.r = [None] * len(tensors)


def build_nc(debug=False, paranoid=0):
    nc = bass.Bass("TRN2", target_bir_lowering=False)

    xT = nc.dram_tensor("xT", [NTF, NHC, 128, TFF], F32, kind="ExternalInput")
    w_in_t = nc.dram_tensor("w_in_t", [NOT, 128, H], F32, kind="ExternalInput")
    wx_t = nc.dram_tensor("wx_t", [MYKT, 128, 160], F32, kind="ExternalInput")
    wdt_t = nc.dram_tensor("wdt_t", [R, IH], F32, kind="ExternalInput")
    wout_t = nc.dram_tensor("wout_t", [NHC, 128, IH], F32, kind="ExternalInput")
    wg_t = nc.dram_tensor("wg_t", [NIC, 128, H], F32, kind="ExternalInput")
    wu_t = nc.dram_tensor("wu_t", [NIC, 128, H], F32, kind="ExternalInput")
    wd_t = nc.dram_tensor("wd_t", [NHC, 128, ICF], F32, kind="ExternalInput")
    convw_s = nc.dram_tensor("convw_s", [128, 64], F32, kind="ExternalInput")
    convb_s = nc.dram_tensor("convb_s", [128, 16], F32, kind="ExternalInput")
    dtb_s = nc.dram_tensor("dtb_s", [128, 16], F32, kind="ExternalInput")
    A_s = nc.dram_tensor("A_s", [128, 256], F32, kind="ExternalInput")
    D_s = nc.dram_tensor("D_s", [128, 16], F32, kind="ExternalInput")
    gs_s = nc.dram_tensor("gs_s", [128, NIC], F32, kind="ExternalInput")
    us_s = nc.dram_tensor("us_s", [128, NIC], F32, kind="ExternalInput")
    ds_s = nc.dram_tensor("ds_s", [128, NHC], F32, kind="ExternalInput")
    dtln_s = nc.dram_tensor("dtln_s", [128, 1], F32, kind="ExternalInput")
    bln_s = nc.dram_tensor("bln_s", [16, 1], F32, kind="ExternalInput")
    cln_s = nc.dram_tensor("cln_s", [16, 1], F32, kind="ExternalInput")
    oneh_s = nc.dram_tensor("oneh_s", [16, NN * 128], F32, kind="ExternalInput")
    cst_in = nc.dram_tensor("cst_in", [128, NCST], F32, kind="ExternalInput")

    o_out = nc.dram_tensor("o_out", [NTF, NHC, 128, TFF], F32, kind="ExternalOutput")

    d_xn1 = nc.dram_tensor("d_xn1", [NTF, NHC, 128, TFF], BF16)
    d_x1 = nc.dram_tensor("d_x1", [NTF, NHC, 128, TFF], F32)
    d_x2 = nc.dram_tensor("d_x2", [NTF, NHC, 128, TFF], F32)
    d_xn3 = nc.dram_tensor("d_xn3", [NTF, NHC, 128, TFF], BF16)
    d_h = nc.dram_tensor("d_h", [NCHG, MYKT, 128, LC], F32)
    d_u = nc.dram_tensor("d_u", [NCHG, MYKT, 128, LC], F32)
    d_gate = nc.dram_tensor("d_gate", [NCHG, MYKT, 128, LC], F32)
    d_bc = nc.dram_tensor("d_bc", [NCHG, 32, LC], F32)
    d_wg = nc.dram_tensor("d_wg", [NIC, 128, H], BF16)
    d_wu = nc.dram_tensor("d_wu", [NIC, 128, H], BF16)
    d_wd = nc.dram_tensor("d_wd", [NHC, 128, ICF], BF16)
    d_wout = nc.dram_tensor("d_wout", [NHC, 128, IH], BF16)
    cc1_in = [nc.dram_tensor(f"cc1i{s}", [2, NHC, 128, TFF], BF16) for s in range(2)]
    cc1_out = [nc.dram_tensor(f"cc1o{s}", [2, 2, NHC, 128, TFF], BF16) for s in range(2)]
    cc2_in = [nc.dram_tensor(f"cc2i{s}", [4, 160, LC], F32) for s in range(2)]
    cc2_out = [nc.dram_tensor(f"cc2o{s}", [4, 160, LC], F32) for s in range(2)]
    cc3_in = [nc.dram_tensor(f"cc3i{j}", [2 * H, LC], F32) for j in range(4)]
    cc3_out = [nc.dram_tensor(f"cc3o{j}", [H, LC], F32) for j in range(4)]

    with ExitStack() as ctx:
        sb = lambda name, shape, dt_: ctx.enter_context(nc.sbuf_tensor(name, shape, dt_))
        cst = sb("cst", [128, NCST], F32)
        ones = sb("ones", [128, 1], F32)
        onesr = sb("onesr", [1, 128], F32)
        convw = sb("convw", [128, 64], F32)
        convb = sb("convb", [128, 16], F32)
        dtb = sb("dtb", [128, 16], F32)
        A_sb = sb("A_sb", [128, 256], F32)
        D_sb = sb("D_sb", [128, 16], F32)
        gs = sb("gs", [128, NIC], F32)
        us = sb("us", [128, NIC], F32)
        ds = sb("ds", [128, NHC], F32)
        hds = sb("hds", [128, NHC], F32)
        dtln = sb("dtln", [128, 1], F32)
        bln = sb("bln", [16, 1], F32)
        cln = sb("cln", [16, 1], F32)
        oneh = sb("oneh", [16, NN * 128], F32)
        wx_bf = sb("wx_bf", [128, MYKT * 160], BF16)
        wdt_bf = sb("wdt_bf", [128, IH], BF16)
        ts_n = sb("ts_n", [128, LG], BF16)
        state = sb("state", [128, MYKT * NN], F32)
        s_ring = sb("s_ring", [128, 2 * KB * (LC + 1)], F32)
        stat_row = sb("stat_row", [4, 512], F32)
        AR_N = 82000
        arena = Arena(sb("arena", [128, AR_N], BF16), AR_N)

        ps_t = [ctx.enter_context(nc.psum_tensor(f"ps{i}", [128, 512], F32))
                for i in range(8)]
        banks = Banks(ps_t)

        p = Prog()

        def ld(dst_ap, src_ap, after=(), q="sq", chan=None):
            return p.do(q, lambda e, d=dst_ap, s=src_ap: e.dma_start(out=d, in_=s),
                        after=after, dma=True, chan=chan)

        st_ = ld

        def cc(kind, op_, in_ap, out_ap, after, chan):
            return p.do("gq", lambda e, k=kind, o=op_, i=in_ap, u=out_ap:
                        e.collective_compute(k, o, replica_groups=PAIRS,
                                             ins=[i], outs=[u]),
                        after=after, cc=True, chan=chan)

        def mm(bank_i, lhsT, rhs, start, stop, after=(), base=0, m=128):
            out = banks.t[bank_i][base:base + m, :rhs.shape[-1]]
            extra = [banks.r[bank_i]] if start else []
            op = p.do("pe", lambda e, o=out, lh=lhsT, r=rhs, s1=start, s2=stop:
                      e.matmul(o, lh, r, start=s1, stop=s2),
                      after=list(after) + extra)
            banks.w[bank_i] = op
            return op

        def bank_read(bi_, op):
            banks.r[bi_] = op
            return op

        def act(out, in_, func, bias=0.0, scale=1.0, after=()):
            return p.do("act", lambda e, o=out, i=in_, f=func, b=bias, s=scale:
                        e.activation(o, i, f, bias=b, scale=s), after=after)

        def tt(out, a, b_, aop, after=(), eng="dve"):
            return p.do(eng, lambda e, o=out, x=a, y=b_, op_=aop:
                        e.tensor_tensor(o, x, y, op_), after=after)

        def ts(out, a, s1, s2, op0, op1=None, after=(), eng="dve"):
            if op1 is None:
                return p.do(eng, lambda e, o=out, x=a, u=s1, v=s2, p0=op0:
                            e.tensor_scalar(o, x, u, v, p0), after=after)
            return p.do(eng, lambda e, o=out, x=a, u=s1, v=s2, p0=op0, p1=op1:
                        e.tensor_scalar(o, x, u, v, p0, p1), after=after)

        def stt(out, in0, scal, in1, op0, op1, after=(), eng="dve"):
            return p.do(eng, lambda e, o=out, x=in0, s=scal, y=in1, p0=op0, p1=op1:
                        e.scalar_tensor_tensor(o, x, s, y, p0, p1), after=after)

        def cp(out, in_, after=(), eng="dve"):
            return p.do(eng, lambda e, o=out, i=in_: e.tensor_copy(o, i), after=after)

        def memset(ap, v, after=(), eng="dve"):
            return p.do(eng, lambda e, a=ap, vv=v: e.memset(a, vv), after=after)

        def scan(out, d0, d1, init, after=()):
            return p.do("dve", lambda e, o=out, a=d0, b_=d1, i=init:
                        e.tensor_tensor_scan(o, a, b_, i, ALU.mult, ALU.add),
                        after=after)

        cEPS1 = cst[0:1, C_EPS:C_EPS + 1]
        cNEGH1 = cst[0:1, C_NEGH:C_NEGH + 1]
        c151 = cst[0:1, C_15:C_15 + 1]
        cI2048 = cst[0:1, C_I2048:C_I2048 + 1]
        cI128 = cst[0:1, C_I128:C_I128 + 1]
        cI16 = cst[0:1, C_I16:C_I16 + 1]

        stat_busy = [None]

        def rstd_chain(src_row, prescale, rows, cols, after=()):
            row_v, row_r0, row_t = rows
            o1 = ts(row_v[:, 0:cols], src_row, prescale, cEPS1,
                    ALU.mult, ALU.add, after=after)
            o2 = act(row_r0[:, 0:cols], row_v[:, 0:cols], AF.Ln, after=[o1])
            o3 = act(row_r0[:, 0:cols], row_r0[:, 0:cols], AF.Exp, scale=cNEGH1,
                     after=[o2])
            o4 = tt(row_t[:, 0:cols], row_r0[:, 0:cols], row_r0[:, 0:cols],
                    ALU.mult, after=[o3])
            o5 = tt(row_t[:, 0:cols], row_t[:, 0:cols], row_v[:, 0:cols],
                    ALU.mult, after=[o4, o1])
            o6 = ts(row_t[:, 0:cols], row_t[:, 0:cols], cNEGH1, c151,
                    ALU.mult, ALU.add, after=[o5])
            o7 = tt(stat_row[0:1, 0:cols], row_r0[:, 0:cols], row_t[:, 0:cols],
                    ALU.mult, after=[o6, o3, stat_busy[0]])
            return o7

        lds = []
        for dst, src in ((cst, cst_in), (convw, convw_s), (convb, convb_s),
                         (dtb, dtb_s), (A_sb, A_s), (D_sb, D_s), (gs, gs_s),
                         (us, us_s), (ds, ds_s), (dtln, dtln_s)):
            lds.append(ld(dst[:], src.ap(), chan="p0c"))
        lds.append(ld(bln[0:16, :], bln_s.ap(), chan="p0c"))
        lds.append(ld(cln[0:16, :], cln_s.ap(), chan="p0c"))
        lds.append(ld(oneh[0:16, :], oneh_s.ap(), chan="p0c"))
        wx_src = bass.AP(wx_t, 0, [[160, 128], [128 * 160, MYKT], [1, 160]])
        lds.append(ld(wx_bf[:], wx_src, q="gq", chan="p0g"))
        lds.append(ld(wdt_bf[:], wdt_t.ap(), q="gq", chan="p0g"))

        memset(ones[:], 1.0)
        memset(onesr[0:1, :], 1.0)
        ts(hds[:], ds[:], cst[:, C_HALF:C_HALF + 1], None, ALU.mult, after=lds)
        memset(state[:], 0.0)

        cast_m0 = arena.mark()
        wc_f = [arena.alloc(4096, F32) for _ in range(2)]
        wc_b = [arena.alloc(4096, BF16) for _ in range(2)]
        cast_jobs = []
        for ic0 in range(0, NIC, 2):
            n = min(2, NIC - ic0)
            for (src, dst) in ((wg_t, d_wg), (wu_t, d_wu)):
                ap_pat = [[H, 128], [128 * H, n], [1, H]]
                cast_jobs.append((bass.AP(src, ic0 * 128 * H, ap_pat),
                                  bass.AP(dst, ic0 * 128 * H, ap_pat), n * H))
        for hc in range(NHC):
            for piece in range(2):
                w = ICF // 2
                ap_pat = [[ICF, 128], [1, w]]
                cast_jobs.append((bass.AP(wd_t, hc * 128 * ICF + piece * w, ap_pat),
                                  bass.AP(d_wd, hc * 128 * ICF + piece * w, ap_pat),
                                  w))
        for ot in range(0, NHC, 2):
            ap_pat = [[IH, 128], [128 * IH, 2], [1, IH]]
            cast_jobs.append((bass.AP(wout_t, ot * 128 * IH, ap_pat),
                              bass.AP(d_wout, ot * 128 * IH, ap_pat), 2 * IH))
        cast_c = [None, None]
        cast_s = [None, None]
        for j, (srcap, dstap, cols) in enumerate(cast_jobs):
            bi = j % 2
            l_ = ld(wc_f[bi][:, 0:cols], srcap, after=[cast_c[bi]],
                    chan=f"cstl{bi}")
            c_ = act(wc_b[bi][:, 0:cols], wc_f[bi][:, 0:cols], AF.Copy,
                     after=[l_, cast_s[bi]])
            s_ = st_(dstap, wc_b[bi][:, 0:cols], after=[c_], chan=f"csts{bi}")
            cast_c[bi] = c_
            cast_s[bi] = s_

        def ln_phase(src_dram, dst_fn):
            m0 = arena.mark()
            x_sb = [arena.alloc(8192, F32) for _ in range(2)]
            sq_sb = [arena.alloc(512, F32) for _ in range(2)]
            rows = [arena.alloc(512, F32)[0:1, :] for _ in range(3)]
            row_mu = arena.alloc(512, F32)[0:1, :]
            row_oc = arena.alloc(512, F32)[0:1, :]
            t1 = arena.alloc(512, F32)
            xn_sb = [arena.alloc(8192, BF16) for _ in range(2)]
            lu_x = [None, None]
            lu_xn = [None, None]
            oc_busy = [None]
            sqmm = [None, None]
            for t in range(NTF):
                bi = t % 2
                bk_x = 4 + bi
                bk_s = 6 + bi
                bk_r = 0 + bi
                bk_c = 2 + bi
                srcap = bass.AP(src_dram, t * 16 * 128 * TFF,
                                [[TFF, 128], [128 * TFF, 16], [1, TFF]])
                ldx = [ld(x_sb[bi][:], srcap, after=[lu_x[bi]], chan=f"lnx{bi}")]
                mmx = None
                mms = None
                for hc in range(NHC):
                    xs = x_sb[bi][:, hc * 512:(hc + 1) * 512]
                    mmx = mm(bk_x, ones[:, 0:1], xs, start=(hc == 0),
                             stop=(hc == NHC - 1), after=ldx, base=0, m=1)
                    sq_ev = act(sq_sb[hc % 2], xs, AF.Square,
                                after=ldx + [sqmm[hc % 2]])
                    mms = mm(bk_s, ones[:, 0:1], sq_sb[hc % 2],
                             start=(hc == 0), stop=(hc == NHC - 1),
                             after=[sq_ev], base=0, m=1)
                    sqmm[hc % 2] = mms
                mu = act(row_mu, banks.t[bk_x][0:1, :], AF.Copy, scale=cI2048,
                         after=[mmx])
                bank_read(bk_x, mu)
                ex2 = act(rows[0][0:1, :], banks.t[bk_s][0:1, :], AF.Copy,
                          scale=cI2048, after=[mms])
                bank_read(bk_s, ex2)
                mu2 = act(rows[2][0:1, :], row_mu, AF.Square, after=[mu])
                var = tt(rows[0][0:1, :], rows[0][0:1, :], rows[2][0:1, :],
                         ALU.subtract, after=[ex2, mu2])
                r7 = rstd_chain(rows[0][0:1, :], cst[0:1, C_ONE:C_ONE + 1],
                                rows, 512, after=[var])
                oc = tt(row_oc, row_mu, stat_row[0:1, :], ALU.mult,
                        after=[r7, mu, oc_busy[0]])
                br = mm(bk_r, onesr[0:1, 0:128], stat_row[0:1, :],
                        start=True, stop=True, after=[r7])
                bc_ = mm(bk_c, onesr[0:1, 0:128], row_oc,
                         start=True, stop=True, after=[oc])
                stat_busy[0] = [br, oc]
                oc_busy[0] = bc_
                m2s = []
                for hc in range(NHC):
                    xs = x_sb[bi][:, hc * 512:(hc + 1) * 512]
                    m1 = tt(t1, xs, banks.t[bk_r][:], ALU.mult,
                            after=[br, lu_xn[bi], (m2s[-1] if m2s else None)])
                    m2 = tt(xn_sb[bi][:, hc * 512:(hc + 1) * 512], t1,
                            banks.t[bk_c][:], ALU.subtract, after=[m1, bc_])
                    m2s.append(m2)
                bank_read(bk_r, m2s[-1])
                bank_read(bk_c, m2s[-1])
                lu_x[bi] = m2s[-1]
                so = dst_fn(t, xn_sb[bi], [m2s[-1]], bi)
                lu_xn[bi] = so
                if paranoid & 1:
                    p.barrier()
            p.barrier()
            arena.reset(m0)

        def mk_dram_dst(dst_dram):
            def f(t, xn_tile, after, bi):
                dstap = bass.AP(dst_dram, t * 16 * 128 * TFF,
                                [[TFF, 128], [128 * TFF, 16], [1, TFF]])
                return st_(dstap, xn_tile[:], after=after, chan=f"lno{bi}")
            return f

        cc1_store_last = [[], []]

        def cc1_dst(t, xn_tile, after, bi):
            s = t // 2
            dstap = bass.AP(cc1_in[s], (t % 2) * 16 * 128 * TFF,
                            [[TFF, 128], [128 * TFF, 16], [1, TFF]])
            so = st_(dstap, xn_tile[:], after=after, chan=f"lno{bi}")
            cc1_store_last[s].append(so)
            return so

        def ffn_phase(xn_dram, res_dram, out_dram):
            m0 = arena.mark()
            xn_T = [arena.alloc(8192, BF16) for _ in range(2)]
            hdn_T = arena.alloc(NIC * 512, BF16)
            wgb = [arena.alloc(2048, BF16) for _ in range(2)]
            wub = [arena.alloc(2048, BF16) for _ in range(2)]
            wdb = [arena.alloc(ICF, BF16) for _ in range(2)]
            gsl = [arena.alloc(512, F32) for _ in range(2)]
            x0b = [arena.alloc(512, F32) for _ in range(2)]
            x1b = [arena.alloc(512, F32) for _ in range(2)]
            lu_xn = [None, None]
            lu_wg = [None, None]; lu_wu = [None, None]; lu_wd = [None, None]
            lu_gs = [None, None]
            lu_x0 = [None, None]; lu_x1 = [None, None]
            hdn_reader = None
            for t in range(NTF):
                bi = t % 2
                srcap = bass.AP(xn_dram, t * 16 * 128 * TFF,
                                [[TFF, 128], [128 * TFF, 16], [1, TFF]])
                ldx = [ld(xn_T[bi][:], srcap, after=[lu_xn[bi]], chan=f"fxn{bi}")]
                hdn_ops = []
                for ic in range(NIC):
                    wi = ic % 2
                    bg = wi * 2
                    bu = wi * 2 + 1
                    lg = ld(wgb[wi][:], bass.AP(d_wg, ic * 128 * H, [[H, 128], [1, H]]),
                            after=[lu_wg[wi]], chan=f"fwg{wi}")
                    lu_ = ld(wub[wi][:], bass.AP(d_wu, ic * 128 * H, [[H, 128], [1, H]]),
                             after=[lu_wu[wi]], chan=f"fwu{wi}")
                    gm = um = None
                    for hc in range(NHC):
                        gm = mm(bg, wgb[wi][:, hc * 128:(hc + 1) * 128],
                                xn_T[bi][:, hc * 512:(hc + 1) * 512],
                                start=(hc == 0), stop=(hc == NHC - 1),
                                after=ldx + [lg])
                        um = mm(bu, wub[wi][:, hc * 128:(hc + 1) * 128],
                                xn_T[bi][:, hc * 512:(hc + 1) * 512],
                                start=(hc == 0), stop=(hc == NHC - 1),
                                after=ldx + [lu_])
                    lu_wg[wi] = gm
                    lu_wu[wi] = um
                    eg = act(gsl[wi], banks.t[bg][:], AF.Silu,
                             scale=gs[:, ic:ic + 1], after=[gm, lu_gs[wi]])
                    bank_read(bg, eg)
                    hm = stt(hdn_T[:, ic * 512:(ic + 1) * 512],
                             banks.t[bu][:], us[:, ic:ic + 1], gsl[wi],
                             ALU.mult, ALU.mult,
                             after=[um, eg, hdn_reader])
                    bank_read(bu, hm)
                    lu_gs[wi] = hm
                    hdn_ops.append(hm)
                lu_xn[bi] = hdn_ops[-1]
                ev_last = None
                for hc in range(NHC):
                    wi = hc % 2
                    bd = 4 + wi
                    lw = ld(wdb[wi][:], bass.AP(d_wd, hc * 128 * ICF,
                                                [[ICF, 128], [1, ICF]]),
                            after=[lu_wd[wi]], chan=f"fwd{wi}")
                    lx = ld(x0b[wi][:], bass.AP(res_dram, (t * 16 + hc) * 128 * TFF,
                                                [[TFF, 128], [1, TFF]]),
                            after=[lu_x0[wi]], chan=f"fx0{wi}")
                    dm = None
                    for ic in range(NIC):
                        dm = mm(bd, wdb[wi][:, ic * 128:(ic + 1) * 128],
                                hdn_T[:, ic * 512:(ic + 1) * 512],
                                start=(ic == 0), stop=(ic == NIC - 1),
                                after=[lw, hdn_ops[ic]])
                    lu_wd[wi] = dm
                    ev = stt(x1b[wi][:], banks.t[bd][:], hds[:, hc:hc + 1],
                             x0b[wi][:], ALU.mult, ALU.add,
                             after=[dm, lx, lu_x1[wi]])
                    bank_read(bd, ev)
                    lu_x0[wi] = ev
                    so = st_(bass.AP(out_dram, (t * 16 + hc) * 128 * TFF,
                                     [[TFF, 128], [1, TFF]]),
                             x1b[wi][:], after=[ev], chan=f"fx1{wi}")
                    lu_x1[wi] = so
                    ev_last = ev
                hdn_reader = ev_last
                if paranoid & 2:
                    p.barrier()
            p.barrier()
            arena.reset(m0)

        def pass1a():
            m0 = arena.mark()
            xnb = [arena.alloc(16 * BLK, BF16) for _ in range(2)]
            winb = [arena.alloc(2048, BF16) for _ in range(2)]
            hst = [arena.alloc(1024, F32) for _ in range(2)]
            lu_xn = [None, None]
            lu_w = [None, None]
            lu_h = [None, None]
            blocks = [(0, 0, 0), (2, 2048, 0), (1, 0, 1), (3, 2048, 1)]
            for bki, (blk, rowb, sup) in enumerate(blocks):
                bbi = bki % 2
                srcap = bass.AP(cc1_out[sup], rowb * BLK,
                                [[TFF, 128], [128 * TFF, 32], [1, TFF]])
                ldx = [ld(xnb[bbi][:], srcap,
                          after=[lu_xn[bbi], cc_ops["cc1"][sup]],
                          chan=f"paxn{bbi}")]
                for ot in range(NOT):
                    wi = ot % 2
                    lw = ld(winb[wi][:],
                            bass.AP(w_in_t, ot * 128 * H, [[H, 128], [1, H]]),
                            after=[lu_w[wi]], q="gq", chan=f"paw{wi}")
                    is_h = ot < MYKT
                    hi = wi
                    evs = []
                    for c4 in range(2):
                        bpj = 6 + (c4 % 2)
                        pm = None
                        for hc in range(NHC):
                            pm = mm(bpj, winb[wi][:, hc * 128:(hc + 1) * 128],
                                    xnb[bbi][:, (c4 * 16 + hc) * 512:
                                             (c4 * 16 + hc) * 512 + 512],
                                    start=(hc == 0), stop=(hc == NHC - 1),
                                    after=ldx + [lw])
                        if is_h:
                            ev = act(hst[hi][:, c4 * 512:(c4 + 1) * 512],
                                     banks.t[bpj][:], AF.Copy,
                                     after=[pm, lu_h[hi]])
                        else:
                            ev = act(hst[hi][:, c4 * 512:(c4 + 1) * 512],
                                     banks.t[bpj][:], AF.Silu,
                                     after=[pm, lu_h[hi]])
                        bank_read(bpj, ev)
                        evs.append(ev)
                    lu_w[wi] = evs[-1]
                    tgt = d_h if is_h else d_gate
                    kt = ot if is_h else ot - MYKT
                    sos = []
                    for c4 in range(2):
                        sos.append(st_(
                            bass.AP(tgt, ((2 * blk + c4) * MYKT + kt) * 128 * LC,
                                    [[LC, 128], [1, LC]]),
                            hst[hi][:, c4 * 512:(c4 + 1) * 512],
                            after=evs, chan=f"pah{hi}"))
                    lu_h[hi] = sos[-1]
                if paranoid & 4:
                    p.barrier()
            p.barrier()
            arena.reset(m0)

        def pass1b():
            m0 = arena.mark()
            hbuf = [arena.alloc(LT + 3, F32) for _ in range(2)]
            acc_a = [arena.alloc(LT, F32) for _ in range(2)]
            acc_b = [arena.alloc(LT, F32) for _ in range(2)]
            ubf = [arena.alloc(LT, BF16) for _ in range(2)]
            tsev = arena.alloc(2048, F32)
            bcev = arena.alloc(2048, F32)
            lu_h = [None, None]
            lu_u = [None, None]
            acc_busy = [None, None]
            ev_stores = []
            for half in range(2):
                col0 = half * LT
                for ot in range(MYKT):
                    wi = ot % 2
                    lh = []
                    if half == 0:
                        lh.append(memset(hbuf[wi][:, 0:3], 0.0,
                                         after=[lu_h[wi]]))
                    else:
                        lh.append(ld(hbuf[wi][:, 0:3],
                                     bass.AP(d_h, (3 * MYKT + ot) * 128 * LC
                                             + LC - 3,
                                             [[LC, 128], [1, 3]]),
                                     after=[lu_h[wi]], chan=f"pbh{wi}"))
                    for cc4 in range(4):
                        gch = half * 4 + cc4
                        lh.append(ld(hbuf[wi][:, 3 + cc4 * LC:3 + (cc4 + 1) * LC],
                                     bass.AP(d_h, (gch * MYKT + ot) * 128 * LC,
                                             [[LC, 128], [1, LC]]),
                                     after=[lu_h[wi]], chan=f"pbh{wi}"))
                    c0 = ts(acc_a[wi], hbuf[wi][:, 0:LT],
                            convw[:, ot * 4:ot * 4 + 1], None, ALU.mult,
                            after=lh + [acc_busy[wi]])
                    c1 = stt(acc_b[wi], hbuf[wi][:, 1:1 + LT],
                             convw[:, ot * 4 + 1:ot * 4 + 2], acc_a[wi],
                             ALU.mult, ALU.add, after=[c0])
                    c2 = stt(acc_a[wi], hbuf[wi][:, 2:2 + LT],
                             convw[:, ot * 4 + 2:ot * 4 + 3], acc_b[wi],
                             ALU.mult, ALU.add, after=[c1])
                    c3 = stt(acc_b[wi], hbuf[wi][:, 3:3 + LT],
                             convw[:, ot * 4 + 3:ot * 4 + 4], acc_a[wi],
                             ALU.mult, ALU.add, after=[c2])
                    lu_h[wi] = c3
                    sl = act(acc_a[wi], acc_b[wi], AF.Silu,
                             bias=convb[:, ot:ot + 1], after=[c3])
                    sus = []
                    for cc4 in range(4):
                        gch = half * 4 + cc4
                        sus.append(st_(
                            bass.AP(d_u, (gch * MYKT + ot) * 128 * LC,
                                    [[LC, 128], [1, LC]]),
                            acc_a[wi][:, cc4 * LC:(cc4 + 1) * LC],
                            after=[sl], chan=f"pbu{wi}"))
                    su = sus[-1]
                    cb = act(ubf[wi][:], acc_a[wi], AF.Copy, after=[sl, lu_u[wi]])
                    acc_busy[wi] = [su, cb]
                    mm_last = []
                    for c4 in range(4):
                        bts = c4
                        bbc = 4 + c4
                        u_sl = ubf[wi][:, c4 * 512:(c4 + 1) * 512]
                        m1 = mm(bts, wx_bf[:, ot * 160:ot * 160 + 128], u_sl,
                                start=(ot == 0), stop=(ot == MYKT - 1),
                                after=[cb])
                        m2 = mm(bbc, wx_bf[:, ot * 160 + 128:ot * 160 + 160],
                                u_sl, start=(ot == 0), stop=(ot == MYKT - 1),
                                after=[cb], m=32)
                        mm_last.append(m1)
                        mm_last.append(m2)
                    lu_u[wi] = mm_last
                evs = []
                for c4 in range(4):
                    e1 = act(tsev[:, c4 * 512:(c4 + 1) * 512], banks.t[c4][:],
                             AF.Copy, after=[banks.w[c4]] + ev_stores)
                    bank_read(c4, e1)
                    e2 = act(bcev[0:32, c4 * 512:(c4 + 1) * 512],
                             banks.t[4 + c4][0:32, :], AF.Copy,
                             after=[banks.w[4 + c4]] + ev_stores)
                    bank_read(4 + c4, e2)
                    evs += [e1, e2]
                s1 = st_(bass.AP(cc2_in[half], 0,
                                 [[LC, 128], [160 * LC, 4], [1, LC]]),
                         tsev[:], after=evs, chan="pbt")
                s2 = st_(bass.AP(cc2_in[half], 128 * LC,
                                 [[LC, 32], [160 * LC, 4], [1, LC]]),
                         bcev[0:32, :], after=evs, chan="pbt")
                ev_stores = [s1, s2]
                cc_ops["cc2"][half] = cc("AllReduce", ALU.add,
                                         cc2_in[half].ap(), cc2_out[half].ap(),
                                         after=[s1, s2], chan=f"cc2_{half}")
                if paranoid & 4:
                    p.barrier()
            p.barrier()
            arena.reset(m0)

        def rms_phase():
            m0 = arena.mark()
            tsb = [arena.alloc(512, F32) for _ in range(2)]
            bb_ = [arena.alloc(512, F32) for _ in range(2)]
            cb_ = [arena.alloc(512, F32) for _ in range(2)]
            sq = arena.alloc(512, F32)
            rows = [arena.alloc(512, F32)[0:1, :] for _ in range(3)]
            tf = arena.alloc(512, F32)
            lu = [None, None]
            lu_bc = [None, None]
            for c in range(NCHG):
                half = c // 4
                coff = (c % 4) * 512
                bi = c % 2
                c4g = c % 4
                lt_ = ld(tsb[bi][:], bass.AP(cc2_out[half], c4g * 160 * LC,
                                             [[LC, 128], [1, LC]]),
                         after=[lu[bi], cc_ops["cc2"][half]], chan=f"rmt{bi}")
                lb_ = ld(bb_[bi][0:16, :],
                         bass.AP(cc2_out[half], c4g * 160 * LC + 128 * LC,
                                 [[LC, 16], [1, LC]]),
                         after=[lu_bc[bi], cc_ops["cc2"][half]], chan=f"rmb{bi}")
                lc2 = ld(cb_[bi][0:16, :],
                         bass.AP(cc2_out[half], c4g * 160 * LC + 144 * LC,
                                 [[LC, 16], [1, LC]]),
                         after=[lu_bc[bi], cc_ops["cc2"][half]], chan=f"rmb{bi}")
                sq1 = act(sq, tsb[bi][:], AF.Square, after=[lt_])
                sm = mm(6, ones[:, 0:1], sq, start=True, stop=True,
                        after=[sq1], m=1)
                r7 = rstd_chain(banks.t[6][0:1, :], cI128, rows, 512, after=[sm])
                bank_read(6, r7)
                br = mm(7, onesr[0:1, 0:128], stat_row[0:1, :], start=True,
                        stop=True, after=[r7])
                stat_busy[0] = br
                t1_ = tt(tf, tsb[bi][:], banks.t[7][:], ALU.mult,
                         after=[br, sq1])
                t2_ = ts(ts_n[:, c * 512:(c + 1) * 512], tf, dtln[:, 0:1],
                         None, ALU.mult, after=[t1_])
                bank_read(7, t2_)
                lu[bi] = t2_
                prev = t2_
                for half2, w_ln in ((0, bln), (1, cln)):
                    src16 = bb_[bi][0:16, :] if half2 == 0 else cb_[bi][0:16, :]
                    sq2 = act(sq[0:16, :], src16, AF.Square,
                              after=[lb_, lc2, prev, sm])
                    sm2 = mm(6, ones[0:16, 0:1], sq[0:16, :], start=True,
                             stop=True, after=[sq2], m=1)
                    r72 = rstd_chain(banks.t[6][0:1, :], cI16, rows, 512,
                                     after=[sm2])
                    bank_read(6, r72)
                    br2 = mm(7, onesr[0:1, 0:128], stat_row[0:1, :],
                             start=True, stop=True, after=[r72])
                    stat_busy[0] = br2
                    n1 = tt(tf[0:16, :], src16, banks.t[7][0:16, :], ALU.mult,
                            after=[br2, sq2])
                    n2 = ts(src16, tf[0:16, :],
                            w_ln[0:16, 0:1], None, ALU.mult, after=[n1])
                    bank_read(7, n2)
                    sb2 = st_(bass.AP(d_bc, (c * 32 + half2 * 16) * LC,
                                      [[LC, 16], [1, LC]]),
                              src16, after=[n2], chan="rmo")
                    prev = sb2
                lu_bc[bi] = prev
            p.barrier()
            arena.reset(m0)

        def scan_phase():
            m0 = arena.mark()
            Bb = arena.alloc(NN * 512, F32)
            Cb = arena.alloc(NN * 512, F32)
            bst = arena.alloc(512, F32)
            cstg = arena.alloc(512, F32)
            ub = [arena.alloc(KB * 512, F32)]
            gat = [arena.alloc(KB * 512, F32)]
            delta = arena.alloc(KB * 512, F32)
            du = arena.alloc(KB * 512, F32)
            yb = arena.alloc(KB * 512, F32)
            dab = [arena.alloc(LC + 1, F32) for _ in range(4)]
            dbb = [arena.alloc(LC + 1, F32) for _ in range(4)]
            ymt_d = arena.alloc(512, F32)
            ymt_p = arena.alloc(512, F32)
            tmp_sp = [arena.alloc(512, F32) for _ in range(2)]
            y_g = arena.alloc(MYKT * 512, BF16)
            woutb = [arena.alloc(IH, BF16) for _ in range(2)]
            wev = [arena.alloc(512, F32)]
            pad_ok = ([memset(dab[i][:, LC:LC + 1], 1.0) for i in range(4)]
                      + [memset(dbb[i][:, LC:LC + 1], 0.0) for i in range(4)])
            sring_v = s_ring[:].rearrange("p (q k l) -> p q k l", l=LC + 1, k=KB)
            state_v = state[:].rearrange("p (k n) -> p k n", n=NN)
            tbl_busy = [None]
            db_last = [None, None]
            lu_u = [None]; lu_g = [None]
            lu_w = [None, None]
            lu_wev = [None]
            delta_busy = None
            du_busy = None
            tmp_busy = [None, None]
            st_by_par = {0: None, 1: None}
            scan_hist = {}
            y2_hist = {}
            yg_last = {"dve": None, "pool": None}
            wout_done = None
            wout_stores = []
            for c in range(NCHG):
                lbst = ld(bst[0:16, :], bass.AP(d_bc, c * 32 * LC,
                                                [[LC, 16], [1, LC]]),
                          after=[tbl_busy[0]], chan="stb")
                lcst = ld(cstg[0:16, :], bass.AP(d_bc, (c * 32 + 16) * LC,
                                                 [[LC, 16], [1, LC]]),
                          after=[tbl_busy[0]], chan="stc")
                tev = []
                for n in range(NN):
                    bsel = n % 2
                    mb = mm(bsel, oneh[0:16, n * 128:(n + 1) * 128],
                            bst[0:16, :], start=True,
                            stop=True, after=[tbl_busy[0], lbst])
                    eb = act(Bb[:, n * 512:(n + 1) * 512], banks.t[bsel][:],
                             AF.Copy, after=[mb, tbl_busy[0]])
                    bank_read(bsel, eb)
                    mc = mm(bsel, oneh[0:16, n * 128:(n + 1) * 128],
                            cstg[0:16, :], start=True,
                            stop=True, after=[eb, lcst])
                    ec = act(Cb[:, n * 512:(n + 1) * 512], banks.t[bsel][:],
                             AF.Copy, after=[mc])
                    bank_read(bsel, ec)
                    tev.append(ec)
                tev_last = tev[-1]
                for kb in range(NKB):
                    d1_ops = []
                    y0_ops = []
                    lg_ops = []
                    delta_ops = []
                    for kk in range(KB):
                        k = kb * KB + kk
                        lh = ld(ub[0][:, kk * 512:kk * 512 + LC],
                                bass.AP(d_u, (c * MYKT + k) * 128 * LC,
                                        [[LC, 128], [1, LC]]),
                                after=[lu_u[0]], chan="su0")
                        lg = ld(gat[0][:, kk * 512:kk * 512 + LC],
                                bass.AP(d_gate, (c * MYKT + k) * 128 * LC,
                                        [[LC, 128], [1, LC]]),
                                after=[lu_g[0]], chan="sg0")
                        bdt = 6 + (kk % 2)
                        dmm = mm(bdt, wdt_bf[:, k * 128:(k + 1) * 128],
                                 ts_n[:, c * LC:(c + 1) * LC], start=True,
                                 stop=True, after=[])
                        dl = delta[:, kk * 512:kk * 512 + LC]
                        sp1 = act(tmp_sp[kk % 2], banks.t[bdt][:, 0:LC],
                                  AF.Abs, bias=dtb[:, k:k + 1],
                                  after=[dmm, tmp_busy[kk % 2]])
                        sp2 = act(tmp_sp[kk % 2], tmp_sp[kk % 2], AF.Exp,
                                  scale=cst[:, C_NEG1:C_NEG1 + 1], after=[sp1])
                        sp3 = act(tmp_sp[kk % 2], tmp_sp[kk % 2], AF.Ln,
                                  bias=cst[:, C_ONE:C_ONE + 1], after=[sp2])
                        sp4 = act(dl, banks.t[bdt][:, 0:LC], AF.Relu,
                                  bias=dtb[:, k:k + 1],
                                  after=[sp3, delta_busy])
                        bank_read(bdt, sp4)
                        sp5 = tt(dl, dl, tmp_sp[kk % 2], ALU.add,
                                 after=[sp4, delta_busy])
                        tmp_busy[kk % 2] = sp5
                        delta_ops.append(sp5)
                        d1 = tt(du[:, kk * 512:kk * 512 + LC], dl,
                                ub[0][:, kk * 512:kk * 512 + LC], ALU.mult,
                                after=[sp5, lh, du_busy])
                        y0 = ts(yb[:, kk * 512:kk * 512 + LC],
                                ub[0][:, kk * 512:kk * 512 + LC],
                                D_sb[:, k:k + 1], None, ALU.mult,
                                after=[lh, yg_last["dve"], yg_last["pool"]])
                        d1_ops.append(d1)
                        y0_ops.append(y0)
                        lg_ops.append(lg)
                    lu_u[0] = [d1_ops[-1], y0_ops[-1]]
                    yop_prev = dict(enumerate(y0_ops))
                    last_ex = None
                    last_db = None
                    st_ops = []
                    for n in range(NN):
                        par = n % 2
                        for kk in range(KB):
                            k = kb * KB + kk
                            dl = delta[:, kk * 512:kk * 512 + LC]
                            ex = act(dab[kk][:, 0:LC], dl, AF.Exp,
                                     scale=A_sb[:, k * NN + n:k * NN + n + 1],
                                     after=[scan_hist.get((kk, 1 - par)),
                                            delta_ops[kk]] + pad_ok)
                            last_ex = ex
                            db_ = tt(dbb[kk][:, 0:LC],
                                     du[:, kk * 512:kk * 512 + LC],
                                     Bb[:, n * 512:(n + 1) * 512], ALU.mult,
                                     after=[tev_last, scan_hist.get((kk, 1 - par)),
                                            d1_ops[kk]])
                            last_db = db_
                            sc = scan(sring_v[:, par, kk, :], dab[kk][:],
                                      dbb[kk][:],
                                      state[:, k * NN + n:k * NN + n + 1],
                                      after=[ex, db_, y2_hist.get((kk, par)),
                                             st_by_par[par]])
                            scan_hist[(kk, par)] = sc
                            yeng = "dve"
                            ymt = ymt_d
                            y1 = tt(ymt, sring_v[:, par, kk, 0:LC],
                                    Cb[:, n * 512:(n + 1) * 512], ALU.mult,
                                    after=[sc, tev_last, yg_last[yeng]],
                                    eng=yeng)
                            y2 = tt(yb[:, kk * 512:kk * 512 + LC],
                                    yb[:, kk * 512:kk * 512 + LC], ymt,
                                    ALU.add, after=[y1, yop_prev.get(kk)],
                                    eng=yeng)
                            yop_prev[kk] = y2
                            y2_hist[(kk, par)] = y2
                            yg_last[yeng] = y2
                        stc = act(state_v[:, kb * KB:(kb + 1) * KB, n],
                                  sring_v[:, par, :, LC], AF.Copy,
                                  after=[scan_hist[(kk2, par)] for kk2 in range(KB)]
                                  + [st_by_par[par]])
                        st_by_par[par] = stc
                        st_ops.append(stc)
                    delta_busy = [last_ex, d1_ops[-1]]
                    du_busy = last_db
                    for kk in range(KB):
                        k = kb * KB + kk
                        yeng = "dve"
                        yg = tt(y_g[:, k * 512:k * 512 + LC],
                                yb[:, kk * 512:kk * 512 + LC],
                                gat[0][:, kk * 512:kk * 512 + LC], ALU.mult,
                                after=[yop_prev[kk], st_ops[-1], st_ops[-2],
                                       lg_ops[kk],
                                       yg_last[yeng], wout_done], eng=yeng)
                        yg_last[yeng] = yg
                    lu_g[0] = [yg_last["dve"], yg_last["pool"]]
                wev_ops = []
                for ot in range(NHC):
                    wi = ot % 2
                    bw = 2 + wi
                    lw = ld(woutb[wi][:],
                            bass.AP(d_wout, ot * 128 * IH, [[IH, 128], [1, IH]]),
                            after=[lu_w[wi]], chan=f"sw{wi}")
                    wm = None
                    for kt in range(MYKT):
                        wm = mm(bw, woutb[wi][:, kt * 128:(kt + 1) * 128],
                                y_g[:, kt * 512:(kt + 1) * 512],
                                start=(kt == 0), stop=(kt == MYKT - 1),
                                after=[lw, yg_last["dve"], yg_last["pool"]])
                    lu_w[wi] = wm
                    ev = act(wev[0][:], banks.t[bw][:, 0:LC], AF.Copy,
                             after=[wm, lu_wev[0]])
                    bank_read(bw, ev)
                    j = c % 4
                    rowb = 0 if c < 4 else H
                    so = st_(bass.AP(cc3_in[j], (rowb + ot * 128) * LC,
                                     [[LC, 128], [1, LC]]),
                             wev[0][:], after=[ev], chan="sv0")
                    lu_wev[0] = so
                    wev_ops.append(so)
                    wev_ops.append(ev)
                wout_done = wev_ops[-1]
                tbl_busy[0] = wev_ops[-1]
                wout_stores.append(wev_ops)
                if c >= 4:
                    j = c - 4
                    cc_ops["cc3"][j] = cc("ReduceScatter", ALU.add,
                                          cc3_in[j].ap(), cc3_out[j].ap(),
                                          after=wout_stores[j] + wout_stores[c],
                                          chan=f"cc3_{j}")
                if paranoid & 8:
                    p.barrier()
            p.barrier()
            arena.reset(m0)

        def x2_phase():
            m0 = arena.mark()
            a_ = [arena.alloc(512, F32) for _ in range(2)]
            b_ = [arena.alloc(512, F32) for _ in range(2)]
            lu = [None, None]
            for j in range(4):
                for hc in range(NHC):
                    bi = (j * NHC + hc) % 2
                    l1 = ld(a_[bi][:], bass.AP(d_x1, (j * 16 + hc) * 128 * LC,
                                               [[LC, 128], [1, LC]]),
                            after=[lu[bi]], chan=f"xa{bi}")
                    l2 = ld(b_[bi][:], bass.AP(cc3_out[j], hc * 128 * LC,
                                               [[LC, 128], [1, LC]]),
                            after=[lu[bi], cc_ops["cc3"][j]], chan=f"xb{bi}")
                    ad = tt(a_[bi][:], a_[bi][:], b_[bi][:], ALU.add,
                            after=[l1, l2])
                    so = st_(bass.AP(d_x2, (j * 16 + hc) * 128 * LC,
                                     [[LC, 128], [1, LC]]),
                             a_[bi][:], after=[ad], chan=f"xc{bi}")
                    lu[bi] = [ad, so]
            p.barrier()
            arena.reset(m0)

        cc_ops = {"cc1": [None, None], "cc2": [None, None],
                  "cc3": [None, None, None, None]}

        ln_phase(xT, mk_dram_dst(d_xn1))
        arena.reset(cast_m0)
        ffn_phase(d_xn1, xT, d_x1)
        ln_phase(d_x1, cc1_dst)
        for s in range(2):
            cc_ops["cc1"][s] = cc("AllGather", ALU.bypass, cc1_in[s].ap(),
                                  cc1_out[s].ap(), after=cc1_store_last[s],
                                  chan=f"cc1_{s}")
        pass1a()
        pass1b()
        rms_phase()
        scan_phase()
        x2_phase()
        ln_phase(d_x2, mk_dram_dst(d_xn3))
        ffn_phase(d_xn3, d_x2, o_out)

        if debug:
            p.barrier()
            dbg_map = {"d_xn1": (d_xn1, [H, LT], BF16), "d_x1": (d_x1, [H, LT], F32),
                       "d_h": (d_h, [MYKT, 128, LG], F32),
                       "d_u": (d_u, [MYKT, 128, LG], F32),
                       "d_gate": (d_gate, [MYKT, 128, LG], F32),
                       "d_x2": (d_x2, [H, LT], F32),
                       "cc1o0": (cc1_out[0], [2 * H, BLK], BF16),
                       "cc1o1": (cc1_out[1], [2 * H, BLK], BF16),
                       "cc2o0": (cc2_out[0], [160, LT], F32),
                       "cc2o1": (cc2_out[1], [160, LT], F32),
                       "cc3o0": (cc3_out[0], [H, LC], F32)}
            for name, (src, shape, dt_) in dbg_map.items():
                dbg = nc.dram_tensor("dbg_" + name, shape, dt_,
                                     kind="ExternalOutput")
                ld(dbg.ap(), src.ap(), chan="dbg")
            dbg_tsn = nc.dram_tensor("dbg_tsn", [128, LG], BF16,
                                     kind="ExternalOutput")
            dbg_bn = nc.dram_tensor("dbg_bn", [16, LG], F32,
                                    kind="ExternalOutput")
            dbg_cn = nc.dram_tensor("dbg_cn", [16, LG], F32,
                                    kind="ExternalOutput")
            ld(dbg_tsn.ap(), ts_n[:], chan="dbg")
            ld(dbg_bn.ap(), b_n[0:16, :], chan="dbg")
            ld(dbg_cn.ap(), c_n[0:16, :], chan="dbg")

        p.do("sq", None, after=list(p.chan_last.values()))

        per_stream, keys = p.finalize()
        sems = {k: ctx.enter_context(nc.semaphore(f"sem_{k}")) for k in keys}
        hw = {}
        block = ctx.enter_context(nc.Block())

        @block.sync
        def _(sync):
            Prog.emit_stream(sync, "sync", per_stream["sync"], sems, hw)

        @block.tensor
        def _(tensor):
            Prog.emit_stream(tensor, "tensor", per_stream["tensor"], sems, hw)

        @block.scalar
        def _(scalar):
            Prog.emit_stream(scalar, "scalar", per_stream["scalar"], sems, hw)

        @block.vector
        def _(vector):
            Prog.emit_stream(vector, "vector", per_stream["vector"], sems, hw)

        @block.gpsimd
        def _(gpsimd):
            Prog.emit_stream(gpsimd, "gpsimd", per_stream["gpsimd"], sems, hw)

    return nc


def _prep_shared(inp):
    gw = inp["L0_ffn_gate_w"]
    uw = inp["L0_ffn_up_w"]
    dw = inp["L0_ffn_down_w"]
    d = {}
    d["wg_t"] = np.ascontiguousarray(
        np.asarray(gw).reshape(NIC, 128, NHC, 128).transpose(0, 3, 2, 1)
        .reshape(NIC, 128, H))
    d["wu_t"] = np.ascontiguousarray(
        np.asarray(uw).reshape(NIC, 128, NHC, 128).transpose(0, 3, 2, 1)
        .reshape(NIC, 128, H))
    d["wd_t"] = np.ascontiguousarray(
        np.asarray(dw).reshape(NHC, 128, NIC, 128).transpose(0, 3, 2, 1)
        .reshape(NHC, 128, ICF))
    d["gs_s"] = np.ascontiguousarray(
        np.asarray(inp["L0_ffn_gate_s"]).reshape(NIC, 128).T)
    d["us_s"] = np.ascontiguousarray(
        np.asarray(inp["L0_ffn_up_s"]).reshape(NIC, 128).T)
    d["ds_s"] = np.ascontiguousarray(
        np.asarray(inp["L0_ffn_down_s"]).reshape(NHC, 128).T)
    d["dtln_s"] = np.ascontiguousarray(
        np.asarray(inp["L0_attn_dt_ln_w"]).reshape(128, 1))
    d["bln_s"] = np.ascontiguousarray(
        np.asarray(inp["L0_attn_b_ln_w"]).reshape(16, 1))
    d["cln_s"] = np.ascontiguousarray(
        np.asarray(inp["L0_attn_c_ln_w"]).reshape(16, 1))
    oneh = np.zeros((16, NN * 128), np.float32)
    for n in range(NN):
        oneh[n, n * 128:(n + 1) * 128] = 1.0
    d["oneh_s"] = oneh
    cvals = np.zeros(NCST, np.float32)
    cvals[C_EPS] = EPS
    cvals[C_ONE] = 1.0
    cvals[C_NEG1] = -1.0
    cvals[C_NEGH] = -0.5
    cvals[C_15] = 1.5
    cvals[C_HALF] = 0.5
    cvals[C_I2048] = 1.0 / 2048.0
    cvals[C_I128] = 1.0 / 128.0
    cvals[C_I16] = 1.0 / 16.0
    d["cst_in"] = np.ascontiguousarray(
        np.broadcast_to(cvals.astype(np.float32), (128, NCST)))
    return d


def _prep_core(inp, half):
    ch0 = half * IH
    W_in = np.asarray(inp["L0_attn_W_in"])
    Wx = np.asarray(inp["L0_attn_Wx"])
    Wdt = np.asarray(inp["L0_attn_Wdt"])
    Wout = np.asarray(inp["L0_attn_Wout"])
    d = {}
    w_in_rows = np.concatenate([W_in[ch0:ch0 + IH], W_in[I + ch0:I + ch0 + IH]],
                               axis=0)
    d["w_in_t"] = np.ascontiguousarray(
        w_in_rows.reshape(NOT, 128, NHC, 128).transpose(0, 3, 2, 1)
        .reshape(NOT, 128, H))
    d["wx_t"] = np.ascontiguousarray(
        Wx[:, ch0:ch0 + IH].T.reshape(MYKT, 128, 160))
    d["wdt_t"] = np.ascontiguousarray(Wdt[ch0:ch0 + IH].T)
    d["wout_t"] = np.ascontiguousarray(
        Wout[:, ch0:ch0 + IH].reshape(NHC, 128, MYKT, 128)
        .transpose(0, 3, 2, 1).reshape(NHC, 128, IH))
    d["convw_s"] = np.ascontiguousarray(
        np.asarray(inp["L0_attn_conv_w"])[ch0:ch0 + IH].reshape(16, 128, 4)
        .transpose(1, 0, 2).reshape(128, 64))
    d["convb_s"] = np.ascontiguousarray(
        np.asarray(inp["L0_attn_conv_b"])[ch0:ch0 + IH].reshape(16, 128).T)
    d["dtb_s"] = np.ascontiguousarray(
        np.asarray(inp["L0_attn_dt_bias"])[ch0:ch0 + IH].reshape(16, 128).T)
    d["A_s"] = np.ascontiguousarray(
        np.asarray(inp["L0_attn_A"])[ch0:ch0 + IH].reshape(16, 128, 16)
        .transpose(1, 0, 2).reshape(128, 256))
    d["D_s"] = np.ascontiguousarray(
        np.asarray(inp["L0_attn_D"])[ch0:ch0 + IH].reshape(16, 128).T)
    return d


def kernel(**inputs):
    shared = _prep_shared(inputs)
    x = np.asarray(inputs["L0_attn_x"])
    nc = build_nc()
    in_maps = []
    core_prep = [_prep_core(inputs, 0), _prep_core(inputs, 1)]
    for core in range(8):
        b = core // 2
        half = core % 2
        m = dict(shared)
        m.update(core_prep[half])
        xr = x[b].T[:, half * LT:(half + 1) * LT]
        m["xT"] = np.ascontiguousarray(
            xr.reshape(NHC, 128, NTF, TFF).transpose(2, 0, 1, 3))
        in_maps.append(m)
    res = run_bass_kernel_spmd(nc, in_maps, core_ids=list(range(8)))
    out = np.empty((B, LG, H), np.float32)
    for b in range(B):
        for half in range(2):
            o = np.asarray(res.results[2 * b + half]["o_out"])
            o = o.transpose(1, 2, 0, 3).reshape(H, LT)
            out[b, half * LT:(half + 1) * LT] = o.T
    return out
